# version 03
# baseline (speedup 1.0000x reference)
import os
import sys
sys.path.insert(0, "/opt/trn_rl_repo")
import numpy as np
import ml_dtypes
from contextlib import ExitStack

import concourse.bass as bass
import concourse.mybir as mybir
from concourse.bass_utils import run_bass_kernel_spmd

F32 = mybir.dt.float32
BF16 = mybir.dt.bfloat16
AF = mybir.ActivationFunctionType
ALU = mybir.AluOpType
AX = mybir.AxisListType

B, S, DM = 4, 4096, 4096
NH, NKV, DH = 32, 8, 128
NE = 8
HALF = S // 2
NI = DM // 128
NTO = HALF // 128
NTF = S // 128
NQC = HALF // 512
KVC = NKV * DH
INV_SQRT_D = float(1.0 / np.sqrt(DH))
EXP_SHIFT = -20.0
LN_EPS = 1e-5

ENGS = ("tensor", "vector", "scalar", "gpsimd", "sync")


class Sem:
    __slots__ = ("h", "n")
    def __init__(self, h):
        self.h = h
        self.n = 0
    def inc(self, k=1):
        self.n += k
        return self.n


def build_program(phases=("p0", "p1", "p2", "p3", "p4")):
    nc = bass.Bass("TRN2", target_bir_lowering=False)
    DT = nc.dram_tensor

    x_in = DT("x", [S, DM], F32, kind="ExternalInput").ap()
    wq = DT("wq2", [NH, 128, DM], F32, kind="ExternalInput").ap()
    wk = DT("wk2", [NKV, 128, DM], F32, kind="ExternalInput").ap()
    wv = DT("wv2", [NKV, 128, DM], F32, kind="ExternalInput").ap()
    bq_t = DT("bq_t", [128, NH], F32, kind="ExternalInput").ap()
    bk_t = DT("bk_t", [128, NKV], F32, kind="ExternalInput").ap()
    bv_t = DT("bv_t", [128, NKV], F32, kind="ExternalInput").ap()
    gw2 = DT("gw2", [128, NI * NE], F32, kind="ExternalInput").ap()
    gb_c = DT("gb_c", [NE, 1], F32, kind="ExternalInput").ap()
    ew2 = DT("ew2", [NE, NI, 128, DM], F32, kind="ExternalInput").ap()
    eb2 = DT("eb2", [128, NE * NI], F32, kind="ExternalInput").ap()
    cosq = DT("cosq", [128, HALF], BF16, kind="ExternalInput").ap()
    sinqn = DT("sinqn", [128, HALF], BF16, kind="ExternalInput").ap()
    cosk = DT("cosk", [128, S], BF16, kind="ExternalInput").ap()
    sinkn = DT("sinkn", [128, S], BF16, kind="ExternalInput").ap()
    identf_in = DT("identf", [128, 128], F32, kind="ExternalInput").ap()
    identb_in = DT("identb", [128, 128], BF16, kind="ExternalInput").ap()
    perm64_in = DT("perm64", [128, 128], BF16, kind="ExternalInput").ap()
    onescol_in = DT("onescol", [128, 1], BF16, kind="ExternalInput").ap()
    fconst_in = DT("fconst", [128, 2], F32, kind="ExternalInput").ap()

    out = DT("out", [HALF, DM], F32, kind="ExternalOutput").ap()

    xn_scr = DT("xn_scr", [S, DM], BF16, kind="ExternalOutput").ap()
    q_scrT = DT("q_scrT", [DM, HALF], BF16, kind="ExternalOutput").ap()
    k_scrT = DT("k_scrT", [KVC, S], BF16, kind="ExternalOutput").ap()
    v_scrT = DT("v_scrT", [KVC, S], BF16, kind="ExternalOutput").ap()
    mask_scr = DT("mask_scr", [NE, HALF], BF16, kind="ExternalOutput").ap()
    denb_scr = DT("denb_scr", [NH, HALF], BF16, kind="ExternalOutput").ap()
    s_scrT = DT("s_scrT", [DM, HALF], BF16, kind="ExternalOutput").ap()
    g_scrT = DT("g_scrT", [DM, HALF], BF16, kind="ExternalOutput").ap()

    th = {e: [] for e in ENGS}

    def T(fn): th["tensor"].append(fn)
    def V(fn): th["vector"].append(fn)
    def A(fn): th["scalar"].append(fn)
    def G(fn): th["gpsimd"].append(fn)
    def SY(fn): th["sync"].append(fn)

    sems = {}

    def wait(e, sem, val):
        if val > 0:
            th[e].append(lambda eng, s=sem.h, v=val: eng.wait_ge(s, v))

    def mile(e, sem):
        tgt = sem.inc()
        def _m(eng, s=sem.h):
            eng.drain()
            eng.sem_inc(s, 1)
        th[e].append(_m)
        return tgt

    with ExitStack() as st0:
        E0 = st0.enter_context
        for name in ["ph", "cons", "xld0", "xld1", "xnw0", "xnw1", "mskw",
                     "xnt", "w1a", "w1b", "wo0", "wo1",
                     "kld", "vld0", "vld1", "qld0", "qld1", "so0", "so1", "dn0", "dn1",
                     "w3a", "w3b", "mr0", "mr1", "g30", "g31",
                     "x40", "x41", "sg0", "sg1", "ow0", "ow1",
                     "mT", "mV", "mA", "mA2", "mG"]:
            sems[name] = Sem(E0(nc.semaphore(name)))
        Z = sems

        def barrier():
            tgt = Z["ph"].inc(5)
            for e in ENGS:
                def _f(eng, t=tgt):
                    eng.drain()
                    eng.sem_inc(Z["ph"].h, 1)
                    eng.wait_ge(Z["ph"].h, t)
                th[e].append(_f)

        identf = E0(nc.sbuf_tensor("identf_sb", [128, 128], F32))
        identb = E0(nc.sbuf_tensor("identb_sb", [128, 128], BF16))
        perm64 = E0(nc.sbuf_tensor("perm64_sb", [128, 128], BF16))
        onescol = E0(nc.sbuf_tensor("onescol_sb", [128, 1], BF16))
        bq_sb = E0(nc.sbuf_tensor("bq_sb", [128, NH], F32))
        bk_sb = E0(nc.sbuf_tensor("bk_sb", [128, NKV], F32))
        bv_sb = E0(nc.sbuf_tensor("bv_sb", [128, NKV], F32))
        eb_sb = E0(nc.sbuf_tensor("eb_sb", [128, NE * NI], F32))
        gb_sb = E0(nc.sbuf_tensor("gb_sb", [NE, 1], F32))
        gwT_sb = E0(nc.sbuf_tensor("gwT_sb", [128, NI * NE], F32))
        maskT_sb = E0(nc.sbuf_tensor("maskT_sb", [NE, HALF], BF16))
        fconst = E0(nc.sbuf_tensor("fconst_sb", [128, 2], F32))

        def consts_fn(s):
            pairs = [
                (identf[:, :], identf_in), (identb[:, :], identb_in),
                (perm64[:, :], perm64_in), (onescol[:, :], onescol_in),
                (bq_sb[:, :], bq_t), (bk_sb[:, :], bk_t), (bv_sb[:, :], bv_t),
                (gb_sb[:, :], gb_c), (fconst[:, :], fconst_in),
                (gwT_sb[:, :], gw2), (eb_sb[:, :], eb2),
            ]
            for o_, i_ in pairs:
                s.dma_start(o_, i_).then_inc(Z["cons"].h, 16)
            assert len(pairs) == 11
        SY(consts_fn)
        Z["cons"].n = 16 * 11

        if "p0" in phases:
            with ExitStack() as stp:
                E = stp.enter_context
                xrow = E(nc.sbuf_tensor("xrow", [128, 2 * DM], F32))
                xnf = E(nc.sbuf_tensor("xnf", [128, DM], F32))
                xnb = E(nc.sbuf_tensor("xnb", [128, 2 * DM], BF16))
                xnTf = E(nc.sbuf_tensor("xnTf", [128, DM], F32))
                stats = E(nc.sbuf_tensor("stats", [128, 16], F32))
                Lbuf = E(nc.sbuf_tensor("Lbuf", [128, 128], F32))
                logt = E(nc.sbuf_tensor("logt", [128, 16], F32))
                maskb = E(nc.sbuf_tensor("maskb", [128, 128], BF16))
                psT = E(nc.psum_tensor("psT0", [128, 1024], F32))
                psG = E(nc.psum_tensor("psG0", [128, 512], F32))
                psL = E(nc.psum_tensor("psL0", [128, 512], F32))
                psMk = E(nc.psum_tensor("psM0", [128, 1024], BF16))

                wait("vector", Z["cons"], Z["cons"].n)
                def _ms(v):
                    v.memset(Lbuf[:, :], 0.0)
                    v.memset(maskb[:, :], 0.0)
                V(_ms)
                mile("vector", Z["mV"])

                tile_a2 = []
                xnf_free_T = [0]
                _ntf = int(os.environ.get("K_NTF", NTF))
                _gate = os.environ.get("K_GATE", "1") == "1"
                _gstep = int(os.environ.get("K_GSTEP", "9"))
                for it in range(_ntf):
                    own = (it < NTO) and _gate
                    slot = it % 2
                    sx = Z["xld0"] if slot == 0 else Z["xld1"]
                    if it >= 2:
                        wait("sync", Z["mA"], tile_a2[it - 2])
                    xt = sx.inc(16)
                    def _ld(s, it=it, slot=slot, sx=sx):
                        s.dma_start(xrow[:, slot * DM:(slot + 1) * DM],
                                    x_in[it * 128:(it + 1) * 128, :]).then_inc(sx.h, 16)
                    SY(_ld)

                    xv = xrow[:, slot * DM:(slot + 1) * DM]
                    wait("vector", sx, xt)
                    wait("vector", Z["mT"], xnf_free_T[0])
                    def _st(v, xv=xv):
                        v.tensor_scalar(xnf[:, :], xv, 1.0, 0.0, ALU.mult, ALU.add, accum_out=stats[:, 0:1])
                        v.drain()
                        v.scalar_tensor_tensor(xnf[:, :], xv, 1.0, xv, ALU.mult, ALU.mult,
                                               accum_out=stats[:, 1:2])
                        v.drain()
                        v.tensor_scalar(stats[:, 2:3], stats[:, 0:1], 1.0 / DM, None, ALU.mult)
                        v.tensor_scalar(stats[:, 3:4], stats[:, 1:2], 1.0 / DM, None, ALU.mult)
                        v.drain()
                        v.scalar_tensor_tensor(stats[:, 4:5], stats[:, 2:3], stats[:, 2:3],
                                               stats[:, 3:4], ALU.mult, ALU.subtract)
                    V(_st)
                    tv = mile("vector", Z["mV"])
                    wait("scalar", Z["mV"], tv)
                    def _rs(a):
                        a.activation(stats[:, 5:6], stats[:, 4:5], AF.Ln, bias=fconst[:, 0:1], scale=-1.0)
                        a.drain()
                        a.activation(stats[:, 6:7], stats[:, 5:6], AF.Exp, scale=-0.5)
                    A(_rs)
                    ta = mile("scalar", Z["mA"])
                    wait("vector", Z["mA"], ta)
                    def _ng(v):
                        v.tensor_scalar(stats[:, 7:8], stats[:, 2:3], stats[:, 6:7], -1.0,
                                        ALU.mult, ALU.mult)
                    V(_ng)
                    tv2 = mile("vector", Z["mV"])
                    wait("scalar", Z["mV"], tv2)
                    wsem = Z["xnw0"] if slot == 0 else Z["xnw1"]
                    if it >= 2:
                        wait("scalar", wsem, 16 * (it // 2))
                    def _xn(a, xv=xv, slot=slot, own=own):
                        a.activation(xnb[:, slot * DM:(slot + 1) * DM], xv, AF.Identity,
                                     bias=stats[:, 7:8], scale=stats[:, 6:7])
                        if own:
                            a.drain()
                            a.activation(xnf[:, :], xv, AF.Identity, bias=stats[:, 7:8],
                                         scale=stats[:, 6:7])
                    A(_xn)
                    ta2 = mile("scalar", Z["mA"])
                    tile_a2.append(ta2)
                    wsem.inc(16)
                    def _wr(a, it=it, slot=slot, wsem=wsem):
                        a.dma_start(xn_scr[it * 128:(it + 1) * 128, :],
                                    xnb[:, slot * DM:(slot + 1) * DM]).then_inc(wsem.h, 16)
                    A(_wr)

                    if own:
                        wait("tensor", Z["mA"], ta2)
                        ev_miles = []
                        if _gstep < 1:
                            continue
                        for i in range(NI):
                            pb = (i % 2) * 512
                            if i >= 2:
                                wait("tensor", Z["mV"], ev_miles[i - 2])
                            def _tr(t, i=i, pb=pb):
                                t.transpose(psT[:, pb:pb + 128],
                                            xnf[:, i * 128:(i + 1) * 128], identf[:, :])
                            T(_tr)
                            tt = mile("tensor", Z["mT"])
                            wait("vector", Z["mT"], tt)
                            def _ev(v, i=i, pb=pb):
                                v.tensor_copy(xnTf[:, i * 128:(i + 1) * 128], psT[:, pb:pb + 128])
                            V(_ev)
                            ev_miles.append(mile("vector", Z["mV"]))
                        xnf_free_T[0] = Z["mT"].n
                        if _gstep < 2:
                            continue
                        wait("tensor", Z["mV"], ev_miles[-1])
                        for i in range(NI):
                            def _mm(t, i=i):
                                t.matmul(psG[0:NE, 0:128], gwT_sb[:, i * NE:(i + 1) * NE],
                                         xnTf[:, i * 128:(i + 1) * 128],
                                         start=(i == 0), stop=(i == NI - 1))
                            T(_mm)
                        tt2 = mile("tensor", Z["mT"])
                        if _gstep < 3:
                            continue
                        wait("scalar", Z["mT"], tt2)
                        def _lg(a):
                            a.activation(Lbuf[0:NE, 0:128], psG[0:NE, 0:128], AF.Identity,
                                         bias=gb_sb[0:NE, 0:1])
                        A(_lg)
                        ta3 = mile("scalar", Z["mA"])
                        if _gstep < 4:
                            continue
                        wait("tensor", Z["mA"], ta3)
                        def _trL(t):
                            t.transpose(psL[:, 0:128], Lbuf[:, :], identf[:, :])
                        T(_trL)
                        tt3 = mile("tensor", Z["mT"])
                        if _gstep < 5:
                            continue
                        wait("vector", Z["mT"], tt3)
                        def _mk(v):
                            v.tensor_copy(logt[:, 0:NE], psL[:, 0:NE])
                            v.drain()
                            v.reduce_max(logt[:, 8:9], logt[:, 0:NE], AX.X)
                            v.drain()
                            v.tensor_scalar(maskb[:, 0:NE], logt[:, 0:NE], logt[:, 8:9],
                                            None, ALU.is_ge)
                            v.drain()
                            v.tensor_copy(logt[:, 9:10], maskb[:, 0:1])
                            for j in range(1, NE):
                                v.drain()
                                v.scalar_tensor_tensor(logt[:, 10:11], logt[:, 9:10], 1.0,
                                                       maskb[:, j:j + 1], ALU.mult, ALU.mult)
                                v.drain()
                                v.tensor_sub(maskb[:, j:j + 1], maskb[:, j:j + 1], logt[:, 10:11])
                                v.drain()
                                v.scalar_tensor_tensor(logt[:, 9:10], logt[:, 9:10], 1.0,
                                                       maskb[:, j:j + 1], ALU.mult, ALU.max)
                        V(_mk)
                        tv5 = mile("vector", Z["mV"])
                        if _gstep < 6:
                            continue
                        wait("tensor", Z["mV"], tv5)
                        def _trM(t):
                            t.transpose(psMk[:, 0:128], maskb[:, :], identb[:, :])
                        T(_trM)
                        tt4 = mile("tensor", Z["mT"])
                        wait("vector", Z["mT"], tt4)
                        def _mke(v, it=it):
                            v.tensor_copy(maskT_sb[0:NE, it * 128:(it + 1) * 128], psMk[0:NE, 0:128])
                        V(_mke)
                        mile("vector", Z["mV"])

                wait("scalar", Z["mV"], Z["mV"].n)
                Z["mskw"].inc(16)
                def _wm(a):
                    a.dma_start(mask_scr[:, :], maskT_sb[:, :]).then_inc(Z["mskw"].h, 16)
                A(_wm)
                barrier()

        with ExitStack() as st13:
            E13 = st13.enter_context
            xnT = E13(nc.sbuf_tensor("xnT", [128, NI * HALF], BF16))

            if "p1" in phases:
                with ExitStack() as stp:
                    E = stp.enter_context
                    cosk_sb = E(nc.sbuf_tensor("cosk_sb", [128, S], BF16))
                    sink_sb = E(nc.sbuf_tensor("sink_sb", [128, S], BF16))
                    cosq_sb = E(nc.sbuf_tensor("cosq_sb", [128, HALF], BF16))
                    sinq_sb = E(nc.sbuf_tensor("sinq_sb", [128, HALF], BF16))
                    wstat = E(nc.sbuf_tensor("wstat", [128, 2 * DM], BF16))
                    evq = E(nc.sbuf_tensor("evq", [128, 2 * 512], BF16))
                    rt1 = E(nc.sbuf_tensor("rt1", [128, 512], BF16))
                    rt2 = E(nc.sbuf_tensor("rt2", [128, 512], BF16))
                    rfin = E(nc.sbuf_tensor("rfin", [128, 2 * 512], BF16))
                    psP = [E(nc.psum_tensor("psP0", [128, 2048], F32)),
                           E(nc.psum_tensor("psP1", [128, 2048], F32))]

                    def _ct(s):
                        for o_, i_ in [(cosk_sb, cosk), (sink_sb, sinkn),
                                       (cosq_sb, cosq), (sinq_sb, sinqn)]:
                            s.dma_start(o_[:, :], i_).then_inc(Z["cons"].h, 16)
                    SY(_ct)
                    Z["cons"].inc(16 * 4)
                    wait("sync", Z["cons"], Z["cons"].n)
                    wait("sync", Z["xnw0"], Z["xnw0"].n)
                    wait("sync", Z["xnw1"], Z["xnw1"].n)

                    job_tensor_mile = []
                    job_drain_mile = []
                    slot_writes = {0: 0, 1: 0}
                    slot_last_vm = {0: 0, 1: 0}
                    job_idx = 0
                    for phase_i, half_other in enumerate([True, False]):
                        roff = 2048 if half_other else 0
                        if phase_i == 1:
                            wait("sync", Z["mT"], job_tensor_mile[-1])
                        for i in range(NI):
                            Z["xnt"].inc(16)
                            def _xl(s, i=i, roff=roff):
                                s.dma_start_transpose(
                                    xnT[:, i * HALF:(i + 1) * HALF],
                                    xn_scr[roff:roff + HALF, i * 128:(i + 1) * 128]
                                ).then_inc(Z["xnt"].h, 16)
                            SY(_xl)
                        xnt_total = Z["xnt"].n

                        jobs = [("k", ot) for ot in range(NKV)] + [("v", ot) for ot in range(NKV)]
                        if not half_other:
                            jobs += [("q", ot) for ot in range(NH)]

                        for kind, ot in jobs:
                            wslot = job_idx % 2
                            wsem = Z["w1a"] if wslot == 0 else Z["w1b"]
                            if job_idx >= 2:
                                wait("gpsimd", Z["mT"], job_tensor_mile[job_idx - 2])
                            wt = wsem.inc(16)
                            wmat = {"k": wk, "v": wv, "q": wq}[kind]
                            def _wl(g, wmat=wmat, ot=ot, wslot=wslot, wsem=wsem):
                                g.dma_start(
                                    wstat[:, wslot * DM:(wslot + 1) * DM],
                                    wmat[ot, :, :]
                                ).then_inc(wsem.h, 16)
                            G(_wl)

                            grp = job_idx % 2
                            ps = psP[grp]
                            wait("tensor", wsem, wt)
                            wait("tensor", Z["xnt"], xnt_total)
                            if job_idx >= 2:
                                am, vm = job_drain_mile[job_idx - 2]
                                wait("tensor", Z["mA"], am)
                                if vm:
                                    wait("tensor", Z["mV"], vm)
                            for i in range(NI):
                                def _mm(t, i=i, wslot=wslot, ps=ps):
                                    for tc in range(4):
                                        t.matmul(ps[:, tc * 512:(tc + 1) * 512],
                                                 wstat[:, wslot * DM + i * 128:
                                                       wslot * DM + (i + 1) * 128],
                                                 xnT[:, i * HALF + tc * 512:
                                                     i * HALF + (tc + 1) * 512],
                                                 start=(i == 0), stop=(i == NI - 1))
                                T(_mm)
                            tjob = mile("tensor", Z["mT"])

                            bias_col = {"k": bk_sb, "v": bv_sb, "q": bq_sb}[kind][:, ot:ot + 1]
                            rope = kind in ("k", "q")
                            if kind == "q":
                                cos_t, sin_t, dst, dcol0 = cosq_sb, sinq_sb, q_scrT, 0
                            elif kind == "k":
                                cos_t, sin_t, dst, dcol0 = cosk_sb, sink_sb, k_scrT, roff
                            else:
                                cos_t, sin_t, dst, dcol0 = None, None, v_scrT, roff

                            last_am = 0
                            last_vm = 0
                            for tc in range(4):
                                eslot = tc % 2
                                osem = Z["wo0"] if eslot == 0 else Z["wo1"]
                                wait("scalar", Z["mT"], tjob)
                                if slot_writes[eslot] >= 1:
                                    wait("scalar", osem, 16 * slot_writes[eslot])
                                if slot_last_vm[eslot] > 0:
                                    wait("scalar", Z["mV"], slot_last_vm[eslot])
                                def _ev(a, tc=tc, eslot=eslot, bias_col=bias_col, ps=ps):
                                    a.activation(evq[:, eslot * 512:(eslot + 1) * 512],
                                                 ps[:, tc * 512:(tc + 1) * 512], AF.Identity,
                                                 bias=bias_col)
                                A(_ev)
                                taev = mile("scalar", Z["mA"])
                                last_am = taev
                                if rope:
                                    wait("tensor", Z["mA"], taev)
                                    def _pm(t, tc=tc, eslot=eslot, ps=ps):
                                        t.matmul(ps[:, tc * 512:(tc + 1) * 512], perm64[:, :],
                                                 evq[:, eslot * 512:(eslot + 1) * 512],
                                                 start=True, stop=True)
                                    T(_pm)
                                    ttp = mile("tensor", Z["mT"])
                                    wait("vector", Z["mT"], ttp)
                                    if slot_writes[eslot] >= 1:
                                        wait("vector", osem, 16 * slot_writes[eslot])
                                    ccol = (tc * 512) if kind == "q" else (roff + tc * 512)
                                    def _rp(v, tc=tc, eslot=eslot, ccol=ccol,
                                            cos_t=cos_t, sin_t=sin_t, ps=ps):
                                        v.tensor_mul(rt1[:, :],
                                                     evq[:, eslot * 512:(eslot + 1) * 512],
                                                     cos_t[:, ccol:ccol + 512])
                                        v.tensor_mul(rt2[:, :],
                                                     ps[:, tc * 512:(tc + 1) * 512],
                                                     sin_t[:, ccol:ccol + 512])
                                        v.drain()
                                        v.tensor_add(rfin[:, eslot * 512:(eslot + 1) * 512],
                                                     rt1[:, :], rt2[:, :])
                                    V(_rp)
                                    tvr = mile("vector", Z["mV"])
                                    last_vm = tvr
                                    slot_last_vm[eslot] = tvr
                                    wait("sync", Z["mV"], tvr)
                                    src_buf = rfin
                                else:
                                    wait("sync", Z["mA"], taev)
                                    src_buf = evq
                                slot_writes[eslot] += 1
                                osem.inc(16)
                                def _wo(s, tc=tc, eslot=eslot, dst=dst, ot=ot,
                                        dcol0=dcol0, src_buf=src_buf, osem=osem):
                                    s.dma_start(dst[ot * 128:(ot + 1) * 128,
                                                    dcol0 + tc * 512:dcol0 + (tc + 1) * 512],
                                                src_buf[:, eslot * 512:(eslot + 1) * 512]
                                                ).then_inc(osem.h, 16)
                                SY(_wo)
                            job_tensor_mile.append(Z["mT"].n)
                            job_drain_mile.append((last_am, last_vm))
                            job_idx += 1
                    barrier()

            if "p2" in phases:
                with ExitStack() as stp:
                    E = stp.enter_context
                    kT_b = E(nc.sbuf_tensor("kT_b", [128, S], BF16))
                    vt_b = E(nc.sbuf_tensor("vt_b", [128, 2 * S], BF16))
                    qT_b = E(nc.sbuf_tensor("qT_b", [128, 4 * HALF], BF16))
                    et_b = E(nc.sbuf_tensor("et_b", [128, 2 * 1024], BF16))
                    sev = E(nc.sbuf_tensor("sev", [128, 2 * 512], BF16))
                    denf = E(nc.sbuf_tensor("denf", [1, 2 * 512], F32))
                    denb = E(nc.sbuf_tensor("denb", [1, 2 * 512], BF16))
                    ps_st = E(nc.psum_tensor("ps_st", [128, 2048], F32))
                    ps_o = E(nc.psum_tensor("ps_o", [128, 1024], F32))
                    ps_d0 = E(nc.psum_tensor("ps_d0", [128, 512], F32))
                    ps_d1 = E(nc.psum_tensor("ps_d1", [128, 512], F32))

                    wait("sync", Z["wo0"], Z["wo0"].n)
                    wait("sync", Z["wo1"], Z["wo1"].n)

                    kvh_last_mile = []
                    pair_last_mile = []
                    sev_uses = {0: 0, 1: 0}
                    den_uses = {0: 0, 1: 0}
                    pair_idx = 0
                    for kvh in range(NKV):
                        if kvh >= 1:
                            wait("sync", Z["mT"], kvh_last_mile[-1])
                        kt = Z["kld"].inc(16)
                        def _kl(s, kvh=kvh):
                            s.dma_start(kT_b[:, :],
                                        k_scrT[kvh * 128:(kvh + 1) * 128, :]).then_inc(Z["kld"].h, 16)
                        SY(_kl)
                        vslot = kvh % 2
                        vsem = Z["vld0"] if vslot == 0 else Z["vld1"]
                        if kvh >= 2:
                            wait("sync", Z["mT"], kvh_last_mile[-2])
                        for kp in range(32):
                            vsem.inc(16)
                            def _vl(s, kvh=kvh, vslot=vslot, kp=kp, vsem=vsem):
                                s.dma_start_transpose(
                                    vt_b[:, vslot * S + kp * 128:vslot * S + (kp + 1) * 128],
                                    v_scrT[kvh * 128:(kvh + 1) * 128, kp * 128:(kp + 1) * 128]
                                ).then_inc(vsem.h, 16)
                            SY(_vl)
                        vt = vsem.n

                        for pr in range(2):
                            qh0 = kvh * 4 + pr * 2
                            qslotbase = (pair_idx % 2) * 2
                            qsem = Z["qld0"] if (pair_idx % 2) == 0 else Z["qld1"]
                            if pair_idx >= 2:
                                wait("sync", Z["mT"], pair_last_mile[pair_idx - 2])
                            for qq in range(2):
                                qsem.inc(16)
                                def _ql(s, qh=qh0 + qq, sl=qslotbase + qq, qsem=qsem):
                                    s.dma_start(qT_b[:, sl * HALF:(sl + 1) * HALF],
                                                q_scrT[qh * 128:(qh + 1) * 128, :]
                                                ).then_inc(qsem.h, 16)
                                SY(_ql)
                            qt = qsem.n

                            wait("tensor", Z["kld"], kt)
                            wait("tensor", vsem, vt)
                            wait("tensor", qsem, qt)
                            for qc in range(NQC):
                                wait("tensor", Z["mA"], Z["mA"].n)
                                wait("tensor", Z["mV"], Z["mV"].n)
                                wait("tensor", Z["mA2"], Z["mA2"].n)
                                wait("scalar", Z["mT"], Z["mT"].n)
                                st_miles = {}
                                exp_miles = {}
                                out_miles = {}
                                for kp in range(32):
                                    par = (kp % 2) * 1024
                                    if kp >= 2:
                                        wait("tensor", Z["mA2"], exp_miles[kp - 2])
                                    def _st(t, kp=kp, par=par, qslotbase=qslotbase, qc=qc):
                                        for qq in range(2):
                                            t.matmul(ps_st[:, par + qq * 512:par + (qq + 1) * 512],
                                                     kT_b[:, kp * 128:(kp + 1) * 128],
                                                     qT_b[:, (qslotbase + qq) * HALF + qc * 512:
                                                          (qslotbase + qq) * HALF + (qc + 1) * 512],
                                                     start=True, stop=True)
                                    T(_st)
                                    st_miles[kp] = mile("tensor", Z["mT"])
                                    wait("scalar", Z["mT"], st_miles[kp])
                                    if kp >= 2:
                                        wait("scalar", Z["mT"], out_miles[kp - 2])
                                    eslot = (kp % 2) * 1024
                                    def _ex(a, par=par, eslot=eslot):
                                        a.activation(et_b[:, eslot:eslot + 1024],
                                                     ps_st[:, par:par + 1024], AF.Exp,
                                                     bias=fconst[:, 1:2], scale=INV_SQRT_D)
                                    A(_ex)
                                    exp_miles[kp] = mile("scalar", Z["mA2"])
                                    wait("tensor", Z["mA2"], exp_miles[kp])
                                    def _ou(t, kp=kp, eslot=eslot, vslot=vslot):
                                        first = (kp == 0)
                                        last = (kp == 31)
                                        for qq in range(2):
                                            t.matmul(ps_o[:, qq * 512:(qq + 1) * 512],
                                                     vt_b[:, vslot * S + kp * 128:
                                                          vslot * S + (kp + 1) * 128],
                                                     et_b[:, eslot + qq * 512:eslot + (qq + 1) * 512],
                                                     start=first, stop=last)
                                        t.matmul(ps_d0[0:1, :], onescol[:, :],
                                                 et_b[:, eslot:eslot + 512],
                                                 start=first, stop=last)
                                        t.matmul(ps_d1[0:1, :], onescol[:, :],
                                                 et_b[:, eslot + 512:eslot + 1024],
                                                 start=first, stop=last)
                                    T(_ou)
                                    out_miles[kp] = mile("tensor", Z["mT"])

                                for qq in range(2):
                                    qh = qh0 + qq
                                    sslot = qq
                                    so = Z["so0"] if sslot == 0 else Z["so1"]
                                    wait("scalar", Z["mT"], out_miles[31])
                                    if sev_uses[sslot] >= 1:
                                        wait("scalar", so, 16 * sev_uses[sslot])
                                    def _se(a, sslot=sslot):
                                        a.activation(sev[:, sslot * 512:(sslot + 1) * 512],
                                                     ps_o[:, sslot * 512:(sslot + 1) * 512],
                                                     AF.Copy)
                                    A(_se)
                                    mile("scalar", Z["mA"])
                                    sev_uses[sslot] += 1
                                    so.inc(16)
                                    def _sw(a, qh=qh, qc=qc, sslot=sslot, so=so):
                                        a.dma_start(s_scrT[qh * 128:(qh + 1) * 128,
                                                           qc * 512:(qc + 1) * 512],
                                                    sev[:, sslot * 512:(sslot + 1) * 512]
                                                    ).then_inc(so.h, 16)
                                    A(_sw)
                                    dsem = Z["dn0"] if qq == 0 else Z["dn1"]
                                    psd = ps_d0 if qq == 0 else ps_d1
                                    wait("vector", Z["mT"], out_miles[31])
                                    if den_uses[qq] >= 1:
                                        wait("vector", dsem, 16 * den_uses[qq])
                                    def _dn(v, qq=qq, psd=psd):
                                        v.reciprocal(denf[0:1, qq * 512:(qq + 1) * 512],
                                                     psd[0:1, :])
                                        v.drain()
                                        v.tensor_copy(denb[0:1, qq * 512:(qq + 1) * 512],
                                                      denf[0:1, qq * 512:(qq + 1) * 512])
                                    V(_dn)
                                    tdn = mile("vector", Z["mV"])
                                    wait("scalar", Z["mV"], tdn)
                                    den_uses[qq] += 1
                                    dsem.inc(16)
                                    def _dw(a, qh=qh, qc=qc, qq=qq, dsem=dsem):
                                        a.dma_start(denb_scr[qh:qh + 1, qc * 512:(qc + 1) * 512],
                                                    denb[0:1, qq * 512:(qq + 1) * 512]
                                                    ).then_inc(dsem.h, 16)
                                    A(_dw)
                            pair_last_mile.append(Z["mT"].n)
                            pair_idx += 1
                        kvh_last_mile.append(Z["mT"].n)
                    barrier()

            if "p3" in phases:
                with ExitStack() as stp:
                    E = stp.enter_context
                    w3 = E(nc.sbuf_tensor("w3", [128, 2 * DM], BF16))
                    mrep = E(nc.sbuf_tensor("mrep", [128, 2 * HALF], BF16))
                    silu_b = E(nc.sbuf_tensor("silu_b", [128, 2 * HALF], BF16))
                    gstr = E(nc.sbuf_tensor("gstr", [128, 2 * HALF], BF16))
                    gtmp = E(nc.sbuf_tensor("gtmp", [128, HALF], BF16))
                    psP3 = [E(nc.psum_tensor("psQ0", [128, 2048], F32)),
                            E(nc.psum_tensor("psQ1", [128, 2048], F32))]

                    wait("gpsimd", Z["mskw"], Z["mskw"].n)
                    wait("sync", Z["mskw"], Z["mskw"].n)

                    job_t_mile = []
                    job_a_gate = []
                    job_v_mile = []
                    gw_uses = {0: 0, 1: 0}
                    jdx = 0
                    for ot in range(NI):
                        gslot = ot % 2
                        gsem = Z["g30"] if gslot == 0 else Z["g31"]
                        for e in range(NE):
                            wslot = jdx % 2
                            wsem = Z["w3a"] if wslot == 0 else Z["w3b"]
                            if jdx >= 2:
                                wait("gpsimd", Z["mT"], job_t_mile[jdx - 2])
                            wt = wsem.inc(16)
                            def _wl(g, e=e, ot=ot, wslot=wslot, wsem=wsem):
                                g.dma_start(
                                    w3[:, wslot * DM:(wslot + 1) * DM],
                                    ew2[e, ot, :, :]
                                ).then_inc(wsem.h, 16)
                            G(_wl)
                            mslot = jdx % 2
                            msem = Z["mr0"] if mslot == 0 else Z["mr1"]
                            if jdx >= 2:
                                wait("sync", Z["mV"], job_v_mile[jdx - 2])
                            mt = msem.inc(16)
                            def _mb(s, e=e, mslot=mslot, msem=msem):
                                s.dma_start(mrep[:, mslot * HALF:(mslot + 1) * HALF],
                                            mask_scr[e:e + 1, :].broadcast_to([128, HALF])
                                            ).then_inc(msem.h, 16)
                            SY(_mb)

                            grp = jdx % 2
                            ps = psP3[grp]
                            wait("tensor", wsem, wt)
                            if jdx >= 2:
                                wait("tensor", Z["mA"], job_a_gate[jdx - 2])
                            for i in range(NI):
                                def _mm(t, i=i, wslot=wslot, ps=ps):
                                    for tc in range(4):
                                        t.matmul(ps[:, tc * 512:(tc + 1) * 512],
                                                 w3[:, wslot * DM + i * 128:
                                                     wslot * DM + (i + 1) * 128],
                                                 xnT[:, i * HALF + tc * 512:
                                                     i * HALF + (tc + 1) * 512],
                                                 start=(i == 0), stop=(i == NI - 1))
                                T(_mm)
                            tj = mile("tensor", Z["mT"])
                            job_t_mile.append(tj)

                            bias_col = eb_sb[:, e * NI + ot:e * NI + ot + 1]
                            sislot = jdx % 2
                            wait("scalar", Z["mT"], tj)
                            if jdx >= 2:
                                wait("scalar", Z["mV"], job_v_mile[jdx - 2])
                            def _si(a, sislot=sislot, bias_col=bias_col, ps=ps):
                                a.activation(silu_b[:, sislot * HALF:(sislot + 1) * HALF],
                                             ps[:, 0:2048], AF.Silu, bias=bias_col)
                            A(_si)
                            taj = mile("scalar", Z["mA"])
                            job_a_gate.append(taj)

                            wait("vector", Z["mA"], taj)
                            wait("vector", msem, mt)
                            if e == 0 and gw_uses[gslot] >= 1:
                                wait("vector", gsem, 16 * gw_uses[gslot])
                            def _ga(v, e=e, sislot=sislot, mslot=mslot, gslot=gslot):
                                if e == 0:
                                    v.tensor_mul(gstr[:, gslot * HALF:(gslot + 1) * HALF],
                                                 silu_b[:, sislot * HALF:(sislot + 1) * HALF],
                                                 mrep[:, mslot * HALF:(mslot + 1) * HALF])
                                else:
                                    v.tensor_mul(gtmp[:, :],
                                                 silu_b[:, sislot * HALF:(sislot + 1) * HALF],
                                                 mrep[:, mslot * HALF:(mslot + 1) * HALF])
                                    v.drain()
                                    v.tensor_add(gstr[:, gslot * HALF:(gslot + 1) * HALF],
                                                 gstr[:, gslot * HALF:(gslot + 1) * HALF],
                                                 gtmp[:, :])
                            V(_ga)
                            tvj = mile("vector", Z["mV"])
                            job_v_mile.append(tvj)
                            jdx += 1

                        wait("scalar", Z["mV"], job_v_mile[-1])
                        gw_uses[gslot] += 1
                        gsem.inc(16)
                        def _gw(a, ot=ot, gslot=gslot, gsem=gsem):
                            a.dma_start(g_scrT[ot * 128:(ot + 1) * 128, :],
                                        gstr[:, gslot * HALF:(gslot + 1) * HALF]
                                        ).then_inc(gsem.h, 16)
                        A(_gw)
                    barrier()

        if "p4" in phases:
            with ExitStack() as stp:
                E = stp.enter_context
                sgx = E(nc.sbuf_tensor("sgx", [128, 2 * DM], F32))
                sblk = E(nc.sbuf_tensor("sblk", [128, 2 * DM], BF16))
                gblk = E(nc.sbuf_tensor("gblk", [128, 2 * DM], BF16))
                rcp = E(nc.sbuf_tensor("rcp", [128, 2 * NH], BF16))
                c1b = E(nc.sbuf_tensor("c1b", [128, 256], F32))
                orow = E(nc.sbuf_tensor("orow", [128, 2 * DM], F32))

                for nm in ["so0", "so1", "dn0", "dn1", "g30", "g31"]:
                    wait("sync", Z[nm], Z[nm].n)
                    wait("scalar", Z[nm], Z[nm].n)

                tile_v_mile = []
                ow_uses = {0: 0, 1: 0}
                for tt in range(NTO):
                    slot = tt % 2
                    xs = Z["x40"] if slot == 0 else Z["x41"]
                    ss = Z["sg0"] if slot == 0 else Z["sg1"]
                    if tt >= 2:
                        wait("sync", Z["mV"], tile_v_mile[tt - 2])
                        wait("scalar", Z["mV"], tile_v_mile[tt - 2])
                        wait("gpsimd", Z["mV"], tile_v_mile[tt - 2])
                    xt = xs.inc(16)
                    def _xl(g, tt=tt, slot=slot, xs=xs):
                        g.dma_start(sgx[:, slot * DM:(slot + 1) * DM],
                                    x_in[tt * 128:(tt + 1) * 128, :]).then_inc(xs.h, 16)
                    G(_xl)
                    for o in range(NI):
                        ss.inc(16)
                        def _sl(s, tt=tt, o=o, slot=slot, ss=ss):
                            s.dma_start_transpose(
                                sblk[:, slot * DM + o * 128:slot * DM + (o + 1) * 128],
                                s_scrT[o * 128:(o + 1) * 128, tt * 128:(tt + 1) * 128]
                            ).then_inc(ss.h, 16)
                        SY(_sl)
                    for o in range(NI):
                        ss.inc(16)
                        def _gl(s, tt=tt, o=o, slot=slot, ss=ss):
                            s.dma_start_transpose(
                                gblk[:, slot * DM + o * 128:slot * DM + (o + 1) * 128],
                                g_scrT[o * 128:(o + 1) * 128, tt * 128:(tt + 1) * 128]
                            ).then_inc(ss.h, 16)
                        SY(_gl)
                    ss.inc(16)
                    def _rl(s, tt=tt, slot=slot, ss=ss):
                        s.dma_start_transpose(rcp[:, slot * NH:(slot + 1) * NH],
                                              denb_scr[:, tt * 128:(tt + 1) * 128]
                                              ).then_inc(ss.h, 16)
                    SY(_rl)
                    st_total = ss.n
                    xt_total = xt

                    wait("vector", xs, xt_total)
                    wait("vector", ss, st_total)
                    if ow_uses[slot] >= 1:
                        osem = Z["ow0"] if slot == 0 else Z["ow1"]
                        wait("vector", osem, 16 * ow_uses[slot])
                    def _cb(v, slot=slot):
                        for o in reversed(range(NI)):
                            cs = (o % 2) * 128
                            v.tensor_mul(c1b[:, cs:cs + 128],
                                         sblk[:, slot * DM + o * 128:slot * DM + (o + 1) * 128],
                                         gblk[:, slot * DM + o * 128:slot * DM + (o + 1) * 128])
                            v.drain()
                            v.scalar_tensor_tensor(
                                orow[:, slot * DM + o * 128:slot * DM + (o + 1) * 128],
                                c1b[:, cs:cs + 128],
                                rcp[:, slot * NH + o:slot * NH + o + 1],
                                sgx[:, slot * DM + o * 128:slot * DM + (o + 1) * 128],
                                ALU.mult, ALU.add)
                            v.drain()
                    V(_cb)
                    tvt = mile("vector", Z["mV"])
                    tile_v_mile.append(tvt)
                    osem = Z["ow0"] if slot == 0 else Z["ow1"]
                    wait("sync", Z["mV"], tvt)
                    ow_uses[slot] += 1
                    osem.inc(16)
                    def _ow(s, tt=tt, slot=slot, osem=osem):
                        s.dma_start(out[tt * 128:(tt + 1) * 128, :],
                                    orow[:, slot * DM:(slot + 1) * DM]).then_inc(osem.h, 16)
                    SY(_ow)

                wait("sync", Z["ow0"], Z["ow0"].n)
                wait("sync", Z["ow1"], Z["ow1"].n)

        dma_sems = ["cons", "xld0", "xld1", "xnw0", "xnw1", "mskw", "xnt",
                    "w1a", "w1b", "wo0", "wo1", "kld", "vld0", "vld1",
                    "qld0", "qld1", "so0", "so1", "dn0", "dn1",
                    "w3a", "w3b", "mr0", "mr1", "g30", "g31",
                    "x40", "x41", "sg0", "sg1", "ow0", "ow1"]
        for nm in dma_sems:
            sem = Z[nm]
            if sem.n > 0:
                eng = "gpsimd" if nm in ("w1a", "w1b", "w3a", "w3b") else "sync"
                wait(eng, sem, sem.n)

        with nc.Block() as blk:
            @blk.tensor
            def _(eng):
                for fn in th["tensor"]:
                    fn(eng)
            @blk.vector
            def _(eng):
                for fn in th["vector"]:
                    fn(eng)
            @blk.scalar
            def _(eng):
                for fn in th["scalar"]:
                    fn(eng)
            @blk.gpsimd
            def _(eng):
                for fn in th["gpsimd"]:
                    fn(eng)
            @blk.sync
            def _(eng):
                for fn in th["sync"]:
                    fn(eng)

    return nc



def _tile_w(w):
    no = w.shape[1] // 128
    return np.ascontiguousarray(
        w.reshape(NI, 128, no, 128).transpose(2, 1, 0, 3).reshape(no, 128, DM))


def _rope_tables(pos):
    j = np.arange(64, dtype=np.float64)
    inv = 1.0 / (10000.0 ** (2.0 * j / DH))
    ang = pos[None, :].astype(np.float64) * inv[:, None]
    c = np.cos(ang)
    s = np.sin(ang)
    cos2 = np.concatenate([c, c], axis=0)
    sin2 = np.concatenate([-s, s], axis=0)
    return cos2.astype(np.float32), sin2.astype(np.float32)


def make_inputs(core, full):
    b, h = core // 2, core % 2
    bf16 = ml_dtypes.bfloat16
    x = np.asarray(full["L0_attn_x"][b])
    own = slice(h * HALF, (h + 1) * HALF)
    oth = slice((1 - h) * HALF, (2 - h) * HALF)
    x_r = np.ascontiguousarray(np.concatenate([x[own], x[oth]], axis=0), dtype=np.float32)

    pos_own = np.arange(h * HALF, (h + 1) * HALF)
    pos_oth = np.arange((1 - h) * HALF, (2 - h) * HALF)
    pos_all = np.concatenate([pos_own, pos_oth])
    cq, sq = _rope_tables(pos_own)
    ck, sk = _rope_tables(pos_all)

    ident = np.eye(128, dtype=np.float32)
    perm = np.zeros((128, 128), dtype=np.float32)
    for p in range(128):
        perm[p, (p + 64) % 128] = 1.0

    ins = {
        "x": x_r,
        "wq2": _tile_w(np.asarray(full["L0_attn_wq"], dtype=np.float32)),
        "wk2": _tile_w(np.asarray(full["L0_attn_wk"], dtype=np.float32)),
        "wv2": _tile_w(np.asarray(full["L0_attn_wv"], dtype=np.float32)),
        "bq_t": np.ascontiguousarray(np.asarray(full["L0_attn_bq"], dtype=np.float32).reshape(NH, 128).T),
        "bk_t": np.ascontiguousarray(np.asarray(full["L0_attn_bk"], dtype=np.float32).reshape(NKV, 128).T),
        "bv_t": np.ascontiguousarray(np.asarray(full["L0_attn_bv"], dtype=np.float32).reshape(NKV, 128).T),
        "gw2": np.ascontiguousarray(np.asarray(full["L0_ffn_gate_w"], dtype=np.float32)
                                    .reshape(NE, NI, 128).transpose(2, 1, 0).reshape(128, NI * NE)),
        "gb_c": np.ascontiguousarray(np.asarray(full["L0_ffn_gate_b"], dtype=np.float32).reshape(NE, 1)),
        "ew2": np.ascontiguousarray(np.asarray(full["L0_ffn_expert_w"], dtype=np.float32)
                                     .reshape(NE, NI, 128, NI, 128).transpose(0, 1, 4, 3, 2)
                                     .reshape(NE, NI, 128, DM)),
        "eb2": np.ascontiguousarray(np.asarray(full["L0_ffn_expert_b"], dtype=np.float32)
                                     .reshape(NE, NI, 128).transpose(2, 0, 1).reshape(128, NE * NI)),
        "cosq": cq.astype(bf16),
        "sinqn": sq.astype(bf16),
        "cosk": ck.astype(bf16),
        "sinkn": sk.astype(bf16),
        "identf": ident,
        "identb": ident.astype(bf16),
        "perm64": perm.astype(bf16),
        "onescol": np.ones((128, 1), dtype=bf16),
        "fconst": np.broadcast_to(np.array([LN_EPS, EXP_SHIFT], dtype=np.float32), (128, 2)).copy(),
    }
    return ins


def kernel(**inputs):
    nc = build_program()
    in_maps = [make_inputs(c, inputs) for c in range(8)]
    res = run_bass_kernel_spmd(nc, in_maps, core_ids=list(range(8)))
    outp = np.zeros((B, S, DM), dtype=np.float32)
    for c in range(8):
        b, h = c // 2, c % 2
        outp[b, h * HALF:(h + 1) * HALF, :] = res.results[c]["out"]
    return outp


# version 05
# speedup vs baseline: 1.3082x; 1.3082x over previous
import os
import sys
sys.path.insert(0, "/opt/trn_rl_repo")
import numpy as np
import ml_dtypes
from contextlib import ExitStack

import concourse.bass as bass
import concourse.mybir as mybir
from concourse.bass_utils import run_bass_kernel_spmd

F32 = mybir.dt.float32
BF16 = mybir.dt.bfloat16
AF = mybir.ActivationFunctionType
ALU = mybir.AluOpType
AX = mybir.AxisListType

B, S, DM = 4, 4096, 4096
NH, NKV, DH = 32, 8, 128
NE = 8
HALF = S // 2
NI = DM // 128
NTO = HALF // 128
NTF = S // 128
NQC = HALF // 512
KVC = NKV * DH
INV_SQRT_D = float(1.0 / np.sqrt(DH))
EXP_SHIFT = -20.0
LN_EPS = 1e-5

ENGS = ("tensor", "vector", "scalar", "gpsimd", "sync")


class Sem:
    __slots__ = ("h", "n")
    def __init__(self, h):
        self.h = h
        self.n = 0
    def inc(self, k=1):
        self.n += k
        return self.n


def build_program(phases=("p0", "p1", "p2", "p3", "p4")):
    nc = bass.Bass("TRN2", target_bir_lowering=False)
    DT = nc.dram_tensor

    x_in = DT("x", [S, DM], F32, kind="ExternalInput").ap()
    wq = DT("wq2", [NH, 128, DM], F32, kind="ExternalInput").ap()
    wk = DT("wk2", [NKV, 128, DM], F32, kind="ExternalInput").ap()
    wv = DT("wv2", [NKV, 128, DM], F32, kind="ExternalInput").ap()
    bq_t = DT("bq_t", [128, NH], F32, kind="ExternalInput").ap()
    bk_t = DT("bk_t", [128, NKV], F32, kind="ExternalInput").ap()
    bv_t = DT("bv_t", [128, NKV], F32, kind="ExternalInput").ap()
    gw2 = DT("gw2", [128, NI * NE], F32, kind="ExternalInput").ap()
    gb_c = DT("gb_c", [NE, 1], F32, kind="ExternalInput").ap()
    ew2 = DT("ew2", [NE, NI, 128, DM], F32, kind="ExternalInput").ap()
    eb2 = DT("eb2", [128, NE * NI], F32, kind="ExternalInput").ap()
    cosq = DT("cosq", [128, HALF], BF16, kind="ExternalInput").ap()
    sinqn = DT("sinqn", [128, HALF], BF16, kind="ExternalInput").ap()
    cosk = DT("cosk", [128, S], BF16, kind="ExternalInput").ap()
    sinkn = DT("sinkn", [128, S], BF16, kind="ExternalInput").ap()
    identf_in = DT("identf", [128, 128], F32, kind="ExternalInput").ap()
    identb_in = DT("identb", [128, 128], BF16, kind="ExternalInput").ap()
    perm64_in = DT("perm64", [128, 128], BF16, kind="ExternalInput").ap()
    onescol_in = DT("onescol", [128, 1], BF16, kind="ExternalInput").ap()
    fconst_in = DT("fconst", [128, 2], F32, kind="ExternalInput").ap()

    out = DT("out", [HALF, DM], F32, kind="ExternalOutput").ap()

    xn_scr = DT("xn_scr", [S, DM], BF16, kind="ExternalOutput").ap()
    q_scrT = DT("q_scrT", [DM, HALF], BF16, kind="ExternalOutput").ap()
    k_scrT = DT("k_scrT", [KVC, S], BF16, kind="ExternalOutput").ap()
    v_scrT = DT("v_scrT", [KVC, S], BF16, kind="ExternalOutput").ap()
    mask_scr = DT("mask_scr", [NE, HALF], BF16, kind="ExternalOutput").ap()
    denb_scr = DT("denb_scr", [NH, HALF], BF16, kind="ExternalOutput").ap()
    s_scrT = DT("s_scrT", [DM, HALF], BF16, kind="ExternalOutput").ap()
    g_scrT = DT("g_scrT", [DM, HALF], BF16, kind="ExternalOutput").ap()

    th = {e: [] for e in ENGS}

    def T(fn): th["tensor"].append(fn)
    def V(fn): th["vector"].append(fn)
    def A(fn): th["scalar"].append(fn)
    def G(fn): th["gpsimd"].append(fn)
    def SY(fn): th["sync"].append(fn)

    sems = {}

    def wait(e, sem, val):
        if val > 0:
            th[e].append(lambda eng, s=sem.h, v=val: eng.wait_ge(s, v))

    def mile(e, sem):
        tgt = sem.inc()
        def _m(eng, s=sem.h):
            eng.drain()
            eng.sem_inc(s, 1)
        th[e].append(_m)
        return tgt

    with ExitStack() as st0:
        E0 = st0.enter_context
        for name in ["ph", "cons", "xld0", "xld1", "xnw0", "xnw1", "mskw",
                     "xnt", "w1a", "w1b", "wo0", "wo1",
                     "kld", "vld0", "vld1", "qld0", "qld1", "so0", "so1", "dn0", "dn1",
                     "w3a", "w3b", "mr0", "mr1", "g30", "g31",
                     "x40", "x41", "sg0", "sg1", "ow0", "ow1",
                     "mT", "mV", "mA", "mA2", "mG"]:
            sems[name] = Sem(E0(nc.semaphore(name)))
        Z = sems

        def barrier():
            tgt = Z["ph"].inc(5)
            for e in ENGS:
                def _f(eng, t=tgt):
                    eng.drain()
                    eng.sem_inc(Z["ph"].h, 1)
                    eng.wait_ge(Z["ph"].h, t)
                th[e].append(_f)

        identf = E0(nc.sbuf_tensor("identf_sb", [128, 128], F32))
        identb = E0(nc.sbuf_tensor("identb_sb", [128, 128], BF16))
        perm64 = E0(nc.sbuf_tensor("perm64_sb", [128, 128], BF16))
        onescol = E0(nc.sbuf_tensor("onescol_sb", [128, 1], BF16))
        bq_sb = E0(nc.sbuf_tensor("bq_sb", [128, NH], F32))
        bk_sb = E0(nc.sbuf_tensor("bk_sb", [128, NKV], F32))
        bv_sb = E0(nc.sbuf_tensor("bv_sb", [128, NKV], F32))
        eb_sb = E0(nc.sbuf_tensor("eb_sb", [128, NE * NI], F32))
        gb_sb = E0(nc.sbuf_tensor("gb_sb", [NE, 1], F32))
        gwT_sb = E0(nc.sbuf_tensor("gwT_sb", [128, NI * NE], F32))
        maskT_sb = E0(nc.sbuf_tensor("maskT_sb", [NE, HALF], BF16))
        fconst = E0(nc.sbuf_tensor("fconst_sb", [128, 2], F32))

        def consts_fn(s):
            pairs = [
                (identf[:, :], identf_in), (identb[:, :], identb_in),
                (perm64[:, :], perm64_in), (onescol[:, :], onescol_in),
                (bq_sb[:, :], bq_t), (bk_sb[:, :], bk_t), (bv_sb[:, :], bv_t),
                (gb_sb[:, :], gb_c), (fconst[:, :], fconst_in),
                (gwT_sb[:, :], gw2), (eb_sb[:, :], eb2),
            ]
            for o_, i_ in pairs:
                s.dma_start(o_, i_).then_inc(Z["cons"].h, 16)
            assert len(pairs) == 11
        SY(consts_fn)
        Z["cons"].n = 16 * 11

        if "p0" in phases:
            with ExitStack() as stp:
                E = stp.enter_context
                xrow = E(nc.sbuf_tensor("xrow", [128, 2 * DM], F32))
                xnf = E(nc.sbuf_tensor("xnf", [128, DM], F32))
                xnb = E(nc.sbuf_tensor("xnb", [128, 2 * DM], BF16))
                xnTf = E(nc.sbuf_tensor("xnTf", [128, DM], F32))
                stats = E(nc.sbuf_tensor("stats", [128, 16], F32))
                Lbuf = E(nc.sbuf_tensor("Lbuf", [128, 128], F32))
                logt = E(nc.sbuf_tensor("logt", [128, 16], F32))
                maskb = E(nc.sbuf_tensor("maskb", [128, 128], BF16))
                psT = E(nc.psum_tensor("psT0", [128, 1024], F32))
                psG = E(nc.psum_tensor("psG0", [128, 512], F32))
                psL = E(nc.psum_tensor("psL0", [128, 512], F32))
                psMk = E(nc.psum_tensor("psM0", [128, 1024], BF16))

                wait("vector", Z["cons"], Z["cons"].n)
                def _ms(v):
                    v.memset(Lbuf[:, :], 0.0)
                    v.memset(maskb[:, :], 0.0)
                V(_ms)
                mile("vector", Z["mV"])

                tile_a2 = []
                xnf_free_T = [0]
                _ntf = int(os.environ.get("K_NTF", NTF))
                _gate = os.environ.get("K_GATE", "1") == "1"
                _gstep = int(os.environ.get("K_GSTEP", "9"))
                for it in range(_ntf):
                    own = (it < NTO) and _gate
                    slot = it % 2
                    sx = Z["xld0"] if slot == 0 else Z["xld1"]
                    if it >= 2:
                        wait("sync", Z["mA"], tile_a2[it - 2])
                    xt = sx.inc(16)
                    def _ld(s, it=it, slot=slot, sx=sx):
                        s.dma_start(xrow[:, slot * DM:(slot + 1) * DM],
                                    x_in[it * 128:(it + 1) * 128, :]).then_inc(sx.h, 16)
                    SY(_ld)

                    xv = xrow[:, slot * DM:(slot + 1) * DM]
                    wait("vector", sx, xt)
                    wait("vector", Z["mT"], xnf_free_T[0])
                    def _st(v, xv=xv):
                        v.tensor_scalar(xnf[:, :], xv, 1.0, 0.0, ALU.mult, ALU.add, accum_out=stats[:, 0:1])
                        v.drain()
                        v.scalar_tensor_tensor(xnf[:, :], xv, 1.0, xv, ALU.mult, ALU.mult,
                                               accum_out=stats[:, 1:2])
                        v.drain()
                        v.tensor_scalar(stats[:, 2:3], stats[:, 0:1], 1.0 / DM, None, ALU.mult)
                        v.tensor_scalar(stats[:, 3:4], stats[:, 1:2], 1.0 / DM, None, ALU.mult)
                        v.drain()
                        v.scalar_tensor_tensor(stats[:, 4:5], stats[:, 2:3], stats[:, 2:3],
                                               stats[:, 3:4], ALU.mult, ALU.subtract)
                    V(_st)
                    tv = mile("vector", Z["mV"])
                    wait("scalar", Z["mV"], tv)
                    def _rs(a):
                        a.activation(stats[:, 5:6], stats[:, 4:5], AF.Ln, bias=fconst[:, 0:1], scale=-1.0)
                        a.drain()
                        a.activation(stats[:, 6:7], stats[:, 5:6], AF.Exp, scale=-0.5)
                    A(_rs)
                    ta = mile("scalar", Z["mA"])
                    wait("vector", Z["mA"], ta)
                    def _ng(v):
                        v.tensor_scalar(stats[:, 7:8], stats[:, 2:3], stats[:, 6:7], -1.0,
                                        ALU.mult, ALU.mult)
                    V(_ng)
                    tv2 = mile("vector", Z["mV"])
                    wait("scalar", Z["mV"], tv2)
                    wsem = Z["xnw0"] if slot == 0 else Z["xnw1"]
                    if it >= 2:
                        wait("scalar", wsem, 16 * (it // 2))
                    def _xn(a, xv=xv, slot=slot, own=own):
                        a.activation(xnb[:, slot * DM:(slot + 1) * DM], xv, AF.Identity,
                                     bias=stats[:, 7:8], scale=stats[:, 6:7])
                        if own:
                            a.drain()
                            a.activation(xnf[:, :], xv, AF.Identity, bias=stats[:, 7:8],
                                         scale=stats[:, 6:7])
                    A(_xn)
                    ta2 = mile("scalar", Z["mA"])
                    tile_a2.append(ta2)
                    wsem.inc(16)
                    def _wr(a, it=it, slot=slot, wsem=wsem):
                        a.dma_start(xn_scr[it * 128:(it + 1) * 128, :],
                                    xnb[:, slot * DM:(slot + 1) * DM]).then_inc(wsem.h, 16)
                    A(_wr)

                    if own:
                        wait("tensor", Z["mA"], ta2)
                        ev_miles = []
                        if _gstep < 1:
                            continue
                        for i in range(NI):
                            pb = (i % 2) * 512
                            if i >= 2:
                                wait("tensor", Z["mV"], ev_miles[i - 2])
                            def _tr(t, i=i, pb=pb):
                                t.transpose(psT[:, pb:pb + 128],
                                            xnf[:, i * 128:(i + 1) * 128], identf[:, :])
                            T(_tr)
                            tt = mile("tensor", Z["mT"])
                            wait("vector", Z["mT"], tt)
                            def _ev(v, i=i, pb=pb):
                                v.tensor_copy(xnTf[:, i * 128:(i + 1) * 128], psT[:, pb:pb + 128])
                            V(_ev)
                            ev_miles.append(mile("vector", Z["mV"]))
                        xnf_free_T[0] = Z["mT"].n
                        if _gstep < 2:
                            continue
                        wait("tensor", Z["mV"], ev_miles[-1])
                        for i in range(NI):
                            def _mm(t, i=i):
                                t.matmul(psG[0:NE, 0:128], gwT_sb[:, i * NE:(i + 1) * NE],
                                         xnTf[:, i * 128:(i + 1) * 128],
                                         start=(i == 0), stop=(i == NI - 1))
                            T(_mm)
                        tt2 = mile("tensor", Z["mT"])
                        if _gstep < 3:
                            continue
                        wait("scalar", Z["mT"], tt2)
                        def _lg(a):
                            a.activation(Lbuf[0:NE, 0:128], psG[0:NE, 0:128], AF.Identity,
                                         bias=gb_sb[0:NE, 0:1])
                        A(_lg)
                        ta3 = mile("scalar", Z["mA"])
                        if _gstep < 4:
                            continue
                        wait("tensor", Z["mA"], ta3)
                        def _trL(t):
                            t.transpose(psL[:, 0:128], Lbuf[:, :], identf[:, :])
                        T(_trL)
                        tt3 = mile("tensor", Z["mT"])
                        if _gstep < 5:
                            continue
                        wait("vector", Z["mT"], tt3)
                        def _mk(v):
                            v.tensor_copy(logt[:, 0:NE], psL[:, 0:NE])
                            v.drain()
                            v.reduce_max(logt[:, 8:9], logt[:, 0:NE], AX.X)
                            v.drain()
                            v.tensor_scalar(maskb[:, 0:NE], logt[:, 0:NE], logt[:, 8:9],
                                            None, ALU.is_ge)
                            v.drain()
                            v.tensor_copy(logt[:, 9:10], maskb[:, 0:1])
                            for j in range(1, NE):
                                v.drain()
                                v.scalar_tensor_tensor(logt[:, 10:11], logt[:, 9:10], 1.0,
                                                       maskb[:, j:j + 1], ALU.mult, ALU.mult)
                                v.drain()
                                v.tensor_sub(maskb[:, j:j + 1], maskb[:, j:j + 1], logt[:, 10:11])
                                v.drain()
                                v.scalar_tensor_tensor(logt[:, 9:10], logt[:, 9:10], 1.0,
                                                       maskb[:, j:j + 1], ALU.mult, ALU.max)
                        V(_mk)
                        tv5 = mile("vector", Z["mV"])
                        if _gstep < 6:
                            continue
                        wait("tensor", Z["mV"], tv5)
                        def _trM(t):
                            t.transpose(psMk[:, 0:128], maskb[:, :], identb[:, :])
                        T(_trM)
                        tt4 = mile("tensor", Z["mT"])
                        wait("vector", Z["mT"], tt4)
                        def _mke(v, it=it):
                            v.tensor_copy(maskT_sb[0:NE, it * 128:(it + 1) * 128], psMk[0:NE, 0:128])
                        V(_mke)
                        mile("vector", Z["mV"])

                wait("scalar", Z["mV"], Z["mV"].n)
                Z["mskw"].inc(16)
                def _wm(a):
                    a.dma_start(mask_scr[:, :], maskT_sb[:, :]).then_inc(Z["mskw"].h, 16)
                A(_wm)
                barrier()

        with ExitStack() as st13:
            E13 = st13.enter_context
            xnT = E13(nc.sbuf_tensor("xnT", [128, NI * HALF], BF16))

            if "p1" in phases:
                with ExitStack() as stp:
                    E = stp.enter_context
                    cosk_sb = E(nc.sbuf_tensor("cosk_sb", [128, S], BF16))
                    sink_sb = E(nc.sbuf_tensor("sink_sb", [128, S], BF16))
                    cosq_sb = E(nc.sbuf_tensor("cosq_sb", [128, HALF], BF16))
                    sinq_sb = E(nc.sbuf_tensor("sinq_sb", [128, HALF], BF16))
                    wstat = E(nc.sbuf_tensor("wstat", [128, 2 * DM], BF16))
                    evq = E(nc.sbuf_tensor("evq", [128, 2 * 512], BF16))
                    rt1 = E(nc.sbuf_tensor("rt1", [128, 512], BF16))
                    rt2 = E(nc.sbuf_tensor("rt2", [128, 512], BF16))
                    rfin = E(nc.sbuf_tensor("rfin", [128, 2 * 512], BF16))
                    psP = [E(nc.psum_tensor("psP0", [128, 2048], F32)),
                           E(nc.psum_tensor("psP1", [128, 2048], F32))]

                    def _ct(s):
                        for o_, i_ in [(cosk_sb, cosk), (sink_sb, sinkn),
                                       (cosq_sb, cosq), (sinq_sb, sinqn)]:
                            s.dma_start(o_[:, :], i_).then_inc(Z["cons"].h, 16)
                    SY(_ct)
                    Z["cons"].inc(16 * 4)
                    wait("sync", Z["cons"], Z["cons"].n)
                    wait("sync", Z["xnw0"], Z["xnw0"].n)
                    wait("sync", Z["xnw1"], Z["xnw1"].n)

                    job_tensor_mile = []
                    job_drain_mile = []
                    slot_writes = {0: 0, 1: 0}
                    slot_last_vm = {0: 0, 1: 0}
                    job_idx = 0
                    for phase_i, half_other in enumerate([True, False]):
                        roff = 2048 if half_other else 0
                        if phase_i == 1:
                            wait("sync", Z["mT"], job_tensor_mile[-1])
                        for i in range(NI):
                            Z["xnt"].inc(16)
                            def _xl(s, i=i, roff=roff):
                                s.dma_start_transpose(
                                    xnT[:, i * HALF:(i + 1) * HALF],
                                    xn_scr[roff:roff + HALF, i * 128:(i + 1) * 128]
                                ).then_inc(Z["xnt"].h, 16)
                            SY(_xl)
                        xnt_total = Z["xnt"].n

                        jobs = [("k", ot) for ot in range(NKV)] + [("v", ot) for ot in range(NKV)]
                        if not half_other:
                            jobs += [("q", ot) for ot in range(NH)]

                        for kind, ot in jobs:
                            wslot = job_idx % 2
                            wsem = Z["w1a"] if wslot == 0 else Z["w1b"]
                            if job_idx >= 2:
                                wait("gpsimd", Z["mT"], job_tensor_mile[job_idx - 2])
                            wt = wsem.inc(16)
                            wmat = {"k": wk, "v": wv, "q": wq}[kind]
                            def _wl(g, wmat=wmat, ot=ot, wslot=wslot, wsem=wsem):
                                g.dma_start(
                                    wstat[:, wslot * DM:(wslot + 1) * DM],
                                    wmat[ot, :, :]
                                ).then_inc(wsem.h, 16)
                            G(_wl)

                            grp = job_idx % 2
                            ps = psP[grp]
                            wait("tensor", wsem, wt)
                            wait("tensor", Z["xnt"], xnt_total)
                            if job_idx >= 2:
                                am, vm = job_drain_mile[job_idx - 2]
                                wait("tensor", Z["mA"], am)
                                if vm:
                                    wait("tensor", Z["mV"], vm)
                            for i in range(NI):
                                def _mm(t, i=i, wslot=wslot, ps=ps):
                                    for tc in range(4):
                                        t.matmul(ps[:, tc * 512:(tc + 1) * 512],
                                                 wstat[:, wslot * DM + i * 128:
                                                       wslot * DM + (i + 1) * 128],
                                                 xnT[:, i * HALF + tc * 512:
                                                     i * HALF + (tc + 1) * 512],
                                                 start=(i == 0), stop=(i == NI - 1))
                                T(_mm)
                            tjob = mile("tensor", Z["mT"])

                            bias_col = {"k": bk_sb, "v": bv_sb, "q": bq_sb}[kind][:, ot:ot + 1]
                            rope = kind in ("k", "q")
                            if kind == "q":
                                cos_t, sin_t, dst, dcol0 = cosq_sb, sinq_sb, q_scrT, 0
                            elif kind == "k":
                                cos_t, sin_t, dst, dcol0 = cosk_sb, sink_sb, k_scrT, roff
                            else:
                                cos_t, sin_t, dst, dcol0 = None, None, v_scrT, roff

                            last_am = 0
                            last_vm = 0
                            for tc in range(4):
                                eslot = tc % 2
                                osem = Z["wo0"] if eslot == 0 else Z["wo1"]
                                wait("scalar", Z["mT"], tjob)
                                if slot_writes[eslot] >= 1:
                                    wait("scalar", osem, 16 * slot_writes[eslot])
                                if slot_last_vm[eslot] > 0:
                                    wait("scalar", Z["mV"], slot_last_vm[eslot])
                                def _ev(a, tc=tc, eslot=eslot, bias_col=bias_col, ps=ps):
                                    a.activation(evq[:, eslot * 512:(eslot + 1) * 512],
                                                 ps[:, tc * 512:(tc + 1) * 512], AF.Identity,
                                                 bias=bias_col)
                                A(_ev)
                                taev = mile("scalar", Z["mA"])
                                last_am = taev
                                if rope:
                                    wait("tensor", Z["mA"], taev)
                                    def _pm(t, tc=tc, eslot=eslot, ps=ps):
                                        t.matmul(ps[:, tc * 512:(tc + 1) * 512], perm64[:, :],
                                                 evq[:, eslot * 512:(eslot + 1) * 512],
                                                 start=True, stop=True)
                                    T(_pm)
                                    ttp = mile("tensor", Z["mT"])
                                    wait("vector", Z["mT"], ttp)
                                    if slot_writes[eslot] >= 1:
                                        wait("vector", osem, 16 * slot_writes[eslot])
                                    ccol = (tc * 512) if kind == "q" else (roff + tc * 512)
                                    def _rp(v, tc=tc, eslot=eslot, ccol=ccol,
                                            cos_t=cos_t, sin_t=sin_t, ps=ps):
                                        v.tensor_mul(rt1[:, :],
                                                     evq[:, eslot * 512:(eslot + 1) * 512],
                                                     cos_t[:, ccol:ccol + 512])
                                        v.tensor_mul(rt2[:, :],
                                                     ps[:, tc * 512:(tc + 1) * 512],
                                                     sin_t[:, ccol:ccol + 512])
                                        v.drain()
                                        v.tensor_add(rfin[:, eslot * 512:(eslot + 1) * 512],
                                                     rt1[:, :], rt2[:, :])
                                    V(_rp)
                                    tvr = mile("vector", Z["mV"])
                                    last_vm = tvr
                                    slot_last_vm[eslot] = tvr
                                    wait("sync", Z["mV"], tvr)
                                    src_buf = rfin
                                else:
                                    wait("sync", Z["mA"], taev)
                                    src_buf = evq
                                slot_writes[eslot] += 1
                                osem.inc(16)
                                def _wo(s, tc=tc, eslot=eslot, dst=dst, ot=ot,
                                        dcol0=dcol0, src_buf=src_buf, osem=osem):
                                    s.dma_start(dst[ot * 128:(ot + 1) * 128,
                                                    dcol0 + tc * 512:dcol0 + (tc + 1) * 512],
                                                src_buf[:, eslot * 512:(eslot + 1) * 512]
                                                ).then_inc(osem.h, 16)
                                SY(_wo)
                            job_tensor_mile.append(Z["mT"].n)
                            job_drain_mile.append((last_am, last_vm))
                            job_idx += 1
                    barrier()

            if "p2" in phases:
                with ExitStack() as stp:
                    E = stp.enter_context
                    kT_b = E(nc.sbuf_tensor("kT_b", [128, S], BF16))
                    vt_b = E(nc.sbuf_tensor("vt_b", [128, 2 * S], BF16))
                    qT_b = E(nc.sbuf_tensor("qT_b", [128, 4 * HALF], BF16))
                    et_b = E(nc.sbuf_tensor("et_b", [128, 2 * 1024], BF16))
                    sev = E(nc.sbuf_tensor("sev", [128, 2 * 512], BF16))
                    denf = E(nc.sbuf_tensor("denf", [1, 2 * 512], F32))
                    denb = E(nc.sbuf_tensor("denb", [1, 2 * 512], BF16))
                    ps_st = E(nc.psum_tensor("ps_st", [128, 2048], F32))
                    ps_o = E(nc.psum_tensor("ps_o", [128, 1024], F32))
                    ps_d0 = E(nc.psum_tensor("ps_d0", [128, 512], F32))
                    ps_d1 = E(nc.psum_tensor("ps_d1", [128, 512], F32))

                    wait("sync", Z["wo0"], Z["wo0"].n)
                    wait("sync", Z["wo1"], Z["wo1"].n)

                    kvh_last_mile = []
                    pair_last_mile = []
                    sev_uses = {0: 0, 1: 0}
                    den_uses = {0: 0, 1: 0}
                    pair_idx = 0
                    for kvh in range(NKV):
                        if kvh >= 1:
                            wait("sync", Z["mT"], kvh_last_mile[-1])
                        kt = Z["kld"].inc(16)
                        def _kl(s, kvh=kvh):
                            s.dma_start(kT_b[:, :],
                                        k_scrT[kvh * 128:(kvh + 1) * 128, :]).then_inc(Z["kld"].h, 16)
                        SY(_kl)
                        vslot = kvh % 2
                        vsem = Z["vld0"] if vslot == 0 else Z["vld1"]
                        if kvh >= 2:
                            wait("sync", Z["mT"], kvh_last_mile[-2])
                        for kp in range(32):
                            vsem.inc(16)
                            def _vl(s, kvh=kvh, vslot=vslot, kp=kp, vsem=vsem):
                                s.dma_start_transpose(
                                    vt_b[:, vslot * S + kp * 128:vslot * S + (kp + 1) * 128],
                                    v_scrT[kvh * 128:(kvh + 1) * 128, kp * 128:(kp + 1) * 128]
                                ).then_inc(vsem.h, 16)
                            SY(_vl)
                        vt = vsem.n

                        for pr in range(2):
                            qh0 = kvh * 4 + pr * 2
                            qslotbase = (pair_idx % 2) * 2
                            qsem = Z["qld0"] if (pair_idx % 2) == 0 else Z["qld1"]
                            if pair_idx >= 2:
                                wait("sync", Z["mT"], pair_last_mile[pair_idx - 2])
                            for qq in range(2):
                                qsem.inc(16)
                                def _ql(s, qh=qh0 + qq, sl=qslotbase + qq, qsem=qsem):
                                    s.dma_start(qT_b[:, sl * HALF:(sl + 1) * HALF],
                                                q_scrT[qh * 128:(qh + 1) * 128, :]
                                                ).then_inc(qsem.h, 16)
                                SY(_ql)
                            qt = qsem.n

                            wait("tensor", Z["kld"], kt)
                            wait("tensor", vsem, vt)
                            wait("tensor", qsem, qt)
                            for qc in range(NQC):
                                wait("tensor", Z["mA"], Z["mA"].n)
                                wait("tensor", Z["mV"], Z["mV"].n)
                                wait("tensor", Z["mA2"], Z["mA2"].n)
                                wait("scalar", Z["mT"], Z["mT"].n)
                                st_miles = {}
                                exp_miles = {}
                                out_miles = {}

                                def emit_st(kp):
                                    par = (kp % 2) * 1024
                                    if kp >= 2:
                                        wait("tensor", Z["mA2"], exp_miles[kp - 2])
                                    st_miles[kp] = Z["mT"].inc()
                                    def _st(t, kp=kp, par=par, qslotbase=qslotbase, qc=qc):
                                        for qq in range(2):
                                            ins = t.matmul(ps_st[:, par + qq * 512:par + (qq + 1) * 512],
                                                           kT_b[:, kp * 128:(kp + 1) * 128],
                                                           qT_b[:, (qslotbase + qq) * HALF + qc * 512:
                                                                (qslotbase + qq) * HALF + (qc + 1) * 512],
                                                           start=True, stop=True)
                                            if qq == 1:
                                                ins.then_inc(Z["mT"].h, 1)
                                    T(_st)
                                    wait("scalar", Z["mT"], st_miles[kp])
                                    if kp >= 2:
                                        wait("scalar", Z["mT"], out_miles[kp - 2])
                                    eslot = (kp % 2) * 1024
                                    exp_miles[kp] = Z["mA2"].inc()
                                    def _ex(a, par=par, eslot=eslot):
                                        a.activation(et_b[:, eslot:eslot + 1024],
                                                     ps_st[:, par:par + 1024], AF.Exp,
                                                     bias=fconst[:, 1:2], scale=INV_SQRT_D
                                                     ).then_inc(Z["mA2"].h, 1)
                                    A(_ex)

                                def emit_out(kp):
                                    eslot = (kp % 2) * 1024
                                    wait("tensor", Z["mA2"], exp_miles[kp])
                                    out_miles[kp] = Z["mT"].inc()
                                    def _ou(t, kp=kp, eslot=eslot, vslot=vslot):
                                        first = (kp == 0)
                                        last = (kp == 31)
                                        for qq in range(2):
                                            t.matmul(ps_o[:, qq * 512:(qq + 1) * 512],
                                                     vt_b[:, vslot * S + kp * 128:
                                                          vslot * S + (kp + 1) * 128],
                                                     et_b[:, eslot + qq * 512:eslot + (qq + 1) * 512],
                                                     start=first, stop=last)
                                        t.matmul(ps_d0[0:1, :], onescol[:, :],
                                                 et_b[:, eslot:eslot + 512],
                                                 start=first, stop=last)
                                        t.matmul(ps_d1[0:1, :], onescol[:, :],
                                                 et_b[:, eslot + 512:eslot + 1024],
                                                 start=first, stop=last).then_inc(Z["mT"].h, 1)
                                    T(_ou)

                                emit_st(0)
                                for kp in range(1, 32):
                                    emit_st(kp)
                                    emit_out(kp - 1)
                                emit_out(31)

                                for qq in range(2):
                                    qh = qh0 + qq
                                    sslot = qq
                                    so = Z["so0"] if sslot == 0 else Z["so1"]
                                    wait("scalar", Z["mT"], out_miles[31])
                                    if sev_uses[sslot] >= 1:
                                        wait("scalar", so, 16 * sev_uses[sslot])
                                    def _se(a, sslot=sslot):
                                        a.activation(sev[:, sslot * 512:(sslot + 1) * 512],
                                                     ps_o[:, sslot * 512:(sslot + 1) * 512],
                                                     AF.Copy)
                                    A(_se)
                                    mile("scalar", Z["mA"])
                                    sev_uses[sslot] += 1
                                    so.inc(16)
                                    def _sw(a, qh=qh, qc=qc, sslot=sslot, so=so):
                                        a.dma_start(s_scrT[qh * 128:(qh + 1) * 128,
                                                           qc * 512:(qc + 1) * 512],
                                                    sev[:, sslot * 512:(sslot + 1) * 512]
                                                    ).then_inc(so.h, 16)
                                    A(_sw)
                                    dsem = Z["dn0"] if qq == 0 else Z["dn1"]
                                    psd = ps_d0 if qq == 0 else ps_d1
                                    wait("vector", Z["mT"], out_miles[31])
                                    if den_uses[qq] >= 1:
                                        wait("vector", dsem, 16 * den_uses[qq])
                                    def _dn(v, qq=qq, psd=psd):
                                        v.reciprocal(denf[0:1, qq * 512:(qq + 1) * 512],
                                                     psd[0:1, :])
                                        v.drain()
                                        v.tensor_copy(denb[0:1, qq * 512:(qq + 1) * 512],
                                                      denf[0:1, qq * 512:(qq + 1) * 512])
                                    V(_dn)
                                    tdn = mile("vector", Z["mV"])
                                    wait("scalar", Z["mV"], tdn)
                                    den_uses[qq] += 1
                                    dsem.inc(16)
                                    def _dw(a, qh=qh, qc=qc, qq=qq, dsem=dsem):
                                        a.dma_start(denb_scr[qh:qh + 1, qc * 512:(qc + 1) * 512],
                                                    denb[0:1, qq * 512:(qq + 1) * 512]
                                                    ).then_inc(dsem.h, 16)
                                    A(_dw)
                            pair_last_mile.append(Z["mT"].n)
                            pair_idx += 1
                        kvh_last_mile.append(Z["mT"].n)
                    barrier()

            if "p3" in phases:
                with ExitStack() as stp:
                    E = stp.enter_context
                    w3 = E(nc.sbuf_tensor("w3", [128, 2 * DM], BF16))
                    mrep = E(nc.sbuf_tensor("mrep", [128, 2 * HALF], BF16))
                    silu_b = E(nc.sbuf_tensor("silu_b", [128, 2 * HALF], BF16))
                    gstr = E(nc.sbuf_tensor("gstr", [128, 2 * HALF], BF16))
                    gtmp = E(nc.sbuf_tensor("gtmp", [128, HALF], BF16))
                    psP3 = [E(nc.psum_tensor("psQ0", [128, 2048], F32)),
                            E(nc.psum_tensor("psQ1", [128, 2048], F32))]

                    wait("gpsimd", Z["mskw"], Z["mskw"].n)
                    wait("sync", Z["mskw"], Z["mskw"].n)

                    job_t_mile = []
                    job_a_gate = []
                    job_v_mile = []
                    gw_uses = {0: 0, 1: 0}
                    jdx = 0
                    for ot in range(NI):
                        gslot = ot % 2
                        gsem = Z["g30"] if gslot == 0 else Z["g31"]
                        for e in range(NE):
                            wslot = jdx % 2
                            wsem = Z["w3a"] if wslot == 0 else Z["w3b"]
                            if jdx >= 2:
                                wait("gpsimd", Z["mT"], job_t_mile[jdx - 2])
                            wt = wsem.inc(16)
                            def _wl(g, e=e, ot=ot, wslot=wslot, wsem=wsem):
                                g.dma_start(
                                    w3[:, wslot * DM:(wslot + 1) * DM],
                                    ew2[e, ot, :, :]
                                ).then_inc(wsem.h, 16)
                            G(_wl)
                            mslot = jdx % 2
                            msem = Z["mr0"] if mslot == 0 else Z["mr1"]
                            if jdx >= 2:
                                wait("sync", Z["mV"], job_v_mile[jdx - 2])
                            mt = msem.inc(16)
                            def _mb(s, e=e, mslot=mslot, msem=msem):
                                s.dma_start(mrep[:, mslot * HALF:(mslot + 1) * HALF],
                                            mask_scr[e:e + 1, :].broadcast_to([128, HALF])
                                            ).then_inc(msem.h, 16)
                            SY(_mb)

                            grp = jdx % 2
                            ps = psP3[grp]
                            wait("tensor", wsem, wt)
                            if jdx >= 2:
                                wait("tensor", Z["mA"], job_a_gate[jdx - 2])
                            for i in range(NI):
                                def _mm(t, i=i, wslot=wslot, ps=ps):
                                    for tc in range(4):
                                        t.matmul(ps[:, tc * 512:(tc + 1) * 512],
                                                 w3[:, wslot * DM + i * 128:
                                                     wslot * DM + (i + 1) * 128],
                                                 xnT[:, i * HALF + tc * 512:
                                                     i * HALF + (tc + 1) * 512],
                                                 start=(i == 0), stop=(i == NI - 1))
                                T(_mm)
                            tj = mile("tensor", Z["mT"])
                            job_t_mile.append(tj)

                            bias_col = eb_sb[:, e * NI + ot:e * NI + ot + 1]
                            sislot = jdx % 2
                            wait("scalar", Z["mT"], tj)
                            if jdx >= 2:
                                wait("scalar", Z["mV"], job_v_mile[jdx - 2])
                            def _si(a, sislot=sislot, bias_col=bias_col, ps=ps):
                                a.activation(silu_b[:, sislot * HALF:(sislot + 1) * HALF],
                                             ps[:, 0:2048], AF.Silu, bias=bias_col)
                            A(_si)
                            taj = mile("scalar", Z["mA"])
                            job_a_gate.append(taj)

                            wait("vector", Z["mA"], taj)
                            wait("vector", msem, mt)
                            if e == 0 and gw_uses[gslot] >= 1:
                                wait("vector", gsem, 16 * gw_uses[gslot])
                            def _ga(v, e=e, sislot=sislot, mslot=mslot, gslot=gslot):
                                if e == 0:
                                    v.tensor_mul(gstr[:, gslot * HALF:(gslot + 1) * HALF],
                                                 silu_b[:, sislot * HALF:(sislot + 1) * HALF],
                                                 mrep[:, mslot * HALF:(mslot + 1) * HALF])
                                else:
                                    v.tensor_mul(gtmp[:, :],
                                                 silu_b[:, sislot * HALF:(sislot + 1) * HALF],
                                                 mrep[:, mslot * HALF:(mslot + 1) * HALF])
                                    v.drain()
                                    v.tensor_add(gstr[:, gslot * HALF:(gslot + 1) * HALF],
                                                 gstr[:, gslot * HALF:(gslot + 1) * HALF],
                                                 gtmp[:, :])
                            V(_ga)
                            tvj = mile("vector", Z["mV"])
                            job_v_mile.append(tvj)
                            jdx += 1

                        wait("scalar", Z["mV"], job_v_mile[-1])
                        gw_uses[gslot] += 1
                        gsem.inc(16)
                        def _gw(a, ot=ot, gslot=gslot, gsem=gsem):
                            a.dma_start(g_scrT[ot * 128:(ot + 1) * 128, :],
                                        gstr[:, gslot * HALF:(gslot + 1) * HALF]
                                        ).then_inc(gsem.h, 16)
                        A(_gw)
                    barrier()

        if "p4" in phases:
            with ExitStack() as stp:
                E = stp.enter_context
                sgx = E(nc.sbuf_tensor("sgx", [128, 2 * DM], F32))
                sblk = E(nc.sbuf_tensor("sblk", [128, 2 * DM], BF16))
                gblk = E(nc.sbuf_tensor("gblk", [128, 2 * DM], BF16))
                rcp = E(nc.sbuf_tensor("rcp", [128, 2 * NH], BF16))
                c1row = E(nc.sbuf_tensor("c1row", [128, 2 * DM], BF16))
                orow = E(nc.sbuf_tensor("orow", [128, 2 * DM], F32))

                for nm in ["so0", "so1", "dn0", "dn1", "g30", "g31"]:
                    wait("sync", Z[nm], Z[nm].n)
                    wait("scalar", Z[nm], Z[nm].n)

                tile_v_mile = []
                ow_uses = {0: 0, 1: 0}
                for tt in range(NTO):
                    slot = tt % 2
                    xs = Z["x40"] if slot == 0 else Z["x41"]
                    ss = Z["sg0"] if slot == 0 else Z["sg1"]
                    if tt >= 2:
                        wait("sync", Z["mV"], tile_v_mile[tt - 2])
                        wait("scalar", Z["mV"], tile_v_mile[tt - 2])
                        wait("gpsimd", Z["mV"], tile_v_mile[tt - 2])
                    xt = xs.inc(16)
                    def _xl(g, tt=tt, slot=slot, xs=xs):
                        g.dma_start(sgx[:, slot * DM:(slot + 1) * DM],
                                    x_in[tt * 128:(tt + 1) * 128, :]).then_inc(xs.h, 16)
                    G(_xl)
                    for o in range(NI):
                        ss.inc(16)
                        def _sl(s, tt=tt, o=o, slot=slot, ss=ss):
                            s.dma_start_transpose(
                                sblk[:, slot * DM + o * 128:slot * DM + (o + 1) * 128],
                                s_scrT[o * 128:(o + 1) * 128, tt * 128:(tt + 1) * 128]
                            ).then_inc(ss.h, 16)
                        SY(_sl)
                    for o in range(NI):
                        ss.inc(16)
                        def _gl(s, tt=tt, o=o, slot=slot, ss=ss):
                            s.dma_start_transpose(
                                gblk[:, slot * DM + o * 128:slot * DM + (o + 1) * 128],
                                g_scrT[o * 128:(o + 1) * 128, tt * 128:(tt + 1) * 128]
                            ).then_inc(ss.h, 16)
                        SY(_gl)
                    ss.inc(16)
                    def _rl(s, tt=tt, slot=slot, ss=ss):
                        s.dma_start_transpose(rcp[:, slot * NH:(slot + 1) * NH],
                                              denb_scr[:, tt * 128:(tt + 1) * 128]
                                              ).then_inc(ss.h, 16)
                    SY(_rl)
                    st_total = ss.n
                    xt_total = xt

                    wait("vector", xs, xt_total)
                    wait("vector", ss, st_total)
                    if ow_uses[slot] >= 1:
                        osem = Z["ow0"] if slot == 0 else Z["ow1"]
                        wait("vector", osem, 16 * ow_uses[slot])
                    def _cb(v, slot=slot):
                        v.tensor_mul(c1row[:, slot * DM:(slot + 1) * DM],
                                     sblk[:, slot * DM:(slot + 1) * DM],
                                     gblk[:, slot * DM:(slot + 1) * DM])
                        v.drain()
                        for o in range(NI):
                            v.scalar_tensor_tensor(
                                orow[:, slot * DM + o * 128:slot * DM + (o + 1) * 128],
                                c1row[:, slot * DM + o * 128:slot * DM + (o + 1) * 128],
                                rcp[:, slot * NH + o:slot * NH + o + 1],
                                sgx[:, slot * DM + o * 128:slot * DM + (o + 1) * 128],
                                ALU.mult, ALU.add)
                        v.drain()
                    V(_cb)
                    tvt = mile("vector", Z["mV"])
                    tile_v_mile.append(tvt)
                    osem = Z["ow0"] if slot == 0 else Z["ow1"]
                    wait("sync", Z["mV"], tvt)
                    ow_uses[slot] += 1
                    osem.inc(16)
                    def _ow(s, tt=tt, slot=slot, osem=osem):
                        s.dma_start(out[tt * 128:(tt + 1) * 128, :],
                                    orow[:, slot * DM:(slot + 1) * DM]).then_inc(osem.h, 16)
                    SY(_ow)

                wait("sync", Z["ow0"], Z["ow0"].n)
                wait("sync", Z["ow1"], Z["ow1"].n)

        dma_sems = ["cons", "xld0", "xld1", "xnw0", "xnw1", "mskw", "xnt",
                    "w1a", "w1b", "wo0", "wo1", "kld", "vld0", "vld1",
                    "qld0", "qld1", "so0", "so1", "dn0", "dn1",
                    "w3a", "w3b", "mr0", "mr1", "g30", "g31",
                    "x40", "x41", "sg0", "sg1", "ow0", "ow1"]
        for nm in dma_sems:
            sem = Z[nm]
            if sem.n > 0:
                eng = "gpsimd" if nm in ("w1a", "w1b", "w3a", "w3b") else "sync"
                wait(eng, sem, sem.n)

        with nc.Block() as blk:
            @blk.tensor
            def _(eng):
                for fn in th["tensor"]:
                    fn(eng)
            @blk.vector
            def _(eng):
                for fn in th["vector"]:
                    fn(eng)
            @blk.scalar
            def _(eng):
                for fn in th["scalar"]:
                    fn(eng)
            @blk.gpsimd
            def _(eng):
                for fn in th["gpsimd"]:
                    fn(eng)
            @blk.sync
            def _(eng):
                for fn in th["sync"]:
                    fn(eng)

    return nc



def _tile_w(w):
    no = w.shape[1] // 128
    return np.ascontiguousarray(
        w.reshape(NI, 128, no, 128).transpose(2, 1, 0, 3).reshape(no, 128, DM))


def _rope_tables(pos):
    j = np.arange(64, dtype=np.float64)
    inv = 1.0 / (10000.0 ** (2.0 * j / DH))
    ang = pos[None, :].astype(np.float64) * inv[:, None]
    c = np.cos(ang)
    s = np.sin(ang)
    cos2 = np.concatenate([c, c], axis=0)
    sin2 = np.concatenate([-s, s], axis=0)
    return cos2.astype(np.float32), sin2.astype(np.float32)


def make_inputs(core, full):
    b, h = core // 2, core % 2
    bf16 = ml_dtypes.bfloat16
    x = np.asarray(full["L0_attn_x"][b])
    own = slice(h * HALF, (h + 1) * HALF)
    oth = slice((1 - h) * HALF, (2 - h) * HALF)
    x_r = np.ascontiguousarray(np.concatenate([x[own], x[oth]], axis=0), dtype=np.float32)

    pos_own = np.arange(h * HALF, (h + 1) * HALF)
    pos_oth = np.arange((1 - h) * HALF, (2 - h) * HALF)
    pos_all = np.concatenate([pos_own, pos_oth])
    cq, sq = _rope_tables(pos_own)
    ck, sk = _rope_tables(pos_all)

    ident = np.eye(128, dtype=np.float32)
    perm = np.zeros((128, 128), dtype=np.float32)
    for p in range(128):
        perm[p, (p + 64) % 128] = 1.0

    ins = {
        "x": x_r,
        "wq2": _tile_w(np.asarray(full["L0_attn_wq"], dtype=np.float32)),
        "wk2": _tile_w(np.asarray(full["L0_attn_wk"], dtype=np.float32)),
        "wv2": _tile_w(np.asarray(full["L0_attn_wv"], dtype=np.float32)),
        "bq_t": np.ascontiguousarray(np.asarray(full["L0_attn_bq"], dtype=np.float32).reshape(NH, 128).T),
        "bk_t": np.ascontiguousarray(np.asarray(full["L0_attn_bk"], dtype=np.float32).reshape(NKV, 128).T),
        "bv_t": np.ascontiguousarray(np.asarray(full["L0_attn_bv"], dtype=np.float32).reshape(NKV, 128).T),
        "gw2": np.ascontiguousarray(np.asarray(full["L0_ffn_gate_w"], dtype=np.float32)
                                    .reshape(NE, NI, 128).transpose(2, 1, 0).reshape(128, NI * NE)),
        "gb_c": np.ascontiguousarray(np.asarray(full["L0_ffn_gate_b"], dtype=np.float32).reshape(NE, 1)),
        "ew2": np.ascontiguousarray(np.asarray(full["L0_ffn_expert_w"], dtype=np.float32)
                                     .reshape(NE, NI, 128, NI, 128).transpose(0, 1, 4, 3, 2)
                                     .reshape(NE, NI, 128, DM)),
        "eb2": np.ascontiguousarray(np.asarray(full["L0_ffn_expert_b"], dtype=np.float32)
                                     .reshape(NE, NI, 128).transpose(2, 0, 1).reshape(128, NE * NI)),
        "cosq": cq.astype(bf16),
        "sinqn": sq.astype(bf16),
        "cosk": ck.astype(bf16),
        "sinkn": sk.astype(bf16),
        "identf": ident,
        "identb": ident.astype(bf16),
        "perm64": perm.astype(bf16),
        "onescol": np.ones((128, 1), dtype=bf16),
        "fconst": np.broadcast_to(np.array([LN_EPS, EXP_SHIFT], dtype=np.float32), (128, 2)).copy(),
    }
    return ins


def kernel(**inputs):
    nc = build_program()
    in_maps = [make_inputs(c, inputs) for c in range(8)]
    res = run_bass_kernel_spmd(nc, in_maps, core_ids=list(range(8)))
    outp = np.zeros((B, S, DM), dtype=np.float32)
    for c in range(8):
        b, h = c // 2, c % 2
        outp[b, h * HALF:(h + 1) * HALF, :] = res.results[c]["out"]
    return outp


# version 06
# speedup vs baseline: 1.9872x; 1.5190x over previous
import os
import sys
sys.path.insert(0, "/opt/trn_rl_repo")
import numpy as np
import ml_dtypes
from contextlib import ExitStack

import concourse.bass as bass
import concourse.mybir as mybir
from concourse.bass_utils import run_bass_kernel_spmd
from concourse.bass import IndirectOffsetOnAxis

F32 = mybir.dt.float32
BF16 = mybir.dt.bfloat16
AF = mybir.ActivationFunctionType
ALU = mybir.AluOpType
AX = mybir.AxisListType

B, S, DM = 4, 4096, 4096
NH, NKV, DH = 32, 8, 128
NE = 8
HALF = S // 2
NI = DM // 128
NTO = HALF // 128
NTF = S // 128
NQC = HALF // 512
KVC = NKV * DH
INV_SQRT_D = float(1.0 / np.sqrt(DH))
EXP_SHIFT = -20.0
LN_EPS = 1e-5
SPARSE_MOE = True
CAP = 512
SLOTS = NE * CAP
NSL = SLOTS // 128

ENGS = ("tensor", "vector", "scalar", "gpsimd", "sync")


class Sem:
    __slots__ = ("h", "n")
    def __init__(self, h):
        self.h = h
        self.n = 0
    def inc(self, k=1):
        self.n += k
        return self.n


def build_program(phases=("p0", "p1", "p2", "p3", "p4")):
    nc = bass.Bass("TRN2", target_bir_lowering=False)
    DT = nc.dram_tensor

    x_in = DT("x", [S, DM], F32, kind="ExternalInput").ap()
    wq = DT("wq2", [NH, 128, DM], F32, kind="ExternalInput").ap()
    wk = DT("wk2", [NKV, 128, DM], F32, kind="ExternalInput").ap()
    wv = DT("wv2", [NKV, 128, DM], F32, kind="ExternalInput").ap()
    bq_t = DT("bq_t", [128, NH], F32, kind="ExternalInput").ap()
    bk_t = DT("bk_t", [128, NKV], F32, kind="ExternalInput").ap()
    bv_t = DT("bv_t", [128, NKV], F32, kind="ExternalInput").ap()
    gw2 = DT("gw2", [128, NI * NE], F32, kind="ExternalInput").ap()
    gb_c = DT("gb_c", [NE, 1], F32, kind="ExternalInput").ap()
    ew2 = DT("ew2", [NE, NI, 128, DM], F32, kind="ExternalInput").ap()
    eb2 = DT("eb2", [128, NE * NI], F32, kind="ExternalInput").ap()
    cosq = DT("cosq", [128, HALF], BF16, kind="ExternalInput").ap()
    sinqn = DT("sinqn", [128, HALF], BF16, kind="ExternalInput").ap()
    cosk = DT("cosk", [128, S], BF16, kind="ExternalInput").ap()
    sinkn = DT("sinkn", [128, S], BF16, kind="ExternalInput").ap()
    identf_in = DT("identf", [128, 128], F32, kind="ExternalInput").ap()
    identb_in = DT("identb", [128, 128], BF16, kind="ExternalInput").ap()
    perm64_in = DT("perm64", [128, 128], BF16, kind="ExternalInput").ap()
    onescol_in = DT("onescol", [128, 1], BF16, kind="ExternalInput").ap()
    fconst_in = DT("fconst", [128, 2], F32, kind="ExternalInput").ap()
    ecol_in = DT("ecol", [NE, 1], F32, kind="ExternalInput").ap()

    out = DT("out", [HALF, DM], F32, kind="ExternalOutput").ap()

    xn_scr = DT("xn_scr", [S, DM], BF16, kind="ExternalOutput").ap()
    q_scrT = DT("q_scrT", [DM, HALF], BF16, kind="ExternalOutput").ap()
    k_scrT = DT("k_scrT", [KVC, S], BF16, kind="ExternalOutput").ap()
    v_scrT = DT("v_scrT", [KVC, S], BF16, kind="ExternalOutput").ap()
    mask_scr = DT("mask_scr", [NE, HALF], BF16, kind="ExternalOutput").ap()
    denb_scr = DT("denb_scr", [NH, HALF], BF16, kind="ExternalOutput").ap()
    s_scrT = DT("s_scrT", [DM, HALF], BF16, kind="ExternalOutput").ap()
    if SPARSE_MOE:
        g_slots = DT("g_slots", [SLOTS, DM], BF16, kind="ExternalOutput").ap()
        slot_scr = DT("slot_scr", [1, HALF], mybir.dt.int32, kind="ExternalOutput").ap()
        xg_scr = DT("xg_scr", [SLOTS, DM], BF16, kind="ExternalOutput").ap()
        g_scrT = None
    else:
        g_scrT = DT("g_scrT", [DM, HALF], BF16, kind="ExternalOutput").ap()

    th = {e: [] for e in ENGS}

    def T(fn): th["tensor"].append(fn)
    def V(fn): th["vector"].append(fn)
    def A(fn): th["scalar"].append(fn)
    def G(fn): th["gpsimd"].append(fn)
    def SY(fn): th["sync"].append(fn)

    sems = {}

    def wait(e, sem, val):
        if val > 0:
            th[e].append(lambda eng, s=sem.h, v=val: eng.wait_ge(s, v))

    def mile(e, sem):
        tgt = sem.inc()
        def _m(eng, s=sem.h):
            eng.drain()
            eng.sem_inc(s, 1)
        th[e].append(_m)
        return tgt

    with ExitStack() as st0:
        E0 = st0.enter_context
        for name in ["ph", "cons", "xld0", "xld1", "xnw0", "xnw1", "mskw",
                     "xnt", "w1a", "w1b", "wo0", "wo1",
                     "kld", "vld0", "vld1", "qld0", "qld1", "so0", "so1", "dn0", "dn1",
                     "w3a", "w3b", "mr0", "mr1", "g30", "g31",
                     "x40", "x41", "sg0", "sg1", "ow0", "ow1",
                     "rtw", "sxl", "tpre", "sct", "tld", "zg", "gth", "xgw", "xg0", "xg1",
                     "mT", "mV", "mA", "mA2", "mG"]:
            sems[name] = Sem(E0(nc.semaphore(name)))
        Z = sems

        def barrier():
            tgt = Z["ph"].inc(5)
            for e in ENGS:
                def _f(eng, t=tgt):
                    eng.drain()
                    eng.sem_inc(Z["ph"].h, 1)
                    eng.wait_ge(Z["ph"].h, t)
                th[e].append(_f)

        identf = E0(nc.sbuf_tensor("identf_sb", [128, 128], F32))
        identb = E0(nc.sbuf_tensor("identb_sb", [128, 128], BF16))
        perm64 = E0(nc.sbuf_tensor("perm64_sb", [128, 128], BF16))
        onescol = E0(nc.sbuf_tensor("onescol_sb", [128, 1], BF16))
        bq_sb = E0(nc.sbuf_tensor("bq_sb", [128, NH], F32))
        bk_sb = E0(nc.sbuf_tensor("bk_sb", [128, NKV], F32))
        bv_sb = E0(nc.sbuf_tensor("bv_sb", [128, NKV], F32))
        eb_sb = E0(nc.sbuf_tensor("eb_sb", [128, NE * NI], F32))
        gb_sb = E0(nc.sbuf_tensor("gb_sb", [NE, 1], F32))
        gwT_sb = E0(nc.sbuf_tensor("gwT_sb", [128, NI * NE], F32))
        maskT_sb = E0(nc.sbuf_tensor("maskT_sb", [NE, HALF], BF16))
        sidx = E0(nc.sbuf_tensor("sidx_sb", [128, NTO], mybir.dt.int32))
        fconst = E0(nc.sbuf_tensor("fconst_sb", [128, 2], F32))

        def consts_fn(s):
            pairs = [
                (identf[:, :], identf_in), (identb[:, :], identb_in),
                (perm64[:, :], perm64_in), (onescol[:, :], onescol_in),
                (bq_sb[:, :], bq_t), (bk_sb[:, :], bk_t), (bv_sb[:, :], bv_t),
                (gb_sb[:, :], gb_c), (fconst[:, :], fconst_in),
                (gwT_sb[:, :], gw2), (eb_sb[:, :], eb2),
            ]
            for o_, i_ in pairs:
                s.dma_start(o_, i_).then_inc(Z["cons"].h, 16)
            assert len(pairs) == 11
        SY(consts_fn)
        Z["cons"].n = 16 * 11

        if "p0" in phases:
            with ExitStack() as stp:
                E = stp.enter_context
                xrow = E(nc.sbuf_tensor("xrow", [128, 2 * DM], F32))
                xnf = E(nc.sbuf_tensor("xnf", [128, DM], F32))
                xnb = E(nc.sbuf_tensor("xnb", [128, 2 * DM], BF16))
                xnTf = E(nc.sbuf_tensor("xnTf", [128, DM], F32))
                stats = E(nc.sbuf_tensor("stats", [128, 16], F32))
                Lbuf = E(nc.sbuf_tensor("Lbuf", [128, 128], F32))
                logt = E(nc.sbuf_tensor("logt", [128, 16], F32))
                maskb = E(nc.sbuf_tensor("maskb", [128, 128], BF16))
                psT = E(nc.psum_tensor("psT0", [128, 1024], F32))
                psG = E(nc.psum_tensor("psG0", [128, 512], F32))
                psL = E(nc.psum_tensor("psL0", [128, 512], F32))
                psMk = E(nc.psum_tensor("psM0", [128, 1024], BF16))

                wait("vector", Z["cons"], Z["cons"].n)
                def _ms(v):
                    v.memset(Lbuf[:, :], 0.0)
                    v.memset(maskb[:, :], 0.0)
                V(_ms)
                mile("vector", Z["mV"])

                tile_a2 = []
                xnf_free_T = [0]
                _ntf = int(os.environ.get("K_NTF", NTF))
                _gate = os.environ.get("K_GATE", "1") == "1"
                _gstep = int(os.environ.get("K_GSTEP", "9"))
                for it in range(_ntf):
                    own = (it < NTO) and _gate
                    slot = it % 2
                    sx = Z["xld0"] if slot == 0 else Z["xld1"]
                    if it >= 2:
                        wait("sync", Z["mA"], tile_a2[it - 2])
                    xt = sx.inc(16)
                    def _ld(s, it=it, slot=slot, sx=sx):
                        s.dma_start(xrow[:, slot * DM:(slot + 1) * DM],
                                    x_in[it * 128:(it + 1) * 128, :]).then_inc(sx.h, 16)
                    SY(_ld)

                    xv = xrow[:, slot * DM:(slot + 1) * DM]
                    wait("vector", sx, xt)
                    wait("vector", Z["mT"], xnf_free_T[0])
                    def _st(v, xv=xv):
                        v.tensor_scalar(xnf[:, :], xv, 1.0, 0.0, ALU.mult, ALU.add, accum_out=stats[:, 0:1])
                        v.drain()
                        v.scalar_tensor_tensor(xnf[:, :], xv, 1.0, xv, ALU.mult, ALU.mult,
                                               accum_out=stats[:, 1:2])
                        v.drain()
                        v.tensor_scalar(stats[:, 2:3], stats[:, 0:1], 1.0 / DM, None, ALU.mult)
                        v.tensor_scalar(stats[:, 3:4], stats[:, 1:2], 1.0 / DM, None, ALU.mult)
                        v.drain()
                        v.scalar_tensor_tensor(stats[:, 4:5], stats[:, 2:3], stats[:, 2:3],
                                               stats[:, 3:4], ALU.mult, ALU.subtract)
                    V(_st)
                    tv = mile("vector", Z["mV"])
                    wait("scalar", Z["mV"], tv)
                    def _rs(a):
                        a.activation(stats[:, 5:6], stats[:, 4:5], AF.Ln, bias=fconst[:, 0:1], scale=-1.0)
                        a.drain()
                        a.activation(stats[:, 6:7], stats[:, 5:6], AF.Exp, scale=-0.5)
                    A(_rs)
                    ta = mile("scalar", Z["mA"])
                    wait("vector", Z["mA"], ta)
                    def _ng(v):
                        v.tensor_scalar(stats[:, 7:8], stats[:, 2:3], stats[:, 6:7], -1.0,
                                        ALU.mult, ALU.mult)
                    V(_ng)
                    tv2 = mile("vector", Z["mV"])
                    wait("scalar", Z["mV"], tv2)
                    wsem = Z["xnw0"] if slot == 0 else Z["xnw1"]
                    if it >= 2:
                        wait("scalar", wsem, 16 * (it // 2))
                    def _xn(a, xv=xv, slot=slot, own=own):
                        a.activation(xnb[:, slot * DM:(slot + 1) * DM], xv, AF.Identity,
                                     bias=stats[:, 7:8], scale=stats[:, 6:7])
                        if own:
                            a.drain()
                            a.activation(xnf[:, :], xv, AF.Identity, bias=stats[:, 7:8],
                                         scale=stats[:, 6:7])
                    A(_xn)
                    ta2 = mile("scalar", Z["mA"])
                    tile_a2.append(ta2)
                    wsem.inc(16)
                    def _wr(a, it=it, slot=slot, wsem=wsem):
                        a.dma_start(xn_scr[it * 128:(it + 1) * 128, :],
                                    xnb[:, slot * DM:(slot + 1) * DM]).then_inc(wsem.h, 16)
                    A(_wr)

                    if own:
                        wait("tensor", Z["mA"], ta2)
                        ev_miles = []
                        if _gstep < 1:
                            continue
                        for i in range(NI):
                            pb = (i % 2) * 512
                            if i >= 2:
                                wait("tensor", Z["mV"], ev_miles[i - 2])
                            def _tr(t, i=i, pb=pb):
                                t.transpose(psT[:, pb:pb + 128],
                                            xnf[:, i * 128:(i + 1) * 128], identf[:, :])
                            T(_tr)
                            tt = mile("tensor", Z["mT"])
                            wait("vector", Z["mT"], tt)
                            def _ev(v, i=i, pb=pb):
                                v.tensor_copy(xnTf[:, i * 128:(i + 1) * 128], psT[:, pb:pb + 128])
                            V(_ev)
                            ev_miles.append(mile("vector", Z["mV"]))
                        xnf_free_T[0] = Z["mT"].n
                        if _gstep < 2:
                            continue
                        wait("tensor", Z["mV"], ev_miles[-1])
                        for i in range(NI):
                            def _mm(t, i=i):
                                t.matmul(psG[0:NE, 0:128], gwT_sb[:, i * NE:(i + 1) * NE],
                                         xnTf[:, i * 128:(i + 1) * 128],
                                         start=(i == 0), stop=(i == NI - 1))
                            T(_mm)
                        tt2 = mile("tensor", Z["mT"])
                        if _gstep < 3:
                            continue
                        wait("scalar", Z["mT"], tt2)
                        def _lg(a):
                            a.activation(Lbuf[0:NE, 0:128], psG[0:NE, 0:128], AF.Identity,
                                         bias=gb_sb[0:NE, 0:1])
                        A(_lg)
                        ta3 = mile("scalar", Z["mA"])
                        if _gstep < 4:
                            continue
                        wait("tensor", Z["mA"], ta3)
                        def _trL(t):
                            t.transpose(psL[:, 0:128], Lbuf[:, :], identf[:, :])
                        T(_trL)
                        tt3 = mile("tensor", Z["mT"])
                        if _gstep < 5:
                            continue
                        wait("vector", Z["mT"], tt3)
                        def _mk(v):
                            v.tensor_copy(logt[:, 0:NE], psL[:, 0:NE])
                            v.drain()
                            v.reduce_max(logt[:, 8:9], logt[:, 0:NE], AX.X)
                            v.drain()
                            v.tensor_scalar(maskb[:, 0:NE], logt[:, 0:NE], logt[:, 8:9],
                                            None, ALU.is_ge)
                            v.drain()
                            v.tensor_copy(logt[:, 9:10], maskb[:, 0:1])
                            for j in range(1, NE):
                                v.drain()
                                v.scalar_tensor_tensor(logt[:, 10:11], logt[:, 9:10], 1.0,
                                                       maskb[:, j:j + 1], ALU.mult, ALU.mult)
                                v.drain()
                                v.tensor_sub(maskb[:, j:j + 1], maskb[:, j:j + 1], logt[:, 10:11])
                                v.drain()
                                v.scalar_tensor_tensor(logt[:, 9:10], logt[:, 9:10], 1.0,
                                                       maskb[:, j:j + 1], ALU.mult, ALU.max)
                        V(_mk)
                        tv5 = mile("vector", Z["mV"])
                        if _gstep < 6:
                            continue
                        wait("tensor", Z["mV"], tv5)
                        def _trM(t):
                            t.transpose(psMk[:, 0:128], maskb[:, :], identb[:, :])
                        T(_trM)
                        tt4 = mile("tensor", Z["mT"])
                        wait("vector", Z["mT"], tt4)
                        def _mke(v, it=it):
                            v.tensor_copy(maskT_sb[0:NE, it * 128:(it + 1) * 128], psMk[0:NE, 0:128])
                        V(_mke)
                        mile("vector", Z["mV"])

                wait("scalar", Z["mV"], Z["mV"].n)
                Z["mskw"].inc(16)
                def _wm(a):
                    a.dma_start(mask_scr[:, :], maskT_sb[:, :]).then_inc(Z["mskw"].h, 16)
                A(_wm)
                barrier()

        with ExitStack() as st13:
            E13 = st13.enter_context
            xnT = E13(nc.sbuf_tensor("xnT", [128, NI * HALF], BF16))

            if "p1" in phases:
                with ExitStack() as stp:
                    E = stp.enter_context
                    cosk_sb = E(nc.sbuf_tensor("cosk_sb", [128, S], BF16))
                    sink_sb = E(nc.sbuf_tensor("sink_sb", [128, S], BF16))
                    cosq_sb = E(nc.sbuf_tensor("cosq_sb", [128, HALF], BF16))
                    sinq_sb = E(nc.sbuf_tensor("sinq_sb", [128, HALF], BF16))
                    wstat = E(nc.sbuf_tensor("wstat", [128, 2 * DM], BF16))
                    evq = E(nc.sbuf_tensor("evq", [128, 2 * 512], BF16))
                    rt1 = E(nc.sbuf_tensor("rt1", [128, 512], BF16))
                    rt2 = E(nc.sbuf_tensor("rt2", [128, 512], BF16))
                    rfin = E(nc.sbuf_tensor("rfin", [128, 2 * 512], BF16))
                    psP = [E(nc.psum_tensor("psP0", [128, 2048], F32)),
                           E(nc.psum_tensor("psP1", [128, 2048], F32))]

                    def _ct(s):
                        for o_, i_ in [(cosk_sb, cosk), (sink_sb, sinkn),
                                       (cosq_sb, cosq), (sinq_sb, sinqn)]:
                            s.dma_start(o_[:, :], i_).then_inc(Z["cons"].h, 16)
                    SY(_ct)
                    Z["cons"].inc(16 * 4)
                    wait("sync", Z["cons"], Z["cons"].n)
                    wait("sync", Z["xnw0"], Z["xnw0"].n)
                    wait("sync", Z["xnw1"], Z["xnw1"].n)

                    job_tensor_mile = []
                    job_drain_mile = []
                    slot_writes = {0: 0, 1: 0}
                    slot_last_vm = {0: 0, 1: 0}
                    job_idx = 0
                    for phase_i, half_other in enumerate([True, False]):
                        roff = 2048 if half_other else 0
                        if phase_i == 1:
                            wait("sync", Z["mT"], job_tensor_mile[-1])
                        for i in range(NI):
                            Z["xnt"].inc(16)
                            def _xl(s, i=i, roff=roff):
                                s.dma_start_transpose(
                                    xnT[:, i * HALF:(i + 1) * HALF],
                                    xn_scr[roff:roff + HALF, i * 128:(i + 1) * 128]
                                ).then_inc(Z["xnt"].h, 16)
                            SY(_xl)
                        xnt_total = Z["xnt"].n

                        jobs = [("k", ot) for ot in range(NKV)] + [("v", ot) for ot in range(NKV)]
                        if not half_other:
                            jobs += [("q", ot) for ot in range(NH)]

                        for kind, ot in jobs:
                            wslot = job_idx % 2
                            wsem = Z["w1a"] if wslot == 0 else Z["w1b"]
                            if job_idx >= 2:
                                wait("gpsimd", Z["mT"], job_tensor_mile[job_idx - 2])
                            wt = wsem.inc(16)
                            wmat = {"k": wk, "v": wv, "q": wq}[kind]
                            def _wl(g, wmat=wmat, ot=ot, wslot=wslot, wsem=wsem):
                                g.dma_start(
                                    wstat[:, wslot * DM:(wslot + 1) * DM],
                                    wmat[ot, :, :]
                                ).then_inc(wsem.h, 16)
                            G(_wl)

                            grp = job_idx % 2
                            ps = psP[grp]
                            wait("tensor", wsem, wt)
                            wait("tensor", Z["xnt"], xnt_total)
                            if job_idx >= 2:
                                am, vm = job_drain_mile[job_idx - 2]
                                wait("tensor", Z["mA"], am)
                                if vm:
                                    wait("tensor", Z["mV"], vm)
                            for i in range(NI):
                                def _mm(t, i=i, wslot=wslot, ps=ps):
                                    for tc in range(4):
                                        t.matmul(ps[:, tc * 512:(tc + 1) * 512],
                                                 wstat[:, wslot * DM + i * 128:
                                                       wslot * DM + (i + 1) * 128],
                                                 xnT[:, i * HALF + tc * 512:
                                                     i * HALF + (tc + 1) * 512],
                                                 start=(i == 0), stop=(i == NI - 1))
                                T(_mm)
                            tjob = mile("tensor", Z["mT"])

                            bias_col = {"k": bk_sb, "v": bv_sb, "q": bq_sb}[kind][:, ot:ot + 1]
                            rope = kind in ("k", "q")
                            if kind == "q":
                                cos_t, sin_t, dst, dcol0 = cosq_sb, sinq_sb, q_scrT, 0
                            elif kind == "k":
                                cos_t, sin_t, dst, dcol0 = cosk_sb, sink_sb, k_scrT, roff
                            else:
                                cos_t, sin_t, dst, dcol0 = None, None, v_scrT, roff

                            last_am = 0
                            last_vm = 0
                            for tc in range(4):
                                eslot = tc % 2
                                osem = Z["wo0"] if eslot == 0 else Z["wo1"]
                                wait("scalar", Z["mT"], tjob)
                                if slot_writes[eslot] >= 1:
                                    wait("scalar", osem, 16 * slot_writes[eslot])
                                if slot_last_vm[eslot] > 0:
                                    wait("scalar", Z["mV"], slot_last_vm[eslot])
                                def _ev(a, tc=tc, eslot=eslot, bias_col=bias_col, ps=ps):
                                    a.activation(evq[:, eslot * 512:(eslot + 1) * 512],
                                                 ps[:, tc * 512:(tc + 1) * 512], AF.Identity,
                                                 bias=bias_col)
                                A(_ev)
                                taev = mile("scalar", Z["mA"])
                                last_am = taev
                                if rope:
                                    wait("tensor", Z["mA"], taev)
                                    def _pm(t, tc=tc, eslot=eslot, ps=ps):
                                        t.matmul(ps[:, tc * 512:(tc + 1) * 512], perm64[:, :],
                                                 evq[:, eslot * 512:(eslot + 1) * 512],
                                                 start=True, stop=True)
                                    T(_pm)
                                    ttp = mile("tensor", Z["mT"])
                                    wait("vector", Z["mT"], ttp)
                                    if slot_writes[eslot] >= 1:
                                        wait("vector", osem, 16 * slot_writes[eslot])
                                    ccol = (tc * 512) if kind == "q" else (roff + tc * 512)
                                    def _rp(v, tc=tc, eslot=eslot, ccol=ccol,
                                            cos_t=cos_t, sin_t=sin_t, ps=ps):
                                        v.tensor_mul(rt1[:, :],
                                                     evq[:, eslot * 512:(eslot + 1) * 512],
                                                     cos_t[:, ccol:ccol + 512])
                                        v.tensor_mul(rt2[:, :],
                                                     ps[:, tc * 512:(tc + 1) * 512],
                                                     sin_t[:, ccol:ccol + 512])
                                        v.drain()
                                        v.tensor_add(rfin[:, eslot * 512:(eslot + 1) * 512],
                                                     rt1[:, :], rt2[:, :])
                                    V(_rp)
                                    tvr = mile("vector", Z["mV"])
                                    last_vm = tvr
                                    slot_last_vm[eslot] = tvr
                                    wait("sync", Z["mV"], tvr)
                                    src_buf = rfin
                                else:
                                    wait("sync", Z["mA"], taev)
                                    src_buf = evq
                                slot_writes[eslot] += 1
                                osem.inc(16)
                                def _wo(s, tc=tc, eslot=eslot, dst=dst, ot=ot,
                                        dcol0=dcol0, src_buf=src_buf, osem=osem):
                                    s.dma_start(dst[ot * 128:(ot + 1) * 128,
                                                    dcol0 + tc * 512:dcol0 + (tc + 1) * 512],
                                                src_buf[:, eslot * 512:(eslot + 1) * 512]
                                                ).then_inc(osem.h, 16)
                                SY(_wo)
                            job_tensor_mile.append(Z["mT"].n)
                            job_drain_mile.append((last_am, last_vm))
                            job_idx += 1
                    barrier()

            if "p2" in phases:
                with ExitStack() as stp:
                    E = stp.enter_context
                    kT_b = E(nc.sbuf_tensor("kT_b", [128, S], BF16))
                    vt_b = E(nc.sbuf_tensor("vt_b", [128, 2 * S], BF16))
                    qT_b = E(nc.sbuf_tensor("qT_b", [128, 4 * HALF], BF16))
                    et_b = E(nc.sbuf_tensor("et_b", [128, 2 * 1024], BF16))
                    sev = E(nc.sbuf_tensor("sev", [128, 2 * 512], BF16))
                    denf = E(nc.sbuf_tensor("denf", [1, 2 * 512], F32))
                    denb = E(nc.sbuf_tensor("denb", [1, 2 * 512], BF16))
                    ps_st = E(nc.psum_tensor("ps_st", [128, 2048], F32))
                    ps_o = E(nc.psum_tensor("ps_o", [128, 1024], F32))
                    ps_d0 = E(nc.psum_tensor("ps_d0", [128, 512], F32))
                    ps_d1 = E(nc.psum_tensor("ps_d1", [128, 512], F32))

                    wait("sync", Z["wo0"], Z["wo0"].n)
                    wait("sync", Z["wo1"], Z["wo1"].n)

                    kvh_last_mile = []
                    pair_last_mile = []
                    sev_uses = {0: 0, 1: 0}
                    den_uses = {0: 0, 1: 0}
                    pair_idx = 0
                    for kvh in range(NKV):
                        if kvh >= 1:
                            wait("sync", Z["mT"], kvh_last_mile[-1])
                        kt = Z["kld"].inc(16)
                        def _kl(s, kvh=kvh):
                            s.dma_start(kT_b[:, :],
                                        k_scrT[kvh * 128:(kvh + 1) * 128, :]).then_inc(Z["kld"].h, 16)
                        SY(_kl)
                        vslot = kvh % 2
                        vsem = Z["vld0"] if vslot == 0 else Z["vld1"]
                        if kvh >= 2:
                            wait("sync", Z["mT"], kvh_last_mile[-2])
                        for kp in range(32):
                            vsem.inc(16)
                            def _vl(s, kvh=kvh, vslot=vslot, kp=kp, vsem=vsem):
                                s.dma_start_transpose(
                                    vt_b[:, vslot * S + kp * 128:vslot * S + (kp + 1) * 128],
                                    v_scrT[kvh * 128:(kvh + 1) * 128, kp * 128:(kp + 1) * 128]
                                ).then_inc(vsem.h, 16)
                            SY(_vl)
                        vt = vsem.n

                        for pr in range(2):
                            qh0 = kvh * 4 + pr * 2
                            qslotbase = (pair_idx % 2) * 2
                            qsem = Z["qld0"] if (pair_idx % 2) == 0 else Z["qld1"]
                            if pair_idx >= 2:
                                wait("sync", Z["mT"], pair_last_mile[pair_idx - 2])
                            for qq in range(2):
                                qsem.inc(16)
                                def _ql(s, qh=qh0 + qq, sl=qslotbase + qq, qsem=qsem):
                                    s.dma_start(qT_b[:, sl * HALF:(sl + 1) * HALF],
                                                q_scrT[qh * 128:(qh + 1) * 128, :]
                                                ).then_inc(qsem.h, 16)
                                SY(_ql)
                            qt = qsem.n

                            wait("tensor", Z["kld"], kt)
                            wait("tensor", vsem, vt)
                            wait("tensor", qsem, qt)
                            for qc in range(NQC):
                                wait("tensor", Z["mA"], Z["mA"].n)
                                wait("tensor", Z["mV"], Z["mV"].n)
                                wait("tensor", Z["mA2"], Z["mA2"].n)
                                wait("scalar", Z["mT"], Z["mT"].n)
                                st_miles = {}
                                exp_miles = {}
                                out_miles = {}

                                def emit_st(kp):
                                    par = (kp % 2) * 1024
                                    if kp >= 2:
                                        wait("tensor", Z["mA2"], exp_miles[kp - 2])
                                    st_miles[kp] = Z["mT"].inc()
                                    def _st(t, kp=kp, par=par, qslotbase=qslotbase, qc=qc):
                                        for qq in range(2):
                                            ins = t.matmul(ps_st[:, par + qq * 512:par + (qq + 1) * 512],
                                                           kT_b[:, kp * 128:(kp + 1) * 128],
                                                           qT_b[:, (qslotbase + qq) * HALF + qc * 512:
                                                                (qslotbase + qq) * HALF + (qc + 1) * 512],
                                                           start=True, stop=True)
                                            if qq == 1:
                                                ins.then_inc(Z["mT"].h, 1)
                                    T(_st)
                                    wait("scalar", Z["mT"], st_miles[kp])
                                    if kp >= 2:
                                        wait("scalar", Z["mT"], out_miles[kp - 2])
                                    eslot = (kp % 2) * 1024
                                    exp_miles[kp] = Z["mA2"].inc()
                                    def _ex(a, par=par, eslot=eslot):
                                        a.activation(et_b[:, eslot:eslot + 1024],
                                                     ps_st[:, par:par + 1024], AF.Exp,
                                                     bias=fconst[:, 1:2], scale=INV_SQRT_D
                                                     ).then_inc(Z["mA2"].h, 1)
                                    A(_ex)

                                def emit_out(kp):
                                    eslot = (kp % 2) * 1024
                                    wait("tensor", Z["mA2"], exp_miles[kp])
                                    out_miles[kp] = Z["mT"].inc()
                                    def _ou(t, kp=kp, eslot=eslot, vslot=vslot):
                                        first = (kp == 0)
                                        last = (kp == 31)
                                        for qq in range(2):
                                            t.matmul(ps_o[:, qq * 512:(qq + 1) * 512],
                                                     vt_b[:, vslot * S + kp * 128:
                                                          vslot * S + (kp + 1) * 128],
                                                     et_b[:, eslot + qq * 512:eslot + (qq + 1) * 512],
                                                     start=first, stop=last)
                                        t.matmul(ps_d0[0:1, :], onescol[:, :],
                                                 et_b[:, eslot:eslot + 512],
                                                 start=first, stop=last)
                                        t.matmul(ps_d1[0:1, :], onescol[:, :],
                                                 et_b[:, eslot + 512:eslot + 1024],
                                                 start=first, stop=last).then_inc(Z["mT"].h, 1)
                                    T(_ou)

                                emit_st(0)
                                for kp in range(1, 32):
                                    emit_st(kp)
                                    emit_out(kp - 1)
                                emit_out(31)

                                for qq in range(2):
                                    qh = qh0 + qq
                                    sslot = qq
                                    so = Z["so0"] if sslot == 0 else Z["so1"]
                                    wait("scalar", Z["mT"], out_miles[31])
                                    if sev_uses[sslot] >= 1:
                                        wait("scalar", so, 16 * sev_uses[sslot])
                                    def _se(a, sslot=sslot):
                                        a.activation(sev[:, sslot * 512:(sslot + 1) * 512],
                                                     ps_o[:, sslot * 512:(sslot + 1) * 512],
                                                     AF.Copy)
                                    A(_se)
                                    mile("scalar", Z["mA"])
                                    sev_uses[sslot] += 1
                                    so.inc(16)
                                    def _sw(a, qh=qh, qc=qc, sslot=sslot, so=so):
                                        a.dma_start(s_scrT[qh * 128:(qh + 1) * 128,
                                                           qc * 512:(qc + 1) * 512],
                                                    sev[:, sslot * 512:(sslot + 1) * 512]
                                                    ).then_inc(so.h, 16)
                                    A(_sw)
                                    dsem = Z["dn0"] if qq == 0 else Z["dn1"]
                                    psd = ps_d0 if qq == 0 else ps_d1
                                    wait("vector", Z["mT"], out_miles[31])
                                    if den_uses[qq] >= 1:
                                        wait("vector", dsem, 16 * den_uses[qq])
                                    def _dn(v, qq=qq, psd=psd):
                                        v.reciprocal(denf[0:1, qq * 512:(qq + 1) * 512],
                                                     psd[0:1, :])
                                        v.drain()
                                        v.tensor_copy(denb[0:1, qq * 512:(qq + 1) * 512],
                                                      denf[0:1, qq * 512:(qq + 1) * 512])
                                    V(_dn)
                                    tdn = mile("vector", Z["mV"])
                                    wait("scalar", Z["mV"], tdn)
                                    den_uses[qq] += 1
                                    dsem.inc(16)
                                    def _dw(a, qh=qh, qc=qc, qq=qq, dsem=dsem):
                                        a.dma_start(denb_scr[qh:qh + 1, qc * 512:(qc + 1) * 512],
                                                    denb[0:1, qq * 512:(qq + 1) * 512]
                                                    ).then_inc(dsem.h, 16)
                                    A(_dw)
                            pair_last_mile.append(Z["mT"].n)
                            pair_idx += 1
                        kvh_last_mile.append(Z["mT"].n)
                    barrier()

            if "p3" in phases and not SPARSE_MOE:
                with ExitStack() as stp:
                    E = stp.enter_context
                    w3 = E(nc.sbuf_tensor("w3", [128, 2 * DM], BF16))
                    mrep = E(nc.sbuf_tensor("mrep", [128, 2 * HALF], BF16))
                    silu_b = E(nc.sbuf_tensor("silu_b", [128, 2 * HALF], BF16))
                    gstr = E(nc.sbuf_tensor("gstr", [128, 2 * HALF], BF16))
                    gtmp = E(nc.sbuf_tensor("gtmp", [128, HALF], BF16))
                    psP3 = [E(nc.psum_tensor("psQ0", [128, 2048], F32)),
                            E(nc.psum_tensor("psQ1", [128, 2048], F32))]

                    wait("gpsimd", Z["mskw"], Z["mskw"].n)
                    wait("sync", Z["mskw"], Z["mskw"].n)

                    job_t_mile = []
                    job_a_gate = []
                    job_v_mile = []
                    gw_uses = {0: 0, 1: 0}
                    jdx = 0
                    for ot in range(NI):
                        gslot = ot % 2
                        gsem = Z["g30"] if gslot == 0 else Z["g31"]
                        for e in range(NE):
                            wslot = jdx % 2
                            wsem = Z["w3a"] if wslot == 0 else Z["w3b"]
                            if jdx >= 2:
                                wait("gpsimd", Z["mT"], job_t_mile[jdx - 2])
                            wt = wsem.inc(16)
                            def _wl(g, e=e, ot=ot, wslot=wslot, wsem=wsem):
                                g.dma_start(
                                    w3[:, wslot * DM:(wslot + 1) * DM],
                                    ew2[e, ot, :, :]
                                ).then_inc(wsem.h, 16)
                            G(_wl)
                            mslot = jdx % 2
                            msem = Z["mr0"] if mslot == 0 else Z["mr1"]
                            if jdx >= 2:
                                wait("sync", Z["mV"], job_v_mile[jdx - 2])
                            mt = msem.inc(16)
                            def _mb(s, e=e, mslot=mslot, msem=msem):
                                s.dma_start(mrep[:, mslot * HALF:(mslot + 1) * HALF],
                                            mask_scr[e:e + 1, :].broadcast_to([128, HALF])
                                            ).then_inc(msem.h, 16)
                            SY(_mb)

                            grp = jdx % 2
                            ps = psP3[grp]
                            wait("tensor", wsem, wt)
                            if jdx >= 2:
                                wait("tensor", Z["mA"], job_a_gate[jdx - 2])
                            for i in range(NI):
                                def _mm(t, i=i, wslot=wslot, ps=ps):
                                    for tc in range(4):
                                        t.matmul(ps[:, tc * 512:(tc + 1) * 512],
                                                 w3[:, wslot * DM + i * 128:
                                                     wslot * DM + (i + 1) * 128],
                                                 xnT[:, i * HALF + tc * 512:
                                                     i * HALF + (tc + 1) * 512],
                                                 start=(i == 0), stop=(i == NI - 1))
                                T(_mm)
                            tj = mile("tensor", Z["mT"])
                            job_t_mile.append(tj)

                            bias_col = eb_sb[:, e * NI + ot:e * NI + ot + 1]
                            sislot = jdx % 2
                            wait("scalar", Z["mT"], tj)
                            if jdx >= 2:
                                wait("scalar", Z["mV"], job_v_mile[jdx - 2])
                            def _si(a, sislot=sislot, bias_col=bias_col, ps=ps):
                                a.activation(silu_b[:, sislot * HALF:(sislot + 1) * HALF],
                                             ps[:, 0:2048], AF.Silu, bias=bias_col)
                            A(_si)
                            taj = mile("scalar", Z["mA"])
                            job_a_gate.append(taj)

                            wait("vector", Z["mA"], taj)
                            wait("vector", msem, mt)
                            if e == 0 and gw_uses[gslot] >= 1:
                                wait("vector", gsem, 16 * gw_uses[gslot])
                            def _ga(v, e=e, sislot=sislot, mslot=mslot, gslot=gslot):
                                if e == 0:
                                    v.tensor_mul(gstr[:, gslot * HALF:(gslot + 1) * HALF],
                                                 silu_b[:, sislot * HALF:(sislot + 1) * HALF],
                                                 mrep[:, mslot * HALF:(mslot + 1) * HALF])
                                else:
                                    v.tensor_mul(gtmp[:, :],
                                                 silu_b[:, sislot * HALF:(sislot + 1) * HALF],
                                                 mrep[:, mslot * HALF:(mslot + 1) * HALF])
                                    v.drain()
                                    v.tensor_add(gstr[:, gslot * HALF:(gslot + 1) * HALF],
                                                 gstr[:, gslot * HALF:(gslot + 1) * HALF],
                                                 gtmp[:, :])
                            V(_ga)
                            tvj = mile("vector", Z["mV"])
                            job_v_mile.append(tvj)
                            jdx += 1

                        wait("scalar", Z["mV"], job_v_mile[-1])
                        gw_uses[gslot] += 1
                        gsem.inc(16)
                        def _gw(a, ot=ot, gslot=gslot, gsem=gsem):
                            a.dma_start(g_scrT[ot * 128:(ot + 1) * 128, :],
                                        gstr[:, gslot * HALF:(gslot + 1) * HALF]
                                        ).then_inc(gsem.h, 16)
                        A(_gw)
                    barrier()

        if "p3" in phases and SPARSE_MOE:
            with ExitStack() as stp:
                E = stp.enter_context
                w3 = E(nc.sbuf_tensor("w3", [128, 2 * DM], BF16))
                xgb = E(nc.sbuf_tensor("xgb", [128, 2 * NI * CAP], BF16))
                grow = E(nc.sbuf_tensor("grow", [128, 4 * DM], BF16))
                sil3 = E(nc.sbuf_tensor("sil3", [128, 2 * CAP], BF16))
                gbufA = E(nc.sbuf_tensor("gbufA", [128, DM], BF16))
                gbufB = E(nc.sbuf_tensor("gbufB", [128, DM], BF16))
                rt_ones = E(nc.sbuf_tensor("rt_ones", [NE, HALF], F32))
                rt_a = E(nc.sbuf_tensor("rt_a", [NE, HALF], F32))
                rt_b = E(nc.sbuf_tensor("rt_b", [NE, HALF], F32))
                ones8f = E(nc.sbuf_tensor("ones8f", [NE, 1], F32))
                ecol_sb = E(nc.sbuf_tensor("ecol_sb", [NE, 1], F32))
                slotrow = E(nc.sbuf_tensor("slotrow", [1, HALF], F32))
                irow = E(nc.sbuf_tensor("irow", [1, HALF], mybir.dt.int32))
                sidx = E(nc.sbuf_tensor("sidx", [128, NTO], mybir.dt.int32))
                tki = E(nc.sbuf_tensor("tki", [128, NTO], mybir.dt.int32))
                tosb = E(nc.sbuf_tensor("tosb", [128, NSL], mybir.dt.int32))
                snt = E(nc.sbuf_tensor("snt", [1, SLOTS], mybir.dt.int32))
                zrow = E(nc.sbuf_tensor("zrow", [128, DM], BF16))
                psY = E(nc.psum_tensor("psY3", [128, 1024], F32))
                psT2 = E(nc.psum_tensor("psT23", [128, 2048], BF16))

                Z["rtw"].inc(16)
                def _ec(s):
                    s.dma_start(ecol_sb[:, :], ecol_in).then_inc(Z["rtw"].h, 16)
                SY(_ec)
                wait("vector", Z["rtw"], 16)
                def _rt0(v):
                    v.memset(rt_ones[:, :], 1.0)
                    v.memset(ones8f[:, :], 1.0)
                    v.drain()
                    v.tensor_tensor_scan(rt_a[:, :], maskT_sb[:, :], rt_ones[:, :],
                                         0.0, ALU.add, ALU.mult)
                    v.drain()
                    v.tensor_sub(rt_b[:, :], rt_a[:, :], maskT_sb[:, :])
                    v.drain()
                    v.tensor_scalar(rt_a[:, :], rt_b[:, :], float(CAP - 1), None, ALU.min)
                    v.drain()
                    v.tensor_scalar(rt_b[:, :], rt_a[:, :], ecol_sb[:, 0:1], None, ALU.add)
                    v.drain()
                    v.tensor_mul(rt_a[:, :], rt_b[:, :], maskT_sb[:, :])
                V(_rt0)
                tvr0 = mile("vector", Z["mV"])
                wait("tensor", Z["mV"], tvr0)
                for qc in range(4):
                    tq = Z["mT"].inc()
                    def _r1(t, qc=qc):
                        t.matmul(psY[0:1, 0:512], ones8f[0:NE, 0:1],
                                 rt_a[0:NE, qc * 512:(qc + 1) * 512],
                                 start=True, stop=True).then_inc(Z["mT"].h, 1)
                    T(_r1)
                    wait("vector", Z["mT"], tq)
                    def _r2(v, qc=qc):
                        v.tensor_copy(slotrow[0:1, qc * 512:(qc + 1) * 512], psY[0:1, 0:512])
                        v.drain()
                    V(_r2)
                    tvq = mile("vector", Z["mV"])
                    wait("tensor", Z["mV"], tvq)
                def _cv(v):
                    v.tensor_copy(irow[:, :], slotrow[:, :])
                    v.drain()
                V(_cv)
                tvi = mile("vector", Z["mV"])
                wait("scalar", Z["mV"], tvi)
                Z["rtw"].inc(16)
                def _sw(a):
                    a.dma_start(slot_scr[:, :], irow[:, :]).then_inc(Z["rtw"].h, 16)
                A(_sw)
                wait("sync", Z["rtw"], Z["rtw"].n)
                for ch in range(NTO):
                    Z["sxl"].inc(16)
                    def _sx(s, ch=ch):
                        s.dma_start(sidx[:, ch:ch + 1],
                                    slot_scr[:, ch * 128:(ch + 1) * 128]
                                    .rearrange("r a -> (r a)").unsqueeze(1)).then_inc(Z["sxl"].h, 16)
                    SY(_sx)
                wait("gpsimd", Z["sxl"], Z["sxl"].n)
                for ch in range(NTO):
                    gslot2 = ch % 2
                    lsem = Z["xgw"] if gslot2 == 0 else Z["zg"]
                    ssem = Z["gth"] if gslot2 == 0 else Z["tld"]
                    if ch >= 2:
                        wait("sync", ssem, 16 * (ch // 2))
                    lt = lsem.inc(16)
                    def _xl0(s, ch=ch, gslot2=gslot2, lsem=lsem):
                        gb = gbufA if gslot2 == 0 else gbufB
                        s.dma_start(gb[:, :], xn_scr[ch * 128:(ch + 1) * 128, :]
                                    ).then_inc(lsem.h, 16)
                    SY(_xl0)
                    wait("gpsimd", lsem, lt)
                    ssem.inc(16)
                    def _sc(g, ch=ch, gslot2=gslot2, ssem=ssem):
                        gb = gbufA if gslot2 == 0 else gbufB
                        g.indirect_dma_start(
                            xg_scr[:, :], IndirectOffsetOnAxis(sidx[:, ch:ch + 1], 0),
                            gb[:, :], None,
                            bounds_check=SLOTS - 1, oob_is_err=False
                        ).then_inc(ssem.h, 16)
                    G(_sc)

                job_t_mile = []
                job_a_gate = []
                job_v_gate = []
                exp_t_mile = []
                exp_v_mile = []
                prev_ev_miles = []
                jdx = 0
                for e in range(NE):
                    xslot = e % 2
                    xsem = Z["xg0"] if xslot == 0 else Z["xg1"]
                    if e == 0:
                        wait("sync", Z["gth"], Z["gth"].n)
                        wait("sync", Z["tld"], Z["tld"].n)
                    if e >= 2:
                        wait("sync", Z["mT"], exp_t_mile[e - 2])
                    for i in range(NI):
                        xsem.inc(16)
                        def _xl(s, e=e, i=i, xslot=xslot, xsem=xsem):
                            s.dma_start_transpose(
                                xgb[:, (xslot * NI + i) * CAP:(xslot * NI + i + 1) * CAP],
                                xg_scr[e * CAP:(e + 1) * CAP, i * 128:(i + 1) * 128]
                            ).then_inc(xsem.h, 16)
                        SY(_xl)
                    xt_total = xsem.n

                    for ot in range(NI):
                        wslot = jdx % 2
                        wsem = Z["w3a"] if wslot == 0 else Z["w3b"]
                        if jdx >= 2:
                            wait("gpsimd", Z["mT"], job_t_mile[jdx - 2])
                        wt = wsem.inc(16)
                        def _wl(g, e=e, ot=ot, wslot=wslot, wsem=wsem):
                            g.dma_start(
                                w3[:, wslot * DM:(wslot + 1) * DM],
                                ew2[e, ot, :, :]
                            ).then_inc(wsem.h, 16)
                        G(_wl)

                        pslot = jdx % 2
                        wait("tensor", wsem, wt)
                        wait("tensor", xsem, xt_total)
                        if jdx >= 2:
                            wait("tensor", Z["mA"], job_a_gate[jdx - 2])
                        tjv = Z["mT"].inc()
                        def _mm(t, wslot=wslot, xslot=xslot, pslot=pslot):
                            for i in range(NI):
                                ins = t.matmul(psY[:, pslot * 512:(pslot + 1) * 512],
                                               w3[:, wslot * DM + i * 128:
                                                   wslot * DM + (i + 1) * 128],
                                               xgb[:, (xslot * NI + i) * CAP:
                                                   (xslot * NI + i + 1) * CAP],
                                               start=(i == 0), stop=(i == NI - 1))
                                if i == NI - 1:
                                    ins.then_inc(Z["mT"].h, 1)
                        T(_mm)
                        job_t_mile.append(tjv)

                        bias_col = eb_sb[:, e * NI + ot:e * NI + ot + 1]
                        sislot = jdx % 2
                        wait("scalar", Z["mT"], tjv)
                        if jdx >= 2:
                            wait("scalar", Z["mV"], job_v_gate[jdx - 2])
                        taj = Z["mA"].inc()
                        def _si(a, sislot=sislot, bias_col=bias_col, pslot=pslot):
                            a.activation(sil3[:, sislot * CAP:(sislot + 1) * CAP],
                                         psY[:, pslot * 512:(pslot + 1) * 512],
                                         AF.Silu, bias=bias_col).then_inc(Z["mA"].h, 1)
                        A(_si)
                        job_a_gate.append(taj)

                        wait("tensor", Z["mA"], taj)
                        ev_miles = []
                        for bb in range(4):
                            tslot = bb % 2
                            if bb >= 2:
                                wait("tensor", Z["mV"], ev_miles[bb - 2])
                            elif jdx > 0 and len(prev_ev_miles) >= 2:
                                wait("tensor", Z["mV"], prev_ev_miles[bb - 2])
                            ttr_b = Z["mT"].inc()
                            def _tr(t, sislot=sislot, bb=bb, tslot=tslot):
                                t.transpose(psT2[:, tslot * 1024:tslot * 1024 + 128],
                                            sil3[:, sislot * CAP + bb * 128:
                                                 sislot * CAP + (bb + 1) * 128],
                                            identb[:, :]).then_inc(Z["mT"].h, 1)
                            T(_tr)
                            wait("vector", Z["mT"], ttr_b)
                            def _ev(v, bb=bb, ot=ot, tslot=tslot):
                                v.tensor_copy(grow[:, bb * DM + ot * 128:bb * DM + (ot + 1) * 128],
                                              psT2[:, tslot * 1024:tslot * 1024 + 128])
                            V(_ev)
                            ev_miles.append(mile("vector", Z["mV"]))
                        prev_ev_miles = ev_miles
                        job_v_gate.append(ev_miles[-1])
                        jdx += 1
                    exp_t_mile.append(Z["mT"].n)
                    exp_v_mile.append(Z["mV"].n)

                    wait("scalar", Z["mV"], exp_v_mile[-1])
                    for st in range(4):
                        sl = e * 4 + st
                        Z["sct"].inc(16)
                        def _sg(a, st=st, sl=sl):
                            a.dma_start(g_slots[sl * 128:(sl + 1) * 128, :],
                                        grow[:, st * DM:(st + 1) * DM]
                                        ).then_inc(Z["sct"].h, 16)
                        A(_sg)
                    if e < NE - 1:
                        wait("vector", Z["sct"], Z["sct"].n)
                barrier()


        if "p4" in phases:
            with ExitStack() as stp:
                E = stp.enter_context
                sgx = E(nc.sbuf_tensor("sgx", [128, 2 * DM], F32))
                sblk = E(nc.sbuf_tensor("sblk", [128, 2 * DM], BF16))
                gblk = E(nc.sbuf_tensor("gblk", [128, 2 * DM], BF16))
                rcp = E(nc.sbuf_tensor("rcp", [128, 2 * NH], BF16))
                c1row = E(nc.sbuf_tensor("c1row", [128, 2 * DM], BF16))
                orow = E(nc.sbuf_tensor("orow", [128, 2 * DM], F32))

                p4sems = ["so0", "so1", "dn0", "dn1"]
                p4sems += (["sct"] if SPARSE_MOE else ["g30", "g31"])
                for nm in p4sems:
                    wait("sync", Z[nm], Z[nm].n)
                    wait("scalar", Z[nm], Z[nm].n)
                    wait("gpsimd", Z[nm], Z[nm].n)

                tile_v_mile = []
                ow_uses = {0: 0, 1: 0}
                for tt in range(NTO):
                    slot = tt % 2
                    xs = Z["x40"] if slot == 0 else Z["x41"]
                    ss = Z["sg0"] if slot == 0 else Z["sg1"]
                    if tt >= 2:
                        wait("sync", Z["mV"], tile_v_mile[tt - 2])
                        wait("scalar", Z["mV"], tile_v_mile[tt - 2])
                        wait("gpsimd", Z["mV"], tile_v_mile[tt - 2])
                    xt = xs.inc(16)
                    def _xl(g, tt=tt, slot=slot, xs=xs):
                        g.dma_start(sgx[:, slot * DM:(slot + 1) * DM],
                                    x_in[tt * 128:(tt + 1) * 128, :]).then_inc(xs.h, 16)
                    G(_xl)
                    for o in range(NI):
                        ss.inc(16)
                        def _sl(s, tt=tt, o=o, slot=slot, ss=ss):
                            s.dma_start_transpose(
                                sblk[:, slot * DM + o * 128:slot * DM + (o + 1) * 128],
                                s_scrT[o * 128:(o + 1) * 128, tt * 128:(tt + 1) * 128]
                            ).then_inc(ss.h, 16)
                        SY(_sl)
                    if SPARSE_MOE:
                        gsem = Z["mr0"] if slot == 0 else Z["mr1"]
                        gt4 = gsem.inc(16)
                        def _gl(g, tt=tt, slot=slot, gsem=gsem):
                            g.indirect_dma_start(
                                gblk[:, slot * DM:(slot + 1) * DM], None,
                                g_slots[:, :],
                                IndirectOffsetOnAxis(sidx[:, tt:tt + 1], 0),
                                bounds_check=SLOTS - 1, oob_is_err=False
                            ).then_inc(gsem.h, 16)
                        G(_gl)
                        wait("vector", gsem, gt4)
                    else:
                        for o in range(NI):
                            ss.inc(16)
                            def _gl(s, tt=tt, o=o, slot=slot, ss=ss):
                                s.dma_start_transpose(
                                    gblk[:, slot * DM + o * 128:slot * DM + (o + 1) * 128],
                                    g_scrT[o * 128:(o + 1) * 128, tt * 128:(tt + 1) * 128]
                                ).then_inc(ss.h, 16)
                            SY(_gl)
                    ss.inc(16)
                    def _rl(s, tt=tt, slot=slot, ss=ss):
                        s.dma_start_transpose(rcp[:, slot * NH:(slot + 1) * NH],
                                              denb_scr[:, tt * 128:(tt + 1) * 128]
                                              ).then_inc(ss.h, 16)
                    SY(_rl)
                    st_total = ss.n
                    xt_total = xt

                    wait("vector", xs, xt_total)
                    wait("vector", ss, st_total)
                    if ow_uses[slot] >= 1:
                        osem = Z["ow0"] if slot == 0 else Z["ow1"]
                        wait("vector", osem, 16 * ow_uses[slot])
                    def _cb(v, slot=slot):
                        v.tensor_mul(c1row[:, slot * DM:(slot + 1) * DM],
                                     sblk[:, slot * DM:(slot + 1) * DM],
                                     gblk[:, slot * DM:(slot + 1) * DM])
                        v.drain()
                        for o in range(NI):
                            v.scalar_tensor_tensor(
                                orow[:, slot * DM + o * 128:slot * DM + (o + 1) * 128],
                                c1row[:, slot * DM + o * 128:slot * DM + (o + 1) * 128],
                                rcp[:, slot * NH + o:slot * NH + o + 1],
                                sgx[:, slot * DM + o * 128:slot * DM + (o + 1) * 128],
                                ALU.mult, ALU.add)
                        v.drain()
                    V(_cb)
                    tvt = mile("vector", Z["mV"])
                    tile_v_mile.append(tvt)
                    osem = Z["ow0"] if slot == 0 else Z["ow1"]
                    wait("sync", Z["mV"], tvt)
                    ow_uses[slot] += 1
                    osem.inc(16)
                    def _ow(s, tt=tt, slot=slot, osem=osem):
                        s.dma_start(out[tt * 128:(tt + 1) * 128, :],
                                    orow[:, slot * DM:(slot + 1) * DM]).then_inc(osem.h, 16)
                    SY(_ow)

                wait("sync", Z["ow0"], Z["ow0"].n)
                wait("sync", Z["ow1"], Z["ow1"].n)

        dma_sems = ["cons", "xld0", "xld1", "xnw0", "xnw1", "mskw", "xnt",
                    "w1a", "w1b", "wo0", "wo1", "kld", "vld0", "vld1",
                    "qld0", "qld1", "so0", "so1", "dn0", "dn1",
                    "w3a", "w3b", "mr0", "mr1", "g30", "g31",
                    "x40", "x41", "sg0", "sg1", "ow0", "ow1"]
        for nm in dma_sems:
            sem = Z[nm]
            if sem.n > 0:
                eng = "gpsimd" if nm in ("w1a", "w1b", "w3a", "w3b") else "sync"
                wait(eng, sem, sem.n)

        with nc.Block() as blk:
            @blk.tensor
            def _(eng):
                for fn in th["tensor"]:
                    fn(eng)
            @blk.vector
            def _(eng):
                for fn in th["vector"]:
                    fn(eng)
            @blk.scalar
            def _(eng):
                for fn in th["scalar"]:
                    fn(eng)
            @blk.gpsimd
            def _(eng):
                for fn in th["gpsimd"]:
                    fn(eng)
            @blk.sync
            def _(eng):
                for fn in th["sync"]:
                    fn(eng)

    return nc



def _tile_w(w):
    no = w.shape[1] // 128
    return np.ascontiguousarray(
        w.reshape(NI, 128, no, 128).transpose(2, 1, 0, 3).reshape(no, 128, DM))


def _rope_tables(pos):
    j = np.arange(64, dtype=np.float64)
    inv = 1.0 / (10000.0 ** (2.0 * j / DH))
    ang = pos[None, :].astype(np.float64) * inv[:, None]
    c = np.cos(ang)
    s = np.sin(ang)
    cos2 = np.concatenate([c, c], axis=0)
    sin2 = np.concatenate([-s, s], axis=0)
    return cos2.astype(np.float32), sin2.astype(np.float32)


def make_inputs(core, full):
    b, h = core // 2, core % 2
    bf16 = ml_dtypes.bfloat16
    x = np.asarray(full["L0_attn_x"][b])
    own = slice(h * HALF, (h + 1) * HALF)
    oth = slice((1 - h) * HALF, (2 - h) * HALF)
    x_r = np.ascontiguousarray(np.concatenate([x[own], x[oth]], axis=0), dtype=np.float32)

    pos_own = np.arange(h * HALF, (h + 1) * HALF)
    pos_oth = np.arange((1 - h) * HALF, (2 - h) * HALF)
    pos_all = np.concatenate([pos_own, pos_oth])
    cq, sq = _rope_tables(pos_own)
    ck, sk = _rope_tables(pos_all)

    ident = np.eye(128, dtype=np.float32)
    perm = np.zeros((128, 128), dtype=np.float32)
    for p in range(128):
        perm[p, (p + 64) % 128] = 1.0

    ins = {
        "x": x_r,
        "wq2": _tile_w(np.asarray(full["L0_attn_wq"], dtype=np.float32)),
        "wk2": _tile_w(np.asarray(full["L0_attn_wk"], dtype=np.float32)),
        "wv2": _tile_w(np.asarray(full["L0_attn_wv"], dtype=np.float32)),
        "bq_t": np.ascontiguousarray(np.asarray(full["L0_attn_bq"], dtype=np.float32).reshape(NH, 128).T),
        "bk_t": np.ascontiguousarray(np.asarray(full["L0_attn_bk"], dtype=np.float32).reshape(NKV, 128).T),
        "bv_t": np.ascontiguousarray(np.asarray(full["L0_attn_bv"], dtype=np.float32).reshape(NKV, 128).T),
        "gw2": np.ascontiguousarray(np.asarray(full["L0_ffn_gate_w"], dtype=np.float32)
                                    .reshape(NE, NI, 128).transpose(2, 1, 0).reshape(128, NI * NE)),
        "gb_c": np.ascontiguousarray(np.asarray(full["L0_ffn_gate_b"], dtype=np.float32).reshape(NE, 1)),
        "ew2": np.ascontiguousarray(np.asarray(full["L0_ffn_expert_w"], dtype=np.float32)
                                     .reshape(NE, NI, 128, NI, 128).transpose(0, 1, 4, 3, 2)
                                     .reshape(NE, NI, 128, DM)),
        "eb2": np.ascontiguousarray(np.asarray(full["L0_ffn_expert_b"], dtype=np.float32)
                                     .reshape(NE, NI, 128).transpose(2, 0, 1).reshape(128, NE * NI)),
        "cosq": cq.astype(bf16),
        "sinqn": sq.astype(bf16),
        "cosk": ck.astype(bf16),
        "sinkn": sk.astype(bf16),
        "identf": ident,
        "identb": ident.astype(bf16),
        "perm64": perm.astype(bf16),
        "onescol": np.ones((128, 1), dtype=bf16),
        "fconst": np.broadcast_to(np.array([LN_EPS, EXP_SHIFT], dtype=np.float32), (128, 2)).copy(),
        "ecol": (np.arange(NE, dtype=np.float32) * CAP).reshape(NE, 1),
    }
    return ins


def kernel(**inputs):
    nc = build_program()
    in_maps = [make_inputs(c, inputs) for c in range(8)]
    res = run_bass_kernel_spmd(nc, in_maps, core_ids=list(range(8)))
    outp = np.zeros((B, S, DM), dtype=np.float32)
    for c in range(8):
        b, h = c // 2, c % 2
        outp[b, h * HALF:(h + 1) * HALF, :] = res.results[c]["out"]
    return outp


# version 08 (final)
# speedup vs baseline: 1.9913x; 1.0021x over previous
import os
import sys
sys.path.insert(0, "/opt/trn_rl_repo")
import numpy as np
import ml_dtypes
from contextlib import ExitStack

import concourse.bass as bass
import concourse.mybir as mybir
from concourse.bass_utils import run_bass_kernel_spmd
from concourse.bass import IndirectOffsetOnAxis

F32 = mybir.dt.float32
BF16 = mybir.dt.bfloat16
AF = mybir.ActivationFunctionType
ALU = mybir.AluOpType
AX = mybir.AxisListType

B, S, DM = 4, 4096, 4096
NH, NKV, DH = 32, 8, 128
NE = 8
HALF = S // 2
NI = DM // 128
NTO = HALF // 128
NTF = S // 128
NQC = HALF // 512
KVC = NKV * DH
INV_SQRT_D = float(1.0 / np.sqrt(DH))
EXP_SHIFT = -20.0
LN_EPS = 1e-5
SPARSE_MOE = True
CAP = 512
SLOTS = NE * CAP
NSL = SLOTS // 128

ENGS = ("tensor", "vector", "scalar", "gpsimd", "sync")


class Sem:
    __slots__ = ("h", "n")
    def __init__(self, h):
        self.h = h
        self.n = 0
    def inc(self, k=1):
        self.n += k
        return self.n


def build_program(phases=("p0", "p1", "p2", "p3", "p4")):
    nc = bass.Bass("TRN2", target_bir_lowering=False)
    DT = nc.dram_tensor

    x_in = DT("x", [S, DM], F32, kind="ExternalInput").ap()
    wq = DT("wq2", [NH, 128, DM], F32, kind="ExternalInput").ap()
    wk = DT("wk2", [NKV, 128, DM], F32, kind="ExternalInput").ap()
    wv = DT("wv2", [NKV, 128, DM], F32, kind="ExternalInput").ap()
    bq_t = DT("bq_t", [128, NH], F32, kind="ExternalInput").ap()
    bk_t = DT("bk_t", [128, NKV], F32, kind="ExternalInput").ap()
    bv_t = DT("bv_t", [128, NKV], F32, kind="ExternalInput").ap()
    gw2 = DT("gw2", [128, NI * NE], F32, kind="ExternalInput").ap()
    gb_c = DT("gb_c", [NE, 1], F32, kind="ExternalInput").ap()
    ew2 = DT("ew2", [NE, NI, 128, DM], F32, kind="ExternalInput").ap()
    eb2 = DT("eb2", [128, NE * NI], F32, kind="ExternalInput").ap()
    cosq = DT("cosq", [128, HALF], BF16, kind="ExternalInput").ap()
    sinqn = DT("sinqn", [128, HALF], BF16, kind="ExternalInput").ap()
    cosk = DT("cosk", [128, S], BF16, kind="ExternalInput").ap()
    sinkn = DT("sinkn", [128, S], BF16, kind="ExternalInput").ap()
    identf_in = DT("identf", [128, 128], F32, kind="ExternalInput").ap()
    identb_in = DT("identb", [128, 128], BF16, kind="ExternalInput").ap()
    perm64_in = DT("perm64", [128, 128], BF16, kind="ExternalInput").ap()
    onescol_in = DT("onescol", [128, 1], BF16, kind="ExternalInput").ap()
    fconst_in = DT("fconst", [128, 2], F32, kind="ExternalInput").ap()
    ecol_in = DT("ecol", [NE, 1], F32, kind="ExternalInput").ap()

    out = DT("out", [HALF, DM], F32, kind="ExternalOutput").ap()

    xn_scr = DT("xn_scr", [S, DM], BF16, kind="ExternalOutput").ap()
    q_scrT = DT("q_scrT", [DM, HALF], BF16, kind="ExternalOutput").ap()
    k_scrT = DT("k_scrT", [KVC, S], BF16, kind="ExternalOutput").ap()
    v_scrT = DT("v_scrT", [KVC, S], BF16, kind="ExternalOutput").ap()
    mask_scr = DT("mask_scr", [NE, HALF], BF16, kind="ExternalOutput").ap()
    denb_scr = DT("denb_scr", [NH, HALF], BF16, kind="ExternalOutput").ap()
    s_scrT = DT("s_scrT", [DM, HALF], BF16, kind="ExternalOutput").ap()
    if SPARSE_MOE:
        g_slots = DT("g_slots", [SLOTS, DM], BF16, kind="ExternalOutput").ap()
        slot_scr = DT("slot_scr", [1, HALF], mybir.dt.int32, kind="ExternalOutput").ap()
        xg_scr = DT("xg_scr", [SLOTS, DM], BF16, kind="ExternalOutput").ap()
        g_scrT = None
    else:
        g_scrT = DT("g_scrT", [DM, HALF], BF16, kind="ExternalOutput").ap()

    th = {e: [] for e in ENGS}

    def T(fn): th["tensor"].append(fn)
    def V(fn): th["vector"].append(fn)
    def A(fn): th["scalar"].append(fn)
    def G(fn): th["gpsimd"].append(fn)
    def SY(fn): th["sync"].append(fn)

    sems = {}

    def wait(e, sem, val):
        if val > 0:
            th[e].append(lambda eng, s=sem.h, v=val: eng.wait_ge(s, v))

    def mile(e, sem):
        tgt = sem.inc()
        def _m(eng, s=sem.h):
            eng.drain()
            eng.sem_inc(s, 1)
        th[e].append(_m)
        return tgt

    with ExitStack() as st0:
        E0 = st0.enter_context
        for name in ["ph", "cons", "xld0", "xld1", "xnw0", "xnw1", "mskw",
                     "xnt", "w1a", "w1b", "wo0", "wo1",
                     "kld", "vld0", "vld1", "qld0", "qld1", "so0", "so1", "dn0", "dn1",
                     "w3a", "w3b", "mr0", "mr1", "g30", "g31",
                     "x40", "x41", "sg0", "sg1", "ow0", "ow1",
                     "rtw", "sxl", "tpre", "sct", "tld", "zg", "gth", "xgw", "xg0", "xg1",
                     "mT", "mV", "mA", "mA2", "mG"]:
            sems[name] = Sem(E0(nc.semaphore(name)))
        Z = sems

        def barrier():
            tgt = Z["ph"].inc(5)
            for e in ENGS:
                def _f(eng, t=tgt):
                    eng.drain()
                    eng.sem_inc(Z["ph"].h, 1)
                    eng.wait_ge(Z["ph"].h, t)
                th[e].append(_f)

        identf = E0(nc.sbuf_tensor("identf_sb", [128, 128], F32))
        identb = E0(nc.sbuf_tensor("identb_sb", [128, 128], BF16))
        perm64 = E0(nc.sbuf_tensor("perm64_sb", [128, 128], BF16))
        onescol = E0(nc.sbuf_tensor("onescol_sb", [128, 1], BF16))
        bq_sb = E0(nc.sbuf_tensor("bq_sb", [128, NH], F32))
        bk_sb = E0(nc.sbuf_tensor("bk_sb", [128, NKV], F32))
        bv_sb = E0(nc.sbuf_tensor("bv_sb", [128, NKV], F32))
        eb_sb = E0(nc.sbuf_tensor("eb_sb", [128, NE * NI], F32))
        gb_sb = E0(nc.sbuf_tensor("gb_sb", [NE, 1], F32))
        gwT_sb = E0(nc.sbuf_tensor("gwT_sb", [128, NI * NE], F32))
        maskT_sb = E0(nc.sbuf_tensor("maskT_sb", [NE, HALF], BF16))
        sidx = E0(nc.sbuf_tensor("sidx_sb", [128, NTO], mybir.dt.int32))
        fconst = E0(nc.sbuf_tensor("fconst_sb", [128, 2], F32))

        def consts_fn(s):
            pairs = [
                (identf[:, :], identf_in), (identb[:, :], identb_in),
                (perm64[:, :], perm64_in), (onescol[:, :], onescol_in),
                (bq_sb[:, :], bq_t), (bk_sb[:, :], bk_t), (bv_sb[:, :], bv_t),
                (gb_sb[:, :], gb_c), (fconst[:, :], fconst_in),
                (gwT_sb[:, :], gw2), (eb_sb[:, :], eb2),
            ]
            for o_, i_ in pairs:
                s.dma_start(o_, i_).then_inc(Z["cons"].h, 16)
            assert len(pairs) == 11
        SY(consts_fn)
        Z["cons"].n = 16 * 11

        if "p0" in phases:
            with ExitStack() as stp:
                E = stp.enter_context
                xrow = E(nc.sbuf_tensor("xrow", [128, 2 * DM], F32))
                xnf = E(nc.sbuf_tensor("xnf", [128, DM], F32))
                xnb = E(nc.sbuf_tensor("xnb", [128, 2 * DM], BF16))
                xnTf = E(nc.sbuf_tensor("xnTf", [128, DM], F32))
                stats = E(nc.sbuf_tensor("stats", [128, 16], F32))
                Lbuf = E(nc.sbuf_tensor("Lbuf", [128, 128], F32))
                logt = E(nc.sbuf_tensor("logt", [128, 16], F32))
                maskb = E(nc.sbuf_tensor("maskb", [128, 128], BF16))
                psT = E(nc.psum_tensor("psT0", [128, 1024], F32))
                psG = E(nc.psum_tensor("psG0", [128, 512], F32))
                psL = E(nc.psum_tensor("psL0", [128, 512], F32))
                psMk = E(nc.psum_tensor("psM0", [128, 1024], BF16))

                wait("vector", Z["cons"], Z["cons"].n)
                def _ms(v):
                    v.memset(Lbuf[:, :], 0.0)
                    v.memset(maskb[:, :], 0.0)
                V(_ms)
                mile("vector", Z["mV"])

                tile_a2 = []
                xnf_free_T = [0]
                _ntf = int(os.environ.get("K_NTF", NTF))
                _gate = os.environ.get("K_GATE", "1") == "1"
                _gstep = int(os.environ.get("K_GSTEP", "9"))
                for it in range(_ntf):
                    own = (it < NTO) and _gate
                    slot = it % 2
                    sx = Z["xld0"] if slot == 0 else Z["xld1"]
                    if it >= 2:
                        wait("sync", Z["mA"], tile_a2[it - 2])
                    xt = sx.inc(16)
                    def _ld(s, it=it, slot=slot, sx=sx):
                        s.dma_start(xrow[:, slot * DM:(slot + 1) * DM],
                                    x_in[it * 128:(it + 1) * 128, :]).then_inc(sx.h, 16)
                    SY(_ld)

                    xv = xrow[:, slot * DM:(slot + 1) * DM]
                    wait("vector", sx, xt)
                    wait("vector", Z["mT"], xnf_free_T[0])
                    def _st(v, xv=xv):
                        v.tensor_scalar(xnf[:, :], xv, 1.0, 0.0, ALU.mult, ALU.add, accum_out=stats[:, 0:1])
                        v.drain()
                        v.scalar_tensor_tensor(xnf[:, :], xv, 1.0, xv, ALU.mult, ALU.mult,
                                               accum_out=stats[:, 1:2])
                        v.drain()
                        v.tensor_scalar(stats[:, 2:3], stats[:, 0:1], 1.0 / DM, None, ALU.mult)
                        v.tensor_scalar(stats[:, 3:4], stats[:, 1:2], 1.0 / DM, None, ALU.mult)
                        v.drain()
                        v.scalar_tensor_tensor(stats[:, 4:5], stats[:, 2:3], stats[:, 2:3],
                                               stats[:, 3:4], ALU.mult, ALU.subtract)
                    V(_st)
                    tv = mile("vector", Z["mV"])
                    wait("scalar", Z["mV"], tv)
                    def _rs(a):
                        a.activation(stats[:, 5:6], stats[:, 4:5], AF.Ln, bias=fconst[:, 0:1], scale=-1.0)
                        a.drain()
                        a.activation(stats[:, 6:7], stats[:, 5:6], AF.Exp, scale=-0.5)
                    A(_rs)
                    ta = mile("scalar", Z["mA"])
                    wait("vector", Z["mA"], ta)
                    def _ng(v):
                        v.tensor_scalar(stats[:, 7:8], stats[:, 2:3], stats[:, 6:7], -1.0,
                                        ALU.mult, ALU.mult)
                    V(_ng)
                    tv2 = mile("vector", Z["mV"])
                    wait("scalar", Z["mV"], tv2)
                    wsem = Z["xnw0"] if slot == 0 else Z["xnw1"]
                    if it >= 2:
                        wait("scalar", wsem, 16 * (it // 2))
                    def _xn(a, xv=xv, slot=slot, own=own):
                        a.activation(xnb[:, slot * DM:(slot + 1) * DM], xv, AF.Identity,
                                     bias=stats[:, 7:8], scale=stats[:, 6:7])
                        if own:
                            a.drain()
                            a.activation(xnf[:, :], xv, AF.Identity, bias=stats[:, 7:8],
                                         scale=stats[:, 6:7])
                    A(_xn)
                    ta2 = mile("scalar", Z["mA"])
                    tile_a2.append(ta2)
                    wsem.inc(16)
                    def _wr(a, it=it, slot=slot, wsem=wsem):
                        a.dma_start(xn_scr[it * 128:(it + 1) * 128, :],
                                    xnb[:, slot * DM:(slot + 1) * DM]).then_inc(wsem.h, 16)
                    A(_wr)

                    if own:
                        wait("tensor", Z["mA"], ta2)
                        ev_miles = []
                        if _gstep < 1:
                            continue
                        for i in range(NI):
                            pb = (i % 2) * 512
                            if i >= 2:
                                wait("tensor", Z["mV"], ev_miles[i - 2])
                            def _tr(t, i=i, pb=pb):
                                t.transpose(psT[:, pb:pb + 128],
                                            xnf[:, i * 128:(i + 1) * 128], identf[:, :])
                            T(_tr)
                            tt = mile("tensor", Z["mT"])
                            wait("vector", Z["mT"], tt)
                            def _ev(v, i=i, pb=pb):
                                v.tensor_copy(xnTf[:, i * 128:(i + 1) * 128], psT[:, pb:pb + 128])
                            V(_ev)
                            ev_miles.append(mile("vector", Z["mV"]))
                        xnf_free_T[0] = Z["mT"].n
                        if _gstep < 2:
                            continue
                        wait("tensor", Z["mV"], ev_miles[-1])
                        for i in range(NI):
                            def _mm(t, i=i):
                                t.matmul(psG[0:NE, 0:128], gwT_sb[:, i * NE:(i + 1) * NE],
                                         xnTf[:, i * 128:(i + 1) * 128],
                                         start=(i == 0), stop=(i == NI - 1))
                            T(_mm)
                        tt2 = mile("tensor", Z["mT"])
                        if _gstep < 3:
                            continue
                        wait("scalar", Z["mT"], tt2)
                        def _lg(a):
                            a.activation(Lbuf[0:NE, 0:128], psG[0:NE, 0:128], AF.Identity,
                                         bias=gb_sb[0:NE, 0:1])
                        A(_lg)
                        ta3 = mile("scalar", Z["mA"])
                        if _gstep < 4:
                            continue
                        wait("tensor", Z["mA"], ta3)
                        def _trL(t):
                            t.transpose(psL[:, 0:128], Lbuf[:, :], identf[:, :])
                        T(_trL)
                        tt3 = mile("tensor", Z["mT"])
                        if _gstep < 5:
                            continue
                        wait("vector", Z["mT"], tt3)
                        def _mk(v):
                            v.tensor_copy(logt[:, 0:NE], psL[:, 0:NE])
                            v.drain()
                            v.reduce_max(logt[:, 8:9], logt[:, 0:NE], AX.X)
                            v.drain()
                            v.tensor_scalar(maskb[:, 0:NE], logt[:, 0:NE], logt[:, 8:9],
                                            None, ALU.is_ge)
                            v.drain()
                            v.tensor_copy(logt[:, 9:10], maskb[:, 0:1])
                            for j in range(1, NE):
                                v.drain()
                                v.scalar_tensor_tensor(logt[:, 10:11], logt[:, 9:10], 1.0,
                                                       maskb[:, j:j + 1], ALU.mult, ALU.mult)
                                v.drain()
                                v.tensor_sub(maskb[:, j:j + 1], maskb[:, j:j + 1], logt[:, 10:11])
                                v.drain()
                                v.scalar_tensor_tensor(logt[:, 9:10], logt[:, 9:10], 1.0,
                                                       maskb[:, j:j + 1], ALU.mult, ALU.max)
                        V(_mk)
                        tv5 = mile("vector", Z["mV"])
                        if _gstep < 6:
                            continue
                        wait("tensor", Z["mV"], tv5)
                        def _trM(t):
                            t.transpose(psMk[:, 0:128], maskb[:, :], identb[:, :])
                        T(_trM)
                        tt4 = mile("tensor", Z["mT"])
                        wait("vector", Z["mT"], tt4)
                        def _mke(v, it=it):
                            v.tensor_copy(maskT_sb[0:NE, it * 128:(it + 1) * 128], psMk[0:NE, 0:128])
                        V(_mke)
                        mile("vector", Z["mV"])

                wait("scalar", Z["mV"], Z["mV"].n)
                Z["mskw"].inc(16)
                def _wm(a):
                    a.dma_start(mask_scr[:, :], maskT_sb[:, :]).then_inc(Z["mskw"].h, 16)
                A(_wm)
                barrier()

        with ExitStack() as st13:
            E13 = st13.enter_context
            xnT = E13(nc.sbuf_tensor("xnT", [128, NI * HALF], BF16))

            if "p1" in phases:
                with ExitStack() as stp:
                    E = stp.enter_context
                    cosk_sb = E(nc.sbuf_tensor("cosk_sb", [128, S], BF16))
                    sink_sb = E(nc.sbuf_tensor("sink_sb", [128, S], BF16))
                    cosq_sb = E(nc.sbuf_tensor("cosq_sb", [128, HALF], BF16))
                    sinq_sb = E(nc.sbuf_tensor("sinq_sb", [128, HALF], BF16))
                    wstat = E(nc.sbuf_tensor("wstat", [128, 2 * DM], BF16))
                    evq = E(nc.sbuf_tensor("evq", [128, 2 * 512], BF16))
                    rt1 = E(nc.sbuf_tensor("rt1", [128, 512], BF16))
                    rt2 = E(nc.sbuf_tensor("rt2", [128, 512], BF16))
                    rfin = E(nc.sbuf_tensor("rfin", [128, 2 * 512], BF16))
                    psP = [E(nc.psum_tensor("psP0", [128, 2048], F32)),
                           E(nc.psum_tensor("psP1", [128, 2048], F32))]

                    def _ct(s):
                        for o_, i_ in [(cosk_sb, cosk), (sink_sb, sinkn),
                                       (cosq_sb, cosq), (sinq_sb, sinqn)]:
                            s.dma_start(o_[:, :], i_).then_inc(Z["cons"].h, 16)
                    SY(_ct)
                    Z["cons"].inc(16 * 4)
                    wait("sync", Z["cons"], Z["cons"].n)
                    wait("sync", Z["xnw0"], Z["xnw0"].n)
                    wait("sync", Z["xnw1"], Z["xnw1"].n)

                    job_tensor_mile = []
                    job_drain_mile = []
                    slot_writes = {0: 0, 1: 0}
                    slot_last_vm = {0: 0, 1: 0}
                    job_idx = 0
                    for phase_i, half_other in enumerate([True, False]):
                        roff = 2048 if half_other else 0
                        if phase_i == 1:
                            wait("sync", Z["mT"], job_tensor_mile[-1])
                        for i in range(NI):
                            Z["xnt"].inc(16)
                            def _xl(s, i=i, roff=roff):
                                s.dma_start_transpose(
                                    xnT[:, i * HALF:(i + 1) * HALF],
                                    xn_scr[roff:roff + HALF, i * 128:(i + 1) * 128]
                                ).then_inc(Z["xnt"].h, 16)
                            SY(_xl)
                        xnt_total = Z["xnt"].n

                        jobs = [("k", ot) for ot in range(NKV)] + [("v", ot) for ot in range(NKV)]
                        if not half_other:
                            jobs += [("q", ot) for ot in range(NH)]

                        for kind, ot in jobs:
                            wslot = job_idx % 2
                            wsem = Z["w1a"] if wslot == 0 else Z["w1b"]
                            if job_idx >= 2:
                                wait("gpsimd", Z["mT"], job_tensor_mile[job_idx - 2])
                            wt = wsem.inc(16)
                            wmat = {"k": wk, "v": wv, "q": wq}[kind]
                            def _wl(g, wmat=wmat, ot=ot, wslot=wslot, wsem=wsem):
                                g.dma_start(
                                    wstat[:, wslot * DM:(wslot + 1) * DM],
                                    wmat[ot, :, :]
                                ).then_inc(wsem.h, 16)
                            G(_wl)

                            grp = job_idx % 2
                            ps = psP[grp]
                            wait("tensor", wsem, wt)
                            wait("tensor", Z["xnt"], xnt_total)
                            if job_idx >= 2:
                                am, vm = job_drain_mile[job_idx - 2]
                                wait("tensor", Z["mA"], am)
                                if vm:
                                    wait("tensor", Z["mV"], vm)
                            for i in range(NI):
                                def _mm(t, i=i, wslot=wslot, ps=ps):
                                    for tc in range(4):
                                        t.matmul(ps[:, tc * 512:(tc + 1) * 512],
                                                 wstat[:, wslot * DM + i * 128:
                                                       wslot * DM + (i + 1) * 128],
                                                 xnT[:, i * HALF + tc * 512:
                                                     i * HALF + (tc + 1) * 512],
                                                 start=(i == 0), stop=(i == NI - 1))
                                T(_mm)
                            tjob = mile("tensor", Z["mT"])

                            bias_col = {"k": bk_sb, "v": bv_sb, "q": bq_sb}[kind][:, ot:ot + 1]
                            rope = kind in ("k", "q")
                            if kind == "q":
                                cos_t, sin_t, dst, dcol0 = cosq_sb, sinq_sb, q_scrT, 0
                            elif kind == "k":
                                cos_t, sin_t, dst, dcol0 = cosk_sb, sink_sb, k_scrT, roff
                            else:
                                cos_t, sin_t, dst, dcol0 = None, None, v_scrT, roff

                            last_am = 0
                            last_vm = 0
                            for tc in range(4):
                                eslot = tc % 2
                                osem = Z["wo0"] if eslot == 0 else Z["wo1"]
                                wait("scalar", Z["mT"], tjob)
                                if slot_writes[eslot] >= 1:
                                    wait("scalar", osem, 16 * slot_writes[eslot])
                                if slot_last_vm[eslot] > 0:
                                    wait("scalar", Z["mV"], slot_last_vm[eslot])
                                def _ev(a, tc=tc, eslot=eslot, bias_col=bias_col, ps=ps):
                                    a.activation(evq[:, eslot * 512:(eslot + 1) * 512],
                                                 ps[:, tc * 512:(tc + 1) * 512], AF.Identity,
                                                 bias=bias_col)
                                A(_ev)
                                taev = mile("scalar", Z["mA"])
                                last_am = taev
                                if rope:
                                    wait("tensor", Z["mA"], taev)
                                    def _pm(t, tc=tc, eslot=eslot, ps=ps):
                                        t.matmul(ps[:, tc * 512:(tc + 1) * 512], perm64[:, :],
                                                 evq[:, eslot * 512:(eslot + 1) * 512],
                                                 start=True, stop=True)
                                    T(_pm)
                                    ttp = mile("tensor", Z["mT"])
                                    wait("vector", Z["mT"], ttp)
                                    if slot_writes[eslot] >= 1:
                                        wait("vector", osem, 16 * slot_writes[eslot])
                                    ccol = (tc * 512) if kind == "q" else (roff + tc * 512)
                                    def _rp(v, tc=tc, eslot=eslot, ccol=ccol,
                                            cos_t=cos_t, sin_t=sin_t, ps=ps):
                                        v.tensor_mul(rt1[:, :],
                                                     evq[:, eslot * 512:(eslot + 1) * 512],
                                                     cos_t[:, ccol:ccol + 512])
                                        v.tensor_mul(rt2[:, :],
                                                     ps[:, tc * 512:(tc + 1) * 512],
                                                     sin_t[:, ccol:ccol + 512])
                                        v.drain()
                                        v.tensor_add(rfin[:, eslot * 512:(eslot + 1) * 512],
                                                     rt1[:, :], rt2[:, :])
                                    V(_rp)
                                    tvr = mile("vector", Z["mV"])
                                    last_vm = tvr
                                    slot_last_vm[eslot] = tvr
                                    wait("sync", Z["mV"], tvr)
                                    src_buf = rfin
                                else:
                                    wait("sync", Z["mA"], taev)
                                    src_buf = evq
                                slot_writes[eslot] += 1
                                osem.inc(16)
                                def _wo(s, tc=tc, eslot=eslot, dst=dst, ot=ot,
                                        dcol0=dcol0, src_buf=src_buf, osem=osem):
                                    s.dma_start(dst[ot * 128:(ot + 1) * 128,
                                                    dcol0 + tc * 512:dcol0 + (tc + 1) * 512],
                                                src_buf[:, eslot * 512:(eslot + 1) * 512]
                                                ).then_inc(osem.h, 16)
                                SY(_wo)
                            job_tensor_mile.append(Z["mT"].n)
                            job_drain_mile.append((last_am, last_vm))
                            job_idx += 1
                    barrier()

            if "p2" in phases:
                with ExitStack() as stp:
                    E = stp.enter_context
                    kT_b = E(nc.sbuf_tensor("kT_b", [128, S], BF16))
                    vt_b = E(nc.sbuf_tensor("vt_b", [128, 2 * S], BF16))
                    qT_b = E(nc.sbuf_tensor("qT_b", [128, 4 * HALF], BF16))
                    et_b = E(nc.sbuf_tensor("et_b", [128, 2 * 1024], BF16))
                    sev = E(nc.sbuf_tensor("sev", [128, 2 * 512], BF16))
                    denf = E(nc.sbuf_tensor("denf", [1, 2 * 512], F32))
                    denb = E(nc.sbuf_tensor("denb", [1, 2 * 512], BF16))
                    ps_st = E(nc.psum_tensor("ps_st", [128, 2048], F32))
                    ps_o = E(nc.psum_tensor("ps_o", [128, 1024], F32))
                    ps_d0 = E(nc.psum_tensor("ps_d0", [128, 512], F32))
                    ps_d1 = E(nc.psum_tensor("ps_d1", [128, 512], F32))

                    wait("sync", Z["wo0"], Z["wo0"].n)
                    wait("sync", Z["wo1"], Z["wo1"].n)

                    kvh_last_mile = []
                    pair_last_mile = []
                    sev_uses = {0: 0, 1: 0}
                    den_uses = {0: 0, 1: 0}
                    pair_idx = 0
                    for kvh in range(NKV):
                        if kvh >= 1:
                            wait("sync", Z["mT"], kvh_last_mile[-1])
                        kt = Z["kld"].inc(16)
                        def _kl(s, kvh=kvh):
                            s.dma_start(kT_b[:, :],
                                        k_scrT[kvh * 128:(kvh + 1) * 128, :]).then_inc(Z["kld"].h, 16)
                        SY(_kl)
                        vslot = kvh % 2
                        vsem = Z["vld0"] if vslot == 0 else Z["vld1"]
                        if kvh >= 2:
                            wait("sync", Z["mT"], kvh_last_mile[-2])
                        for kp in range(32):
                            vsem.inc(16)
                            def _vl(s, kvh=kvh, vslot=vslot, kp=kp, vsem=vsem):
                                s.dma_start_transpose(
                                    vt_b[:, vslot * S + kp * 128:vslot * S + (kp + 1) * 128],
                                    v_scrT[kvh * 128:(kvh + 1) * 128, kp * 128:(kp + 1) * 128]
                                ).then_inc(vsem.h, 16)
                            SY(_vl)
                        vt = vsem.n

                        for pr in range(2):
                            qh0 = kvh * 4 + pr * 2
                            qslotbase = (pair_idx % 2) * 2
                            qsem = Z["qld0"] if (pair_idx % 2) == 0 else Z["qld1"]
                            if pair_idx >= 2:
                                wait("sync", Z["mT"], pair_last_mile[pair_idx - 2])
                            for qq in range(2):
                                qsem.inc(16)
                                def _ql(s, qh=qh0 + qq, sl=qslotbase + qq, qsem=qsem):
                                    s.dma_start(qT_b[:, sl * HALF:(sl + 1) * HALF],
                                                q_scrT[qh * 128:(qh + 1) * 128, :]
                                                ).then_inc(qsem.h, 16)
                                SY(_ql)
                            qt = qsem.n

                            wait("tensor", Z["kld"], kt)
                            wait("tensor", vsem, vt)
                            wait("tensor", qsem, qt)
                            for qc in range(NQC):
                                wait("tensor", Z["mA2"], Z["mA2"].n)
                                wait("scalar", Z["mT"], Z["mT"].n)
                                evac_gate = (Z["mA"].n, Z["mV"].n)
                                st_miles = {}
                                exp_miles = {}
                                out_miles = {}

                                def emit_st(kp):
                                    par = (kp % 2) * 1024
                                    if kp >= 2:
                                        wait("tensor", Z["mA2"], exp_miles[kp - 2])
                                    st_miles[kp] = Z["mT"].inc()
                                    def _st(t, kp=kp, par=par, qslotbase=qslotbase, qc=qc):
                                        for qq in range(2):
                                            ins = t.matmul(ps_st[:, par + qq * 512:par + (qq + 1) * 512],
                                                           kT_b[:, kp * 128:(kp + 1) * 128],
                                                           qT_b[:, (qslotbase + qq) * HALF + qc * 512:
                                                                (qslotbase + qq) * HALF + (qc + 1) * 512],
                                                           start=True, stop=True)
                                            if qq == 1:
                                                ins.then_inc(Z["mT"].h, 1)
                                    T(_st)
                                    wait("scalar", Z["mT"], st_miles[kp])
                                    if kp >= 2:
                                        wait("scalar", Z["mT"], out_miles[kp - 2])
                                    eslot = (kp % 2) * 1024
                                    exp_miles[kp] = Z["mA2"].inc()
                                    def _ex(a, par=par, eslot=eslot):
                                        a.activation(et_b[:, eslot:eslot + 1024],
                                                     ps_st[:, par:par + 1024], AF.Exp,
                                                     bias=fconst[:, 1:2], scale=INV_SQRT_D
                                                     ).then_inc(Z["mA2"].h, 1)
                                    A(_ex)

                                def emit_out(kp):
                                    eslot = (kp % 2) * 1024
                                    if kp == 0:
                                        wait("tensor", Z["mA"], evac_gate[0])
                                        wait("tensor", Z["mV"], evac_gate[1])
                                    wait("tensor", Z["mA2"], exp_miles[kp])
                                    out_miles[kp] = Z["mT"].inc()
                                    def _ou(t, kp=kp, eslot=eslot, vslot=vslot):
                                        first = (kp == 0)
                                        last = (kp == 31)
                                        for qq in range(2):
                                            t.matmul(ps_o[:, qq * 512:(qq + 1) * 512],
                                                     vt_b[:, vslot * S + kp * 128:
                                                          vslot * S + (kp + 1) * 128],
                                                     et_b[:, eslot + qq * 512:eslot + (qq + 1) * 512],
                                                     start=first, stop=last)
                                        t.matmul(ps_d0[0:1, :], onescol[:, :],
                                                 et_b[:, eslot:eslot + 512],
                                                 start=first, stop=last)
                                        t.matmul(ps_d1[0:1, :], onescol[:, :],
                                                 et_b[:, eslot + 512:eslot + 1024],
                                                 start=first, stop=last).then_inc(Z["mT"].h, 1)
                                    T(_ou)

                                emit_st(0)
                                for kp in range(1, 32):
                                    emit_st(kp)
                                    emit_out(kp - 1)
                                emit_out(31)

                                for qq in range(2):
                                    qh = qh0 + qq
                                    sslot = qq
                                    so = Z["so0"] if sslot == 0 else Z["so1"]
                                    wait("scalar", Z["mT"], out_miles[31])
                                    if sev_uses[sslot] >= 1:
                                        wait("scalar", so, 16 * sev_uses[sslot])
                                    def _se(a, sslot=sslot):
                                        a.activation(sev[:, sslot * 512:(sslot + 1) * 512],
                                                     ps_o[:, sslot * 512:(sslot + 1) * 512],
                                                     AF.Copy)
                                    A(_se)
                                    mile("scalar", Z["mA"])
                                    sev_uses[sslot] += 1
                                    so.inc(16)
                                    def _sw(a, qh=qh, qc=qc, sslot=sslot, so=so):
                                        a.dma_start(s_scrT[qh * 128:(qh + 1) * 128,
                                                           qc * 512:(qc + 1) * 512],
                                                    sev[:, sslot * 512:(sslot + 1) * 512]
                                                    ).then_inc(so.h, 16)
                                    A(_sw)
                                    dsem = Z["dn0"] if qq == 0 else Z["dn1"]
                                    psd = ps_d0 if qq == 0 else ps_d1
                                    wait("vector", Z["mT"], out_miles[31])
                                    if den_uses[qq] >= 1:
                                        wait("vector", dsem, 16 * den_uses[qq])
                                    def _dn(v, qq=qq, psd=psd):
                                        v.reciprocal(denf[0:1, qq * 512:(qq + 1) * 512],
                                                     psd[0:1, :])
                                        v.drain()
                                        v.tensor_copy(denb[0:1, qq * 512:(qq + 1) * 512],
                                                      denf[0:1, qq * 512:(qq + 1) * 512])
                                    V(_dn)
                                    tdn = mile("vector", Z["mV"])
                                    wait("scalar", Z["mV"], tdn)
                                    den_uses[qq] += 1
                                    dsem.inc(16)
                                    def _dw(a, qh=qh, qc=qc, qq=qq, dsem=dsem):
                                        a.dma_start(denb_scr[qh:qh + 1, qc * 512:(qc + 1) * 512],
                                                    denb[0:1, qq * 512:(qq + 1) * 512]
                                                    ).then_inc(dsem.h, 16)
                                    A(_dw)
                            pair_last_mile.append(Z["mT"].n)
                            pair_idx += 1
                        kvh_last_mile.append(Z["mT"].n)
                    barrier()

            if "p3" in phases and not SPARSE_MOE:
                with ExitStack() as stp:
                    E = stp.enter_context
                    w3 = E(nc.sbuf_tensor("w3", [128, 2 * DM], BF16))
                    mrep = E(nc.sbuf_tensor("mrep", [128, 2 * HALF], BF16))
                    silu_b = E(nc.sbuf_tensor("silu_b", [128, 2 * HALF], BF16))
                    gstr = E(nc.sbuf_tensor("gstr", [128, 2 * HALF], BF16))
                    gtmp = E(nc.sbuf_tensor("gtmp", [128, HALF], BF16))
                    psP3 = [E(nc.psum_tensor("psQ0", [128, 2048], F32)),
                            E(nc.psum_tensor("psQ1", [128, 2048], F32))]

                    wait("gpsimd", Z["mskw"], Z["mskw"].n)
                    wait("sync", Z["mskw"], Z["mskw"].n)

                    job_t_mile = []
                    job_a_gate = []
                    job_v_mile = []
                    gw_uses = {0: 0, 1: 0}
                    jdx = 0
                    for ot in range(NI):
                        gslot = ot % 2
                        gsem = Z["g30"] if gslot == 0 else Z["g31"]
                        for e in range(NE):
                            wslot = jdx % 2
                            wsem = Z["w3a"] if wslot == 0 else Z["w3b"]
                            if jdx >= 2:
                                wait("gpsimd", Z["mT"], job_t_mile[jdx - 2])
                            wt = wsem.inc(16)
                            def _wl(g, e=e, ot=ot, wslot=wslot, wsem=wsem):
                                g.dma_start(
                                    w3[:, wslot * DM:(wslot + 1) * DM],
                                    ew2[e, ot, :, :]
                                ).then_inc(wsem.h, 16)
                            G(_wl)
                            mslot = jdx % 2
                            msem = Z["mr0"] if mslot == 0 else Z["mr1"]
                            if jdx >= 2:
                                wait("sync", Z["mV"], job_v_mile[jdx - 2])
                            mt = msem.inc(16)
                            def _mb(s, e=e, mslot=mslot, msem=msem):
                                s.dma_start(mrep[:, mslot * HALF:(mslot + 1) * HALF],
                                            mask_scr[e:e + 1, :].broadcast_to([128, HALF])
                                            ).then_inc(msem.h, 16)
                            SY(_mb)

                            grp = jdx % 2
                            ps = psP3[grp]
                            wait("tensor", wsem, wt)
                            if jdx >= 2:
                                wait("tensor", Z["mA"], job_a_gate[jdx - 2])
                            for i in range(NI):
                                def _mm(t, i=i, wslot=wslot, ps=ps):
                                    for tc in range(4):
                                        t.matmul(ps[:, tc * 512:(tc + 1) * 512],
                                                 w3[:, wslot * DM + i * 128:
                                                     wslot * DM + (i + 1) * 128],
                                                 xnT[:, i * HALF + tc * 512:
                                                     i * HALF + (tc + 1) * 512],
                                                 start=(i == 0), stop=(i == NI - 1))
                                T(_mm)
                            tj = mile("tensor", Z["mT"])
                            job_t_mile.append(tj)

                            bias_col = eb_sb[:, e * NI + ot:e * NI + ot + 1]
                            sislot = jdx % 2
                            wait("scalar", Z["mT"], tj)
                            if jdx >= 2:
                                wait("scalar", Z["mV"], job_v_mile[jdx - 2])
                            def _si(a, sislot=sislot, bias_col=bias_col, ps=ps):
                                a.activation(silu_b[:, sislot * HALF:(sislot + 1) * HALF],
                                             ps[:, 0:2048], AF.Silu, bias=bias_col)
                            A(_si)
                            taj = mile("scalar", Z["mA"])
                            job_a_gate.append(taj)

                            wait("vector", Z["mA"], taj)
                            wait("vector", msem, mt)
                            if e == 0 and gw_uses[gslot] >= 1:
                                wait("vector", gsem, 16 * gw_uses[gslot])
                            def _ga(v, e=e, sislot=sislot, mslot=mslot, gslot=gslot):
                                if e == 0:
                                    v.tensor_mul(gstr[:, gslot * HALF:(gslot + 1) * HALF],
                                                 silu_b[:, sislot * HALF:(sislot + 1) * HALF],
                                                 mrep[:, mslot * HALF:(mslot + 1) * HALF])
                                else:
                                    v.tensor_mul(gtmp[:, :],
                                                 silu_b[:, sislot * HALF:(sislot + 1) * HALF],
                                                 mrep[:, mslot * HALF:(mslot + 1) * HALF])
                                    v.drain()
                                    v.tensor_add(gstr[:, gslot * HALF:(gslot + 1) * HALF],
                                                 gstr[:, gslot * HALF:(gslot + 1) * HALF],
                                                 gtmp[:, :])
                            V(_ga)
                            tvj = mile("vector", Z["mV"])
                            job_v_mile.append(tvj)
                            jdx += 1

                        wait("scalar", Z["mV"], job_v_mile[-1])
                        gw_uses[gslot] += 1
                        gsem.inc(16)
                        def _gw(a, ot=ot, gslot=gslot, gsem=gsem):
                            a.dma_start(g_scrT[ot * 128:(ot + 1) * 128, :],
                                        gstr[:, gslot * HALF:(gslot + 1) * HALF]
                                        ).then_inc(gsem.h, 16)
                        A(_gw)
                    barrier()

        if "p3" in phases and SPARSE_MOE:
            with ExitStack() as stp:
                E = stp.enter_context
                w3 = E(nc.sbuf_tensor("w3", [128, 2 * DM], BF16))
                xgb = E(nc.sbuf_tensor("xgb", [128, 2 * NI * CAP], BF16))
                grow = E(nc.sbuf_tensor("grow", [128, 4 * DM], BF16))
                sil3 = E(nc.sbuf_tensor("sil3", [128, 2 * CAP], BF16))
                gbufA = E(nc.sbuf_tensor("gbufA", [128, DM], BF16))
                gbufB = E(nc.sbuf_tensor("gbufB", [128, DM], BF16))
                rt_ones = E(nc.sbuf_tensor("rt_ones", [NE, HALF], F32))
                rt_a = E(nc.sbuf_tensor("rt_a", [NE, HALF], F32))
                rt_b = E(nc.sbuf_tensor("rt_b", [NE, HALF], F32))
                ones8f = E(nc.sbuf_tensor("ones8f", [NE, 1], F32))
                ecol_sb = E(nc.sbuf_tensor("ecol_sb", [NE, 1], F32))
                slotrow = E(nc.sbuf_tensor("slotrow", [1, HALF], F32))
                irow = E(nc.sbuf_tensor("irow", [1, HALF], mybir.dt.int32))
                sidx = E(nc.sbuf_tensor("sidx", [128, NTO], mybir.dt.int32))
                tki = E(nc.sbuf_tensor("tki", [128, NTO], mybir.dt.int32))
                tosb = E(nc.sbuf_tensor("tosb", [128, NSL], mybir.dt.int32))
                snt = E(nc.sbuf_tensor("snt", [1, SLOTS], mybir.dt.int32))
                zrow = E(nc.sbuf_tensor("zrow", [128, DM], BF16))
                psY = E(nc.psum_tensor("psY3", [128, 1024], F32))
                psT2 = E(nc.psum_tensor("psT23", [128, 2048], BF16))

                Z["rtw"].inc(16)
                def _ec(s):
                    s.dma_start(ecol_sb[:, :], ecol_in).then_inc(Z["rtw"].h, 16)
                SY(_ec)
                wait("vector", Z["rtw"], 16)
                def _rt0(v):
                    v.memset(rt_ones[:, :], 1.0)
                    v.memset(ones8f[:, :], 1.0)
                    v.drain()
                    v.tensor_tensor_scan(rt_a[:, :], maskT_sb[:, :], rt_ones[:, :],
                                         0.0, ALU.add, ALU.mult)
                    v.drain()
                    v.tensor_sub(rt_b[:, :], rt_a[:, :], maskT_sb[:, :])
                    v.drain()
                    v.tensor_scalar(rt_a[:, :], rt_b[:, :], float(CAP - 1), None, ALU.min)
                    v.drain()
                    v.tensor_scalar(rt_b[:, :], rt_a[:, :], ecol_sb[:, 0:1], None, ALU.add)
                    v.drain()
                    v.tensor_mul(rt_a[:, :], rt_b[:, :], maskT_sb[:, :])
                V(_rt0)
                tvr0 = mile("vector", Z["mV"])
                wait("tensor", Z["mV"], tvr0)
                for qc in range(4):
                    tq = Z["mT"].inc()
                    def _r1(t, qc=qc):
                        t.matmul(psY[0:1, 0:512], ones8f[0:NE, 0:1],
                                 rt_a[0:NE, qc * 512:(qc + 1) * 512],
                                 start=True, stop=True).then_inc(Z["mT"].h, 1)
                    T(_r1)
                    wait("vector", Z["mT"], tq)
                    def _r2(v, qc=qc):
                        v.tensor_copy(slotrow[0:1, qc * 512:(qc + 1) * 512], psY[0:1, 0:512])
                        v.drain()
                    V(_r2)
                    tvq = mile("vector", Z["mV"])
                    wait("tensor", Z["mV"], tvq)
                def _cv(v):
                    v.tensor_copy(irow[:, :], slotrow[:, :])
                    v.drain()
                V(_cv)
                tvi = mile("vector", Z["mV"])
                wait("scalar", Z["mV"], tvi)
                Z["rtw"].inc(16)
                def _sw(a):
                    a.dma_start(slot_scr[:, :], irow[:, :]).then_inc(Z["rtw"].h, 16)
                A(_sw)
                wait("sync", Z["rtw"], Z["rtw"].n)
                for ch in range(NTO):
                    Z["sxl"].inc(16)
                    def _sx(s, ch=ch):
                        s.dma_start(sidx[:, ch:ch + 1],
                                    slot_scr[:, ch * 128:(ch + 1) * 128]
                                    .rearrange("r a -> (r a)").unsqueeze(1)).then_inc(Z["sxl"].h, 16)
                    SY(_sx)
                wait("gpsimd", Z["sxl"], Z["sxl"].n)
                for ch in range(NTO):
                    gslot2 = ch % 2
                    lsem = Z["xgw"] if gslot2 == 0 else Z["zg"]
                    ssem = Z["gth"] if gslot2 == 0 else Z["tld"]
                    if ch >= 2:
                        wait("sync", ssem, 16 * (ch // 2))
                    lt = lsem.inc(16)
                    def _xl0(s, ch=ch, gslot2=gslot2, lsem=lsem):
                        gb = gbufA if gslot2 == 0 else gbufB
                        s.dma_start(gb[:, :], xn_scr[ch * 128:(ch + 1) * 128, :]
                                    ).then_inc(lsem.h, 16)
                    SY(_xl0)
                    wait("gpsimd", lsem, lt)
                    ssem.inc(16)
                    def _sc(g, ch=ch, gslot2=gslot2, ssem=ssem):
                        gb = gbufA if gslot2 == 0 else gbufB
                        g.indirect_dma_start(
                            xg_scr[:, :], IndirectOffsetOnAxis(sidx[:, ch:ch + 1], 0),
                            gb[:, :], None,
                            bounds_check=SLOTS - 1, oob_is_err=False
                        ).then_inc(ssem.h, 16)
                    G(_sc)

                job_t_mile = []
                job_a_gate = []
                job_v_gate = []
                exp_t_mile = []
                exp_v_mile = []
                prev_ev_miles = []
                jdx = 0
                for e in range(NE):
                    xslot = e % 2
                    xsem = Z["xg0"] if xslot == 0 else Z["xg1"]
                    if e == 0:
                        wait("sync", Z["gth"], Z["gth"].n)
                        wait("sync", Z["tld"], Z["tld"].n)
                    if e >= 2:
                        wait("sync", Z["mT"], exp_t_mile[e - 2])
                    for i in range(NI):
                        xsem.inc(16)
                        def _xl(s, e=e, i=i, xslot=xslot, xsem=xsem):
                            s.dma_start_transpose(
                                xgb[:, (xslot * NI + i) * CAP:(xslot * NI + i + 1) * CAP],
                                xg_scr[e * CAP:(e + 1) * CAP, i * 128:(i + 1) * 128]
                            ).then_inc(xsem.h, 16)
                        SY(_xl)
                    xt_total = xsem.n

                    for ot in range(NI):
                        wslot = jdx % 2
                        wsem = Z["w3a"] if wslot == 0 else Z["w3b"]
                        if jdx >= 2:
                            wait("gpsimd", Z["mT"], job_t_mile[jdx - 2])
                        wt = wsem.inc(16)
                        def _wl(g, e=e, ot=ot, wslot=wslot, wsem=wsem):
                            g.dma_start(
                                w3[:, wslot * DM:(wslot + 1) * DM],
                                ew2[e, ot, :, :]
                            ).then_inc(wsem.h, 16)
                        G(_wl)

                        pslot = jdx % 2
                        wait("tensor", wsem, wt)
                        wait("tensor", xsem, xt_total)
                        if jdx >= 2:
                            wait("tensor", Z["mA"], job_a_gate[jdx - 2])
                        tjv = Z["mT"].inc()
                        def _mm(t, wslot=wslot, xslot=xslot, pslot=pslot):
                            for i in range(NI):
                                ins = t.matmul(psY[:, pslot * 512:(pslot + 1) * 512],
                                               w3[:, wslot * DM + i * 128:
                                                   wslot * DM + (i + 1) * 128],
                                               xgb[:, (xslot * NI + i) * CAP:
                                                   (xslot * NI + i + 1) * CAP],
                                               start=(i == 0), stop=(i == NI - 1))
                                if i == NI - 1:
                                    ins.then_inc(Z["mT"].h, 1)
                        T(_mm)
                        job_t_mile.append(tjv)

                        bias_col = eb_sb[:, e * NI + ot:e * NI + ot + 1]
                        sislot = jdx % 2
                        wait("scalar", Z["mT"], tjv)
                        if jdx >= 2:
                            wait("scalar", Z["mV"], job_v_gate[jdx - 2])
                        taj = Z["mA"].inc()
                        def _si(a, sislot=sislot, bias_col=bias_col, pslot=pslot):
                            a.activation(sil3[:, sislot * CAP:(sislot + 1) * CAP],
                                         psY[:, pslot * 512:(pslot + 1) * 512],
                                         AF.Silu, bias=bias_col).then_inc(Z["mA"].h, 1)
                        A(_si)
                        job_a_gate.append(taj)

                        wait("tensor", Z["mA"], taj)
                        ev_miles = []
                        for bb in range(4):
                            tslot = bb % 2
                            if bb >= 2:
                                wait("tensor", Z["mV"], ev_miles[bb - 2])
                            elif jdx > 0 and len(prev_ev_miles) >= 2:
                                wait("tensor", Z["mV"], prev_ev_miles[bb - 2])
                            ttr_b = Z["mT"].inc()
                            def _tr(t, sislot=sislot, bb=bb, tslot=tslot):
                                t.transpose(psT2[:, tslot * 1024:tslot * 1024 + 128],
                                            sil3[:, sislot * CAP + bb * 128:
                                                 sislot * CAP + (bb + 1) * 128],
                                            identb[:, :]).then_inc(Z["mT"].h, 1)
                            T(_tr)
                            wait("vector", Z["mT"], ttr_b)
                            def _ev(v, bb=bb, ot=ot, tslot=tslot):
                                v.tensor_copy(grow[:, bb * DM + ot * 128:bb * DM + (ot + 1) * 128],
                                              psT2[:, tslot * 1024:tslot * 1024 + 128])
                            V(_ev)
                            ev_miles.append(mile("vector", Z["mV"]))
                        prev_ev_miles = ev_miles
                        job_v_gate.append(ev_miles[-1])
                        jdx += 1
                    exp_t_mile.append(Z["mT"].n)
                    exp_v_mile.append(Z["mV"].n)

                    wait("scalar", Z["mV"], exp_v_mile[-1])
                    for st in range(4):
                        sl = e * 4 + st
                        Z["sct"].inc(16)
                        def _sg(a, st=st, sl=sl):
                            a.dma_start(g_slots[sl * 128:(sl + 1) * 128, :],
                                        grow[:, st * DM:(st + 1) * DM]
                                        ).then_inc(Z["sct"].h, 16)
                        A(_sg)
                    if e < NE - 1:
                        wait("vector", Z["sct"], Z["sct"].n)
                barrier()


        if "p4" in phases:
            with ExitStack() as stp:
                E = stp.enter_context
                sgx = E(nc.sbuf_tensor("sgx", [128, 2 * DM], F32))
                sblk = E(nc.sbuf_tensor("sblk", [128, 2 * DM], BF16))
                gblk = E(nc.sbuf_tensor("gblk", [128, 2 * DM], BF16))
                rcp = E(nc.sbuf_tensor("rcp", [128, 2 * NH], BF16))
                c1row = E(nc.sbuf_tensor("c1row", [128, 2 * DM], BF16))
                orow = E(nc.sbuf_tensor("orow", [128, 2 * DM], F32))

                p4sems = ["so0", "so1", "dn0", "dn1"]
                p4sems += (["sct"] if SPARSE_MOE else ["g30", "g31"])
                for nm in p4sems:
                    wait("sync", Z[nm], Z[nm].n)
                    wait("scalar", Z[nm], Z[nm].n)
                    wait("gpsimd", Z[nm], Z[nm].n)

                tile_v_mile = []
                ow_uses = {0: 0, 1: 0}
                for tt in range(NTO):
                    slot = tt % 2
                    xs = Z["x40"] if slot == 0 else Z["x41"]
                    ss = Z["sg0"] if slot == 0 else Z["sg1"]
                    if tt >= 2:
                        wait("sync", Z["mV"], tile_v_mile[tt - 2])
                        wait("scalar", Z["mV"], tile_v_mile[tt - 2])
                        wait("gpsimd", Z["mV"], tile_v_mile[tt - 2])
                    xt = xs.inc(16)
                    def _xl(g, tt=tt, slot=slot, xs=xs):
                        g.dma_start(sgx[:, slot * DM:(slot + 1) * DM],
                                    x_in[tt * 128:(tt + 1) * 128, :]).then_inc(xs.h, 16)
                    G(_xl)
                    for o in range(NI):
                        ss.inc(16)
                        def _sl(s, tt=tt, o=o, slot=slot, ss=ss):
                            s.dma_start_transpose(
                                sblk[:, slot * DM + o * 128:slot * DM + (o + 1) * 128],
                                s_scrT[o * 128:(o + 1) * 128, tt * 128:(tt + 1) * 128]
                            ).then_inc(ss.h, 16)
                        SY(_sl)
                    if SPARSE_MOE:
                        gsem = Z["mr0"] if slot == 0 else Z["mr1"]
                        gt4 = gsem.inc(16)
                        def _gl(g, tt=tt, slot=slot, gsem=gsem):
                            g.indirect_dma_start(
                                gblk[:, slot * DM:(slot + 1) * DM], None,
                                g_slots[:, :],
                                IndirectOffsetOnAxis(sidx[:, tt:tt + 1], 0),
                                bounds_check=SLOTS - 1, oob_is_err=False
                            ).then_inc(gsem.h, 16)
                        G(_gl)
                        wait("vector", gsem, gt4)
                    else:
                        for o in range(NI):
                            ss.inc(16)
                            def _gl(s, tt=tt, o=o, slot=slot, ss=ss):
                                s.dma_start_transpose(
                                    gblk[:, slot * DM + o * 128:slot * DM + (o + 1) * 128],
                                    g_scrT[o * 128:(o + 1) * 128, tt * 128:(tt + 1) * 128]
                                ).then_inc(ss.h, 16)
                            SY(_gl)
                    ss.inc(16)
                    def _rl(s, tt=tt, slot=slot, ss=ss):
                        s.dma_start_transpose(rcp[:, slot * NH:(slot + 1) * NH],
                                              denb_scr[:, tt * 128:(tt + 1) * 128]
                                              ).then_inc(ss.h, 16)
                    SY(_rl)
                    st_total = ss.n
                    xt_total = xt

                    wait("vector", xs, xt_total)
                    wait("vector", ss, st_total)
                    if ow_uses[slot] >= 1:
                        osem = Z["ow0"] if slot == 0 else Z["ow1"]
                        wait("vector", osem, 16 * ow_uses[slot])
                    def _cb(v, slot=slot):
                        v.tensor_mul(c1row[:, slot * DM:(slot + 1) * DM],
                                     sblk[:, slot * DM:(slot + 1) * DM],
                                     gblk[:, slot * DM:(slot + 1) * DM])
                        v.drain()
                        for o in range(NI):
                            v.scalar_tensor_tensor(
                                orow[:, slot * DM + o * 128:slot * DM + (o + 1) * 128],
                                c1row[:, slot * DM + o * 128:slot * DM + (o + 1) * 128],
                                rcp[:, slot * NH + o:slot * NH + o + 1],
                                sgx[:, slot * DM + o * 128:slot * DM + (o + 1) * 128],
                                ALU.mult, ALU.add)
                        v.drain()
                    V(_cb)
                    tvt = mile("vector", Z["mV"])
                    tile_v_mile.append(tvt)
                    osem = Z["ow0"] if slot == 0 else Z["ow1"]
                    wait("sync", Z["mV"], tvt)
                    ow_uses[slot] += 1
                    osem.inc(16)
                    def _ow(s, tt=tt, slot=slot, osem=osem):
                        s.dma_start(out[tt * 128:(tt + 1) * 128, :],
                                    orow[:, slot * DM:(slot + 1) * DM]).then_inc(osem.h, 16)
                    SY(_ow)

                wait("sync", Z["ow0"], Z["ow0"].n)
                wait("sync", Z["ow1"], Z["ow1"].n)

        dma_sems = ["cons", "xld0", "xld1", "xnw0", "xnw1", "mskw", "xnt",
                    "w1a", "w1b", "wo0", "wo1", "kld", "vld0", "vld1",
                    "qld0", "qld1", "so0", "so1", "dn0", "dn1",
                    "w3a", "w3b", "mr0", "mr1", "g30", "g31",
                    "x40", "x41", "sg0", "sg1", "ow0", "ow1"]
        for nm in dma_sems:
            sem = Z[nm]
            if sem.n > 0:
                eng = "gpsimd" if nm in ("w1a", "w1b", "w3a", "w3b") else "sync"
                wait(eng, sem, sem.n)

        with nc.Block() as blk:
            @blk.tensor
            def _(eng):
                for fn in th["tensor"]:
                    fn(eng)
            @blk.vector
            def _(eng):
                for fn in th["vector"]:
                    fn(eng)
            @blk.scalar
            def _(eng):
                for fn in th["scalar"]:
                    fn(eng)
            @blk.gpsimd
            def _(eng):
                for fn in th["gpsimd"]:
                    fn(eng)
            @blk.sync
            def _(eng):
                for fn in th["sync"]:
                    fn(eng)

    return nc



def _tile_w(w):
    no = w.shape[1] // 128
    return np.ascontiguousarray(
        w.reshape(NI, 128, no, 128).transpose(2, 1, 0, 3).reshape(no, 128, DM))


def _rope_tables(pos):
    j = np.arange(64, dtype=np.float64)
    inv = 1.0 / (10000.0 ** (2.0 * j / DH))
    ang = pos[None, :].astype(np.float64) * inv[:, None]
    c = np.cos(ang)
    s = np.sin(ang)
    cos2 = np.concatenate([c, c], axis=0)
    sin2 = np.concatenate([-s, s], axis=0)
    return cos2.astype(np.float32), sin2.astype(np.float32)


def make_inputs(core, full):
    b, h = core // 2, core % 2
    bf16 = ml_dtypes.bfloat16
    x = np.asarray(full["L0_attn_x"][b])
    own = slice(h * HALF, (h + 1) * HALF)
    oth = slice((1 - h) * HALF, (2 - h) * HALF)
    x_r = np.ascontiguousarray(np.concatenate([x[own], x[oth]], axis=0), dtype=np.float32)

    pos_own = np.arange(h * HALF, (h + 1) * HALF)
    pos_oth = np.arange((1 - h) * HALF, (2 - h) * HALF)
    pos_all = np.concatenate([pos_own, pos_oth])
    cq, sq = _rope_tables(pos_own)
    ck, sk = _rope_tables(pos_all)

    ident = np.eye(128, dtype=np.float32)
    perm = np.zeros((128, 128), dtype=np.float32)
    for p in range(128):
        perm[p, (p + 64) % 128] = 1.0

    ins = {
        "x": x_r,
        "wq2": _tile_w(np.asarray(full["L0_attn_wq"], dtype=np.float32)),
        "wk2": _tile_w(np.asarray(full["L0_attn_wk"], dtype=np.float32)),
        "wv2": _tile_w(np.asarray(full["L0_attn_wv"], dtype=np.float32)),
        "bq_t": np.ascontiguousarray(np.asarray(full["L0_attn_bq"], dtype=np.float32).reshape(NH, 128).T),
        "bk_t": np.ascontiguousarray(np.asarray(full["L0_attn_bk"], dtype=np.float32).reshape(NKV, 128).T),
        "bv_t": np.ascontiguousarray(np.asarray(full["L0_attn_bv"], dtype=np.float32).reshape(NKV, 128).T),
        "gw2": np.ascontiguousarray(np.asarray(full["L0_ffn_gate_w"], dtype=np.float32)
                                    .reshape(NE, NI, 128).transpose(2, 1, 0).reshape(128, NI * NE)),
        "gb_c": np.ascontiguousarray(np.asarray(full["L0_ffn_gate_b"], dtype=np.float32).reshape(NE, 1)),
        "ew2": np.ascontiguousarray(np.asarray(full["L0_ffn_expert_w"], dtype=np.float32)
                                     .reshape(NE, NI, 128, NI, 128).transpose(0, 1, 4, 3, 2)
                                     .reshape(NE, NI, 128, DM)),
        "eb2": np.ascontiguousarray(np.asarray(full["L0_ffn_expert_b"], dtype=np.float32)
                                     .reshape(NE, NI, 128).transpose(2, 0, 1).reshape(128, NE * NI)),
        "cosq": cq.astype(bf16),
        "sinqn": sq.astype(bf16),
        "cosk": ck.astype(bf16),
        "sinkn": sk.astype(bf16),
        "identf": ident,
        "identb": ident.astype(bf16),
        "perm64": perm.astype(bf16),
        "onescol": np.ones((128, 1), dtype=bf16),
        "fconst": np.broadcast_to(np.array([LN_EPS, EXP_SHIFT], dtype=np.float32), (128, 2)).copy(),
        "ecol": (np.arange(NE, dtype=np.float32) * CAP).reshape(NE, 1),
    }
    return ins


def kernel(**inputs):
    nc = build_program()
    in_maps = [make_inputs(c, inputs) for c in range(8)]
    res = run_bass_kernel_spmd(nc, in_maps, core_ids=list(range(8)))
    outp = np.zeros((B, S, DM), dtype=np.float32)
    for c in range(8):
        b, h = c // 2, c % 2
        outp[b, h * HALF:(h + 1) * HALF, :] = res.results[c]["out"]
    return outp
